# version 18
# baseline (speedup 1.0000x reference)
import os
import numpy as np
import ml_dtypes
import concourse.bass as bass
import concourse.mybir as mybir
from concourse.bass_utils import run_bass_kernel_spmd
from contextlib import ExitStack

F32 = mybir.dt.float32
BF16 = mybir.dt.bfloat16
I32 = mybir.dt.int32
AF = mybir.ActivationFunctionType
ALU = mybir.AluOpType

D, DI, NH, HD, DS, CH = 1024, 2048, 32, 64, 64, 2176
T, TH, Q, NCHK, BS, NBLK, FF = 2048, 2051, 128, 16, 512, 4, 4096
EPS = 1e-5
RM = 12582912.0
NCORES = 8

DBG = os.environ.get("KDBG", "")
DBG_SHAPES = {
    "ln1": (128, 8 * TH), "xbc": (128, 17 * TH), "dt": (32, TH), "AC": (32, T),
    "bctok": (128, NCHK * 128), "scoresM": (128, NCHK * 128),
    "sfin": (64, DI), "sinit": (64, DI), "xt0": (128, DI),
    "hsb0": (128, 16 * 128), "zn0": (128, 16 * 256),
    "attn0": (128, 8 * BS), "x1b0": (128, 8 * BS),
    "h1b0": (128, 32 * BS), "ffn0": (128, 8 * BS),
}


def build(dbg="", ncores=NCORES):
    nc = bass.Bass("TRN2", target_bir_lowering=False)

    u_ft = nc.dram_tensor("u_ft", [D, TH], F32, kind="ExternalInput")
    w1x_t = nc.dram_tensor("w1x_t", [18, 8, 128, 128], F32, kind="ExternalInput")
    convw_t = nc.dram_tensor("convw_t", [4, CH, CH], F32, kind="ExternalInput")
    w2_t = nc.dram_tensor("w2_t", [16, 8, 128, 128], F32, kind="ExternalInput")
    w3_t = nc.dram_tensor("w3_t", [DI, D], F32, kind="ExternalInput")
    fw1_t = nc.dram_tensor("fw1_t", [D, FF], F32, kind="ExternalInput")
    fw2_t = nc.dram_tensor("fw2_t", [FF, D], F32, kind="ExternalInput")
    cb_row = nc.dram_tensor("cb_row", [1, CH], F32, kind="ExternalInput")
    b1_pm = nc.dram_tensor("b1_pm", [128, 32], F32, kind="ExternalInput")
    b2_pm = nc.dram_tensor("b2_pm", [128, 8], F32, kind="ExternalInput")
    lng_pm = nc.dram_tensor("lng_pm", [128, 16], F32, kind="ExternalInput")
    lnb_pm = nc.dram_tensor("lnb_pm", [128, 16], F32, kind="ExternalInput")
    alog_c = nc.dram_tensor("alog_c", [32, 1], F32, kind="ExternalInput")
    prelu_pm = nc.dram_tensor("prelu_pm", [128, 1], F32, kind="ExternalInput")
    triu_h = nc.dram_tensor("triu_h", [128, 128], F32, kind="ExternalInput")
    i128_h = nc.dram_tensor("i128_h", [128, 128], F32, kind="ExternalInput")
    i32_h = nc.dram_tensor("i32_h", [32, 32], F32, kind="ExternalInput")
    i64_h = nc.dram_tensor("i64_h", [64, 64], F32, kind="ExternalInput")
    ones_h = nc.dram_tensor("ones_h", [128, 128], F32, kind="ExternalInput")
    onesc_h = nc.dram_tensor("onesc_h", [128, 1], F32, kind="ExternalInput")
    sel_pm = nc.dram_tensor("sel_pm", [64, 8], F32, kind="ExternalInput")
    out_ft = nc.dram_tensor("out_ft", [D, T], F32, kind="ExternalOutput")

    w3g_sp = nc.dram_tensor("w3g_sp", [8, 16, 128, 128], BF16)
    wq1_sp = nc.dram_tensor("wq1_sp", [32, 8, 128, 128], BF16)
    wq2_sp = nc.dram_tensor("wq2_sp", [8, 32, 128, 128], BF16)
    xt_sp = nc.dram_tensor("xt_sp", [NCHK, 128, DI], BF16)
    ag_in = nc.dram_tensor("ag_in", [64, DI], F32)
    if ncores % 2 == 0:
        ag_out = nc.dram_tensor("ag_out", [ncores * 64, DI], F32, addr_space="Shared")
    else:
        ag_out = nc.dram_tensor("ag_out", [ncores * 64, DI], F32)

    dbg_t = nc.dram_tensor("dbg", list(DBG_SHAPES[dbg]), F32, kind="ExternalOutput") if dbg else None

    with ExitStack() as st:
        E = st.enter_context
        semnames = ("mT", "mV", "mA", "mP",
                    "dK", "dKg", "dUa", "dUb", "dUga", "dUgb",
                    "dW3a", "dW3b", "dSPa", "dSPb", "dSQa", "dSQb", "dSPXa", "dSPXb",
                    "dIWa", "dIWb", "dQa", "dQb",
                    "dCWa", "dCWb", "dCWc", "dCWd", "fCW", "dF2c",
                    "dQc", "dQd", "dSQc", "dSQd",
                    "dGWc", "dGWd", "dAWc", "dAWd", "dF1c", "dF1d",
                    "dXTa", "dXTb",
                    "dGWa", "dGWb", "dAWa", "dAWb", "dF1a", "dF1b", "dF2a", "dF2b",
                    "dUBa", "dUBb", "dUCa", "dUCb", "dX", "cc", "dG2", "dO", "dDBG")
        S = {n: E(nc.semaphore(n)) for n in semnames}
        C = {n: 0 for n in semnames}

        def W(e, *deps):
            for dep in deps:
                if dep is None:
                    continue
                sname, val = dep
                if val > 0:
                    e.wait_ge(S[sname], val)

        def I(inst, sname, n=1):
            C[sname] += n
            inst.then_inc(S[sname], n)
            return (sname, C[sname])

        def tok(sname):
            return (sname, C[sname])

        pb = [E(nc.psum_tensor(f"pb{i}", [128, 512], F32)) for i in range(8)]

        triu_f = E(nc.sbuf_tensor("triu_f", [128, 128], F32))
        i128_f = E(nc.sbuf_tensor("i128_f", [128, 128], F32))
        i32_f = E(nc.sbuf_tensor("i32_f", [32, 32], F32))
        i64_b = E(nc.sbuf_tensor("i64_b", [64, 64], BF16))
        i32_b = E(nc.sbuf_tensor("i32_b", [32, 32], BF16))
        ones_f = E(nc.sbuf_tensor("ones_f", [128, 1], F32))
        ones_b = E(nc.sbuf_tensor("ones_b", [128, 128], BF16))
        cbrow_b = E(nc.sbuf_tensor("cbrow_b", [1, CH], BF16))
        b1c = E(nc.sbuf_tensor("b1c", [128, 32], F32))
        b2c = E(nc.sbuf_tensor("b2c", [128, 8], F32))
        lngc = E(nc.sbuf_tensor("lngc", [128, 16], F32))
        lnbc_b = E(nc.sbuf_tensor("lnbc_b", [128, 16], BF16))
        alogc = E(nc.sbuf_tensor("alogc", [32, 1], F32))
        preluc = E(nc.sbuf_tensor("preluc", [128, 1], F32))
        selc = E(nc.sbuf_tensor("selc", [64, 8], F32))
        aneg = E(nc.sbuf_tensor("aneg", [32, 1], F32))
        c1c = E(nc.sbuf_tensor("c1c", [128, 1], F32))
        c2c = E(nc.sbuf_tensor("c2c", [128, 1], F32))
        bias3 = E(nc.sbuf_tensor("bias3", [128, 8], F32))
        gam1 = E(nc.sbuf_tensor("gam1", [128, 1], F32))
        gam2 = E(nc.sbuf_tensor("gam2", [128, 1], F32))
        rowsc = E(nc.sbuf_tensor("rowsc", [1, 1032], F32))
        sqs = E(nc.sbuf_tensor("sqs", [128, 2048], BF16))
        t1s = E(nc.sbuf_tensor("t1s", [128, 2048], F32))
        w3buf = [E(nc.sbuf_tensor(f"w3buf{i}", [128, D], BF16)) for i in range(3)]
        w3gb = [E(nc.sbuf_tensor(f"w3gb{i}", [128, D], BF16)) for i in range(2)]

        blk = E(nc.Block())

        actset_n = [0]

        def act_set(e, sid):
            actset_n[0] += 1
            e.add_instruction(mybir.InstLoadActFuncSet(
                name=f"actset{sid}_{actset_n[0]}", act_func_set_id=sid, ins=[], outs=[]))

        def dbg_tap(name, src_ap, wait_toks):
            if dbg != name:
                return

            @blk.gpsimd
            def _(e):
                W(e, *wait_toks)
                e.dma_start(out=dbg_t.ap(), in_=src_ap).then_inc(S["dDBG"], 16)
                C["dDBG"] += 16

        @blk.sync
        def _(e):
            for dst, src in ((triu_f, triu_h.ap()), (i128_f, i128_h.ap()),
                             (i32_f, i32_h.ap()), (ones_f, onesc_h.ap()),
                             (b1c, b1_pm.ap()), (b2c, b2_pm.ap()),
                             (lngc, lng_pm.ap()),
                             (alogc, alog_c.ap()), (preluc, prelu_pm.ap()),
                             (selc, sel_pm.ap())):
                e.dma_start(out=dst[:], in_=src).then_inc(S["dK"], 16)
                C["dK"] += 16

        @blk.gpsimd
        def _(e):
            for dst, src in ((i64_b, i64_h.ap()), (i32_b, i32_h.ap()),
                             (ones_b, ones_h.ap()), (cbrow_b, cb_row.ap()),
                             (lnbc_b, lnb_pm.ap())):
                e.dma_start(out=dst[:], in_=src).then_inc(S["dKg"], 16)
                C["dKg"] += 16
        K_ALL, KG_ALL = tok("dK"), tok("dKg")

        @blk.scalar
        def _(e):
            act_set(e, 6)
            W(e, K_ALL)
            I(e.activation(aneg[:], alogc[:], AF.Exp), "mA")
        anegA = tok("mA")

        @blk.vector
        def _(e):
            W(e, anegA)
            e.tensor_scalar(aneg[:], aneg[:], -1.0, None, ALU.mult)
            e.tensor_scalar(c1c[:], preluc[:], 1.0, 0.5, ALU.add, ALU.mult)
            I(e.tensor_scalar(c2c[:], preluc[:], -0.5, 0.5, ALU.mult, ALU.add), "mV")
        P0V = tok("mV")

        def ln_fm(x_f32, x_b16, out_b16, nk, widths, deps, res_out=None, res_in=None,
                  copy_out=None, copy_gate=None):
            sqv = sqs[:].rearrange("p (k w) -> p k w", k=nk)
            t1v = t1s[:].rearrange("p (k w) -> p k w", k=nk)
            nf = float(nk * 128)
            NW = len(widths)
            assert NW <= 2
            sqT = [None] * NW
            tsT = [None] * NW
            vrT = [None] * NW
            exT = [None] * NW
            bcT = [None] * NW
            mulT = [None] * NW
            cpT = [None] * NW

            stT = [None] * NW
            for i, (c0, wdt) in enumerate(widths):
                @blk.vector
                def _(e, i=i, c0=c0, wdt=wdt):
                    W(e, *deps)
                    if i >= 1:
                        W(e, stT[i - 1])
                    inst = None
                    for k in range(nk):
                        inst = e.tensor_tensor(sqv[:, k, 0:wdt], x_b16[:, k, c0:c0 + wdt],
                                               x_b16[:, k, c0:c0 + wdt], ALU.mult)
                    I(inst, "mV")
                sqT[i] = tok("mV")

                b0 = 2 * i

                @blk.tensor
                def _(e, i=i, c0=c0, wdt=wdt, b0=b0):
                    W(e, *deps, sqT[i], KG_ALL)
                    inst = None
                    for k in range(nk):
                        e.matmul(pb[b0][0:1, 0:wdt], ones_b[:, 0:1], x_b16[:, k, c0:c0 + wdt],
                                 start=(k == 0), stop=(k == nk - 1))
                    for k in range(nk):
                        inst = e.matmul(pb[b0 + 1][0:1, 0:wdt], ones_b[:, 0:1],
                                        sqv[:, k, 0:wdt],
                                        start=(k == 0), stop=(k == nk - 1))
                    I(inst, "mT")
                stT[i] = tok("mT")

            for i, (c0, wdt) in enumerate(widths):
                m0 = i * 256
                v0 = 512 + i * 256
                b0 = 2 * i

                @blk.vector
                def _(e, i=i, wdt=wdt, m0=m0, v0=v0, b0=b0):
                    W(e, stT[i])
                    t0 = I(e.tensor_scalar(rowsc[0:1, m0:m0 + wdt], pb[b0][0:1, 0:wdt],
                                           1.0 / nf, None, ALU.mult), "mV")
                    e.tensor_scalar(rowsc[0:1, v0:v0 + wdt], pb[b0 + 1][0:1, 0:wdt], 1.0 / nf, EPS,
                                    ALU.mult, ALU.add)
                    W(e, t0)
                    t1 = I(e.tensor_tensor(t1s[0:1, m0:m0 + wdt], rowsc[0:1, m0:m0 + wdt],
                                           rowsc[0:1, m0:m0 + wdt], ALU.mult), "mV")
                    W(e, t1)
                    I(e.tensor_tensor(rowsc[0:1, v0:v0 + wdt], rowsc[0:1, v0:v0 + wdt],
                                      t1s[0:1, m0:m0 + wdt], ALU.subtract), "mV")
                vrT[i] = tok("mV")
                tsT[i] = vrT[i]

            for i, (c0, wdt) in enumerate(widths):
                v0 = 512 + i * 256

                @blk.scalar
                def _(e, i=i, wdt=wdt, v0=v0):
                    W(e, vrT[i])
                    I(e.activation(rowsc[0:1, v0:v0 + wdt], rowsc[0:1, v0:v0 + wdt], AF.Ln), "mA")
                lnT = tok("mA")

                @blk.scalar
                def _(e, i=i, wdt=wdt, v0=v0, lnT=lnT):
                    W(e, lnT)
                    I(e.activation(rowsc[0:1, v0:v0 + wdt], rowsc[0:1, v0:v0 + wdt],
                                   AF.Exp, scale=-0.5), "mA")
                exT[i] = tok("mA")

            for i, (c0, wdt) in enumerate(widths):
                m0 = i * 256
                v0 = 512 + i * 256
                bb = 2 * ((i + 1) % NW) if NW == 2 else 2

                @blk.tensor
                def _(e, i=i, wdt=wdt, m0=m0, v0=v0, bb=bb):
                    W(e, exT[i], tsT[(i + 1) % NW] if NW == 2 else tsT[i], K_ALL)
                    e.matmul(pb[bb][:, 0:wdt], triu_f[0:1, :], rowsc[0:1, m0:m0 + wdt],
                             start=True, stop=True)
                    I(e.matmul(pb[bb + 1][:, 0:wdt], triu_f[0:1, :], rowsc[0:1, v0:v0 + wdt],
                               start=True, stop=True), "mT")
                bcT[i] = tok("mT")

            out_tok = None
            for i, (c0, wdt) in enumerate(widths):
                bb = 2 * ((i + 1) % NW) if NW == 2 else 2

                @blk.vector
                def _(e, i=i, c0=c0, wdt=wdt, bb=bb):
                    W(e, bcT[i])
                    if i >= 1:
                        W(e, mulT[i - 1])
                    inst = None
                    for k in range(nk):
                        inst = e.tensor_tensor(t1v[:, k, 0:wdt], x_f32[:, k, c0:c0 + wdt],
                                               pb[bb][:, 0:wdt], ALU.subtract)
                    I(inst, "mV")
                subT = tok("mV")

                @blk.vector
                def _(e, i=i, c0=c0, wdt=wdt, bb=bb, subT=subT):
                    W(e, subT)
                    inst = None
                    for k in range(nk):
                        if res_out is None:
                            inst = e.tensor_tensor(out_b16[:, k, c0:c0 + wdt], t1v[:, k, 0:wdt],
                                                   pb[bb + 1][:, 0:wdt], ALU.mult)
                        else:
                            inst = e.tensor_tensor(t1v[:, k, 0:wdt], t1v[:, k, 0:wdt],
                                                   pb[bb + 1][:, 0:wdt], ALU.mult)
                    I(inst, "mV")
                mulT[i] = tok("mV")

                if res_out is not None:
                    @blk.vector
                    def _(e, i=i, c0=c0, wdt=wdt):
                        W(e, mulT[i])
                        inst = None
                        for k in range(nk):
                            inst = e.tensor_tensor(res_out[:, k, c0:c0 + wdt], t1v[:, k, 0:wdt],
                                                   res_in[:, k, c0:c0 + wdt], ALU.add)
                        I(inst, "mV")
                    mulT[i] = tok("mV")
                if copy_out is not None:
                    @blk.vector
                    def _(e, i=i, c0=c0, wdt=wdt):
                        W(e, mulT[i], copy_gate)
                        inst = None
                        for k in range(nk):
                            inst = e.tensor_copy(copy_out[:, k, c0:c0 + wdt],
                                                 res_out[:, k, c0:c0 + wdt])
                        I(inst, "mP")
                    cpT[i] = tok("mP")
                out_tok = mulT[i]
            if copy_out is not None:
                return out_tok, cpT[NW - 1]
            return out_tok

        v0 = C["mV"]
        b3T0 = C["mT"]

        @blk.gpsimd
        def _(e):
            for k in range(16):
                if k >= 2:
                    e.wait_ge(S["mV"], v0 + (k - 1))
                    e.wait_ge(S["mT"], b3T0 + (k - 1))
                par = "dW3a" if k % 2 == 0 else "dW3b"
                e.dma_start(out=w3buf[k % 2][:],
                            in_=w3_t.ap()[k * 128:(k + 1) * 128, :]).then_inc(S[par], 16)
                C[par] += 16

        @blk.tensor
        def _(e):
            inst = None
            for k in range(16):
                par = "dW3a" if k % 2 == 0 else "dW3b"
                e.wait_ge(S[par], 16 * (k // 2 + 1))
                W(e, KG_ALL)
                for m in range(8):
                    inst = e.matmul(pb[m][:, 0:1], w3buf[k % 2][:, m * 128:(m + 1) * 128],
                                    lnbc_b[:, k:k + 1], start=(k == 0), stop=(k == 15))
                I(inst, "mT")
        assert C["mT"] == b3T0 + 16
        bias3T = tok("mT")

        @blk.vector
        def _(e):
            for k in range(16):
                par = "dW3a" if k % 2 == 0 else "dW3b"
                e.wait_ge(S[par], 16 * (k // 2 + 1))
                if k >= 2:
                    e.wait_ge(S["dSPa" if k % 2 == 0 else "dSPb"], 16 * (k // 2))
                W(e, K_ALL)
                I(e.tensor_scalar(w3gb[k % 2][:], w3buf[k % 2][:], lngc[:, k:k + 1], None, ALU.mult), "mV")
            W(e, bias3T)
            inst = None
            for m in range(8):
                inst = e.tensor_copy(bias3[:, m:m + 1], pb[m][:, 0:1])
            I(inst, "mV")
        bias3V = tok("mV")

        @blk.sync
        def _(e):
            for k in range(16):
                e.wait_ge(S["mV"], v0 + k + 1)
                par = "dSPa" if k % 2 == 0 else "dSPb"
                e.dma_start(out=w3g_sp.ap()[:, k].rearrange("m f c -> f m c"),
                            in_=w3gb[k % 2][:].rearrange("f (m c) -> f m c", m=8)).then_inc(S[par], 16)
                C[par] += 16
        w3spD = (("dSPa", C["dSPa"]), ("dSPb", C["dSPb"]))

        uview = u_ft.ap().rearrange("(a f) t -> f a t", f=128)

        with ExitStack() as s1:
            E1 = s1.enter_context
            xbc = E1(nc.sbuf_tensor("xbc", [128, 17, TH], BF16, side="right"))
            wip = [E1(nc.sbuf_tensor(f"wip{i}", [128, 8, 128], BF16, side="right")) for i in range(2)]
            lamr_f = E1(nc.sbuf_tensor("lamr_f", [1, NCHK * 32], F32, side="right"))
            lamr_b = E1(nc.sbuf_tensor("lamr_b", [1, NCHK * 32], BF16, side="right"))

            with ExitStack() as sD:
                ED = sD.enter_context
                dtr = ED(nc.sbuf_tensor("dtr", [32, TH], F32, side="right"))
                ATt = ED(nc.sbuf_tensor("ATt", [32, T], F32, side="right"))
                ACt = ED(nc.sbuf_tensor("ACt", [32, T], F32, side="right"))
                sp_sb = ED(nc.sbuf_tensor("sp_sb", [32, 2 * T], F32, side="right"))

                with ExitStack() as sL:
                    EL = sL.enter_context
                    ln1 = EL(nc.sbuf_tensor("ln1", [128, 8, TH], BF16, side="right"))
                    ubuf = [EL(nc.sbuf_tensor(f"ubuf{i}", [128, 8, 256], F32, side="right")) for i in range(2)]
                    ubbl = [EL(nc.sbuf_tensor(f"ubbl{i}", [128, 8, 256], BF16, side="right")) for i in range(2)]

                    UB = [(c, 256) for c in range(0, 2048, 256)] + [(2048, 3)]
                    lnToks = []
                    for wi, (c0, wdt) in enumerate(UB):
                        pa = "dUa" if wi % 2 == 0 else "dUb"
                        pg = "dUga" if wi % 2 == 0 else "dUgb"

                        @blk.sync
                        def _(e, wi=wi, c0=c0, wdt=wdt, pa=pa):
                            if wi >= 2:
                                W(e, lnToks[wi - 2])
                            e.dma_start(out=ubuf[wi % 2][:, :, 0:wdt],
                                        in_=uview[:, :, c0:c0 + wdt]).then_inc(S[pa], 16)
                            C[pa] += 16
                        uTok = tok(pa)

                        @blk.gpsimd
                        def _(e, wi=wi, c0=c0, wdt=wdt, pg=pg):
                            if wi >= 2:
                                W(e, lnToks[wi - 2])
                            e.dma_start(out=ubbl[wi % 2][:, :, 0:wdt],
                                        in_=uview[:, :, c0:c0 + wdt]).then_inc(S[pg], 16)
                            C[pg] += 16
                        ugTok = tok(pg)

                        lt = ln_fm(ubuf[wi % 2], ubbl[wi % 2],
                                   ln1[:, :, c0:c0 + wdt], 8, [(0, wdt)],
                                   (uTok, ugTok, P0V, bias3V))
                        lnToks.append(lt)
                    ln1V = lnToks[-1]
                    dbg_tap("ln1", ln1[:].rearrange("p a t -> p (a t)"), [ln1V])

                    IPB = [(0, 512, 0), (512, 512, 1), (1024, 512, 2), (1536, 512, 3), (2048, 3, 4)]
                    ipT0 = C["mT"]
                    vIP0 = C["mV"]

                    @blk.gpsimd
                    def _(e):
                        for m in range(18):
                            if m >= 2:
                                e.wait_ge(S["mT"], ipT0 + (m - 1))
                            par = "dIWa" if m % 2 == 0 else "dIWb"
                            e.dma_start(out=wip[m % 2][:],
                                        in_=w1x_t.ap()[m].rearrange("k f c -> f k c")).then_inc(S[par], 16)
                            C[par] += 16

                    @blk.tensor
                    def _(e):
                        W(e, ln1V)
                        for m in range(18):
                            par = "dIWa" if m % 2 == 0 else "dIWb"
                            e.wait_ge(S[par], 16 * (m // 2 + 1))
                            if m >= 1:
                                e.wait_ge(S["mV"], vIP0 + m)
                            inst = None
                            for k in range(8):
                                for (c0, wdt, bank) in IPB:
                                    inst = e.matmul(pb[bank][:, 0:wdt], wip[m % 2][:, k, :],
                                                    ln1[:, k, c0:c0 + wdt], start=(k == 0), stop=(k == 7))
                            I(inst, "mT")
                    assert C["mT"] == ipT0 + 18
                    ipT = tok("mT")

                    @blk.vector
                    def _(e):
                        for m in range(18):
                            e.wait_ge(S["mT"], ipT0 + m + 1)
                            inst = None
                            for (c0, wdt, bank) in IPB:
                                if m < 17:
                                    inst = e.tensor_copy(xbc[:, m, c0:c0 + wdt], pb[bank][:, 0:wdt])
                                else:
                                    inst = e.tensor_copy(dtr[0:32, c0:c0 + wdt], pb[bank][0:32, 0:wdt])
                            I(inst, "mV")
                    assert C["mV"] == vIP0 + 18
                    ipV = tok("mV")
                    dbg_tap("xbc", xbc[:].rearrange("p m t -> p (m t)"), [ipV])
                    dbg_tap("dt", dtr[:], [ipV])

                bctok = E(nc.sbuf_tensor("bctok", [128, NCHK, 128], F32))
                bfeat = E(nc.sbuf_tensor("bfeat", [64, NCHK, 128], BF16))
                cfeat = E(nc.sbuf_tensor("cfeat", [64, NCHK, 128], BF16))
                scoresM = E(nc.sbuf_tensor("scoresM", [128, NCHK, 128], BF16))
                aclt = E(nc.sbuf_tensor("aclt", [32, NCHK, 128], F32))
                eac = E(nc.sbuf_tensor("eac", [32, NCHK, 128], BF16))
                actok = E(nc.sbuf_tensor("actok", [128, NCHK, 32], F32))
                dtet = E(nc.sbuf_tensor("dtet", [128, NCHK, 32], F32))
                dtegt = E(nc.sbuf_tensor("dtegt", [128, NCHK, 32], F32))
                lam64 = E(nc.sbuf_tensor("lam64", [64, NCHK * 32], BF16))

                @blk.scalar
                def _(e):
                    W(e, ipV, P0V)
                    I(e.activation(sp_sb[:, 0:T], dtr[:, 3:3 + T], AF.Abs), "mA")
                absV = tok("mA")

                @blk.scalar
                def _(e):
                    W(e, absV)
                    I(e.activation(sp_sb[:, T:2 * T], sp_sb[:, 0:T], AF.Exp, scale=-1.0), "mA")
                spE = tok("mA")

                @blk.vector
                def _(e):
                    W(e, spE)
                    I(e.tensor_scalar(sp_sb[:, T:2 * T], sp_sb[:, T:2 * T], 1.0, None, ALU.add), "mV")
                spP1 = tok("mV")

                @blk.scalar
                def _(e):
                    W(e, spP1)
                    I(e.activation(sp_sb[:, T:2 * T], sp_sb[:, T:2 * T], AF.Ln), "mA")
                spA = tok("mA")

                @blk.vector
                def _(e):
                    W(e, spA)
                    t0 = I(e.tensor_scalar(sp_sb[:, 0:T], dtr[:, 3:3 + T], 0.0, None, ALU.max), "mV")
                    W(e, t0)
                    t1 = I(e.tensor_tensor(ATt[:], sp_sb[:, 0:T], sp_sb[:, T:2 * T], ALU.add), "mV")
                    W(e, t1)
                    t2 = I(e.tensor_scalar(sp_sb[:, 0:T], ATt[:], aneg[:, 0:1], None, ALU.mult), "mV")
                    W(e, t2)
                    I(e.tensor_tensor_scan(ACt[:], sp_sb[:, 0:T], sp_sb[:, 0:T], 0.0, ALU.add, ALU.bypass), "mV")
                acV = tok("mV")
                dbg_tap("AC", ACt[:], [acV])

                @blk.vector
                def _(e):
                    W(e, acV)
                    inst = None
                    for c in range(NCHK):
                        if c == 0:
                            inst = e.tensor_copy(aclt[:, 0, :], ACt[:, 0:128])
                        else:
                            inst = e.tensor_scalar(aclt[:, c, :], ACt[:, c * 128:(c + 1) * 128],
                                                   ACt[:, c * 128 - 1:c * 128], None, ALU.subtract)
                    inst = e.tensor_scalar(sp_sb[:, 0:T], ACt[:], ACt[:, T - 1:T], -1.0,
                                           ALU.subtract, ALU.mult)
                    I(inst, "mV")
                aclV = tok("mV")

                @blk.scalar
                def _(e):
                    W(e, aclV)
                    inst = None
                    for c in range(NCHK):
                        inst = e.activation(eac[:, c, :], aclt[:, c, :], AF.Exp)
                    inst = e.activation(sp_sb[:, T:2 * T], sp_sb[:, 0:T], AF.Exp)
                    I(inst, "mA")
                eacA = tok("mA")

                prevCp = None
                for c in range(NCHK):
                    @blk.vector
                    def _(e, c=c, pv=prevCp):
                        W(e, eacA)
                        if pv is not None:
                            W(e, pv)
                        I(e.tensor_scalar(sp_sb[:, 0:128], aclt[:, c, :],
                                          aclt[:, c, 127:128], -1.0,
                                          ALU.subtract, ALU.mult), "mV")
                    subV = tok("mV")

                    @blk.scalar
                    def _(e, subV=subV):
                        W(e, subV)
                        I(e.activation(sp_sb[:, 128:256], sp_sb[:, 0:128], AF.Exp), "mA")
                    dteA = tok("mA")

                    @blk.tensor
                    def _(e, c=c, dteA=dteA, pv=prevCp):
                        W(e, dteA, eacA, K_ALL)
                        if pv is not None:
                            W(e, pv)
                        e.transpose(pb[5][0:128, 0:32], aclt[:, c, :], i32_f[:])
                        e.transpose(pb[5][0:128, 32:64], sp_sb[:, 128:256], i32_f[:])
                        e.transpose(pb[5][0:128, 64:96], sp_sb[:, T + c * 128:T + (c + 1) * 128], i32_f[:])
                        I(e.transpose(pb[5][0:1, 96:128], aclt[:, c, 127:128], i32_f[:]), "mT")
                    trT = tok("mT")

                    @blk.vector
                    def _(e, c=c, trT=trT):
                        W(e, trT)
                        e.tensor_copy(actok[:, c, :], pb[5][0:128, 0:32])
                        e.tensor_copy(dtet[:, c, :], pb[5][0:128, 32:64])
                        e.tensor_copy(dtegt[:, c, :], pb[5][0:128, 64:96])
                        I(e.tensor_copy(lamr_f[0:1, c * 32:(c + 1) * 32], pb[5][0:1, 96:128]), "mV")
                    prevCp = tok("mV")
                prepV = prevCp

                @blk.scalar
                def _(e):
                    W(e, prepV)
                    I(e.activation(lamr_f[:], lamr_f[:], AF.Exp), "mA")
                lamA = tok("mA")

                @blk.vector
                def _(e):
                    W(e, lamA)
                    I(e.tensor_copy(lamr_b[:], lamr_f[:]), "mV")
                lamV = tok("mV")

                @blk.tensor
                def _(e):
                    W(e, lamV, KG_ALL)
                    inst = None
                    for c in range(NCHK):
                        inst = e.matmul(pb[6][0:64, c * 32:(c + 1) * 32], ones_b[0:1, 0:64],
                                        lamr_b[0:1, c * 32:(c + 1) * 32], start=True, stop=True)
                    I(inst, "mT")
                lamT = tok("mT")

                @blk.vector
                def _(e):
                    W(e, lamT)
                    I(e.tensor_copy(lam64[:], pb[6][0:64, 0:512]), "mV")
                lamVV = tok("mV")

            with ExitStack() as s2:
                E2 = s2.enter_context
                qbuf = [E2(nc.sbuf_tensor(f"qbuf{i}", [128, 512], F32, side="right")) for i in range(4)]
                qacc = E2(nc.sbuf_tensor("qacc", [128, 132], F32, side="right"))
                qw1p = [E2(nc.sbuf_tensor(f"qw1p{i}", [128, 512], F32, side="right")) for i in range(2)]
                qi1p = [E2(nc.sbuf_tensor(f"qi1p{i}", [128, 512], I32, side="right")) for i in range(2)]
                qo1p = [E2(nc.sbuf_tensor(f"qo1p{i}", [128, 512], BF16, side="right")) for i in range(4)]
                qw1 = qw1p[0]
                cw = [E2(nc.sbuf_tensor(f"cw{i}", [128, 512], BF16, side="right")) for i in range(4)]
                cst = E2(nc.sbuf_tensor("cst", [128, 8, 512], BF16, side="right"))

                QT = ([("w1", k, p, fw1_t.ap()[k * 128:(k + 1) * 128, p * 512:(p + 1) * 512])
                       for k in range(8) for p in range(8)] +
                      [("w2", k, p, fw2_t.ap()[k * 128:(k + 1) * 128, p * 512:(p + 1) * 512])
                       for k in range(32) for p in range(2)])
                NQ = len(QT)

                aQ0 = C["mA"]

                @blk.sync
                def _(e):
                    for i in range(NQ):
                        if i < 2:
                            W(e, ipT, prepV, lamVV)
                        else:
                            e.wait_ge(S["mA"], aQ0 + (i - 1))
                        par = "dQa" if i % 2 == 0 else "dQb"
                        e.dma_start(out=qbuf[i % 2][:], in_=QT[i][3]).then_inc(S[par], 16)
                        C[par] += 16
                qaTotA, qaTotB = C["dQa"], C["dQb"]

                @blk.scalar
                def _(e):
                    for i in range(NQ):
                        par = "dQa" if i % 2 == 0 else "dQb"
                        e.wait_ge(S[par], 16 * (i // 2 + 1))
                        if i >= 1:
                            e.wait_ge(S["mA"], aQ0 + i)
                        I(e.activation(qw1[:], qbuf[i % 2][:], AF.Abs,
                                       accum_out=qacc[:, i:i + 1]), "mA")
                assert C["mA"] == aQ0 + NQ
                qaccA = tok("mA")

                OBLK = [(0, 512), (512, 512), (1024, 512), (1536, 512), (2048, 128)]
                KI = [(kk, ii) for kk in range(4) for ii in range(17)]
                convT0 = C["mT"]
                convV0 = C["mV"]


                @blk.gpsimd
                def _(e):
                    n = 0
                    gi = 0
                    for (o0, ow) in OBLK:
                        for tg in range(2):
                            si = gi - 1
                            if 0 <= si < 8:
                                e.wait_ge(S["mV"], convV0 + 4 * (si + 1))
                                spar = "dSPXa" if si % 2 == 0 else "dSPXb"
                                so0 = OBLK[si // 2][0]
                                stg = si % 2
                                dst = xt_sp.ap()[stg * 8:(stg + 1) * 8, :, so0:so0 + 512] \
                                    .rearrange("c p o -> p c o")
                                e.dma_start(out=dst, in_=cst[:, :, 0:512]).then_inc(S[spar], 16)
                                C[spar] += 16
                            for (kk, ii) in KI:
                                if n < 4:
                                    W(e, ipV, prepV, lamVV)
                                else:
                                    e.wait_ge(S["fCW"], n - 3)
                                par = ("dCWa", "dCWb", "dCWc", "dCWd")[n % 4]
                                e.dma_start(out=cw[n % 4][:, 0:ow],
                                            in_=convw_t.ap()[kk, ii * 128:(ii + 1) * 128,
                                                             o0:o0 + ow]).then_inc(S[par], 16)
                                C[par] += 16
                                n += 1
                            gi += 1
                xtspD = (("dSPXa", C["dSPXa"]), ("dSPXb", C["dSPXb"]))

                @blk.tensor
                def _(e):
                    W(e, ipV, KG_ALL)
                    n = 0
                    gi = 0
                    for (o0, ow) in OBLK:
                        for tg in range(2):
                            if gi == 3:
                                e.wait_ge(S["mP"], 1)
                                e.wait_ge(S["mV"], convV0 + 4 * 3)
                                e.matmul(pb[7][0:1, 0:1], ones_f[:, 0:1], qacc[:, 128:129],
                                         start=True, stop=True).then_inc(S["mP"], 1)
                                e.matmul(pb[7][0:1, 1:2], ones_f[:, 0:1], qacc[:, 129:130],
                                         start=True, stop=True).then_inc(S["mP"], 1)
                                e.wait_ge(S["mP"], 6)
                                e.matmul(pb[7][:, 2:3], triu_f[0:1, :], rowsc[0:1, 1024:1025],
                                         start=True, stop=True).then_inc(S["mP"], 1)
                                e.matmul(pb[7][:, 3:4], triu_f[0:1, :], rowsc[0:1, 1025:1026],
                                         start=True, stop=True).then_inc(S["mP"], 1)
                                e.wait_ge(S["mP"], 10)
                            if gi >= 1:
                                e.wait_ge(S["mV"], convV0 + 4 * gi)
                            inst = None
                            for ki, (kk, ii) in enumerate(KI):
                                par = ("dCWa", "dCWb", "dCWc", "dCWd")[n % 4]
                                e.wait_ge(S[par], 16 * (n // 4 + 1))
                                for tc in range(8):
                                    tci = tg * 8 + tc
                                    tcol = tci * 128 + kk
                                    inst = e.matmul(pb[tc][:, 0:ow], xbc[:, ii, tcol:tcol + 128],
                                                    cw[n % 4][:, 0:ow], start=(ki == 0), stop=False)
                                I(inst, "fCW")
                                n += 1
                            for tc in range(8):
                                inst = e.matmul(pb[tc][:, 0:ow], ones_b[0:1, :],
                                                cbrow_b[0:1, o0:o0 + ow], start=False, stop=True)
                            I(inst, "mT")
                            gi += 1

                @blk.scalar
                def _(e):
                    e.wait_ge(S["mP"], 4)
                    e.activation(rowsc[0:1, 1024:1026], rowsc[0:1, 1024:1026],
                                 AF.Ln).then_inc(S["mP"], 1)
                    e.wait_ge(S["mP"], 5)
                    e.activation(rowsc[0:1, 1024:1026], rowsc[0:1, 1024:1026],
                                 AF.Exp, scale=-1.0).then_inc(S["mP"], 1)

                qd = {-2: 10, -1: 10}

                def emit_qtiles(e, i0, i1):
                    ids = list(range(i0, i1))
                    for p in range(0, len(ids), 2):
                        pair = ids[p:p + 2]
                        toks = {}
                        for step in range(4):
                            for t in pair:
                                b = t % 2
                                if step == 0:
                                    par = ("dQa", "dQb", "dQc", "dQd")[t % 4]
                                    base = (qaTotA, qaTotB, 0, 0)[t % 4]
                                    e.wait_ge(S[par], base + 16 * (t // 4 + 1))
                                    if t >= 4:
                                        e.wait_ge(S[("dSQa", "dSQb", "dSQc", "dSQd")[t % 4]],
                                                  16 * ((t - 4) // 4 + 1))
                                    e.wait_ge(S["mP"], qd[t - 2])
                                    g = gam1 if QT[t][0] == "w1" else gam2
                                    toks[t] = I(e.tensor_scalar(qw1p[b][:], qbuf[t % 4][:],
                                                                g[:, 0:1], RM,
                                                                ALU.mult, ALU.add), "mP")[1]
                                elif step == 1:
                                    e.wait_ge(S["mP"], toks[t])
                                    toks[t] = I(e.tensor_copy(qi1p[b][:], qw1p[b][:]), "mP")[1]
                                elif step == 2:
                                    e.wait_ge(S["mP"], toks[t])
                                    toks[t] = I(e.tensor_scalar(qw1p[b][:], qi1p[b][:],
                                                                -RM, 1.0,
                                                                ALU.add, ALU.min), "mP")[1]
                                else:
                                    e.wait_ge(S["mP"], toks[t])
                                    toks[t] = I(e.tensor_scalar(qo1p[t % 4][:], qw1p[b][:],
                                                                -1.0, None,
                                                                ALU.max), "mP")[1]
                                    qd[t] = toks[t]

                QBATCH = {4: (0, 22), 5: (22, 44), 6: (44, 66), 7: (66, 88),
                          8: (88, 108), 9: (108, 128)}

                @blk.vector
                def _(e):
                    gi = 0
                    prevStt = None
                    for obi, (o0, ow) in enumerate(OBLK):
                        for tg in range(2):
                            if gi == 3:
                                e.wait_ge(S["mP"], 3)
                                e.tensor_scalar(rowsc[0:1, 1024:1025], pb[7][0:1, 0:1],
                                                1.0 / (FF * D), EPS, ALU.mult, ALU.add)
                                e.tensor_scalar(rowsc[0:1, 1025:1026], pb[7][0:1, 1:2],
                                                1.0 / (FF * D), EPS,
                                                ALU.mult, ALU.add).then_inc(S["mP"], 1)
                                e.wait_ge(S["mP"], 8)
                                e.tensor_copy(gam1[:], pb[7][:, 2:3]).then_inc(S["mP"], 1)
                                e.tensor_copy(gam2[:], pb[7][:, 3:4]).then_inc(S["mP"], 1)
                                C["mP"] = 10
                            W(e, P0V, lamVV)
                            if 1 <= gi <= 8 and obi <= 4:
                                si = gi - 1
                                if si < 8:
                                    e.wait_ge(S["dSPXa" if si % 2 == 0 else "dSPXb"],
                                              16 * (si // 2 + 1))
                            for half in range(2):
                                e.wait_ge(S["mT"], convT0 + gi + 1)
                                W(e, prevStt)
                                inst = None
                                for tc2 in range(4):
                                    tc = half * 4 + tc2
                                    inst = e.scalar_tensor_tensor(
                                        t1s[:, tc2 * 512:tc2 * 512 + ow], pb[tc][:, 0:ow],
                                        0.0, preluc[:, 0:1].broadcast_to([128, ow]),
                                        ALU.min, ALU.mult)
                                negT = I(inst, "mV")
                                W(e, negT)
                                inst = None
                                for tc2 in range(4):
                                    tc = half * 4 + tc2
                                    tci = tg * 8 + tc
                                    if o0 < DI:
                                        dst = cst[:, tc, 0:ow]
                                    else:
                                        dst = bctok[:, tci, 0:ow]
                                    inst = e.scalar_tensor_tensor(dst, pb[tc][:, 0:ow], 0.0,
                                                                  t1s[:, tc2 * 512:tc2 * 512 + ow],
                                                                  ALU.max, ALU.add)
                                prevStt = I(inst, "mV")
                            if gi == 2:
                                W(e, qaccA)
                                e.tensor_reduce(qacc[:, 128:129], qacc[:, 0:64],
                                                mybir.AxisListType.X, ALU.add)
                                e.tensor_reduce(qacc[:, 129:130], qacc[:, 64:128],
                                                mybir.AxisListType.X, ALU.add).then_inc(S["mP"], 1)
                            if gi in QBATCH:
                                emit_qtiles(e, *QBATCH[gi])
                            gi += 1
                assert C["mV"] == convV0 + 4 * 10
                assert C["mP"] == 10 + 4 * NQ
                convV = tok("mV")
                qpassB = tok("mP")

                dbg_tap("bctok", bctok[:].rearrange("p c d -> p (c d)"), [convV])

                @blk.sync
                def _(e):
                    def spill(i):
                        e.wait_ge(S["mP"], qd[i])
                        wname, k, p = QT[i][0], QT[i][1], QT[i][2]
                        if wname == "w1":
                            dst = wq1_sp.ap()[4 * p:4 * p + 4, k].rearrange("m f c -> f m c")
                        else:
                            dst = wq2_sp.ap()[4 * p:4 * p + 4, k].rearrange("m f c -> f m c")
                        par = ("dSQa", "dSQb", "dSQc", "dSQd")[i % 4]
                        e.dma_start(out=dst,
                                    in_=qo1p[i % 4][:].rearrange("f (m c) -> f m c",
                                                                 m=4)).then_inc(S[par], 16)
                        C[par] += 16
                    W(e, qaccA)
                    for i in range(NQ):
                        if i >= 4:
                            spill(i - 4)
                        par = ("dQa", "dQb", "dQc", "dQd")[i % 4]
                        e.dma_start(out=qbuf[i % 4][:], in_=QT[i][3]).then_inc(S[par], 16)
                        C[par] += 16
                    for i in range(NQ - 4, NQ):
                        spill(i)
                qspD = (("dSQa", C["dSQa"]), ("dSQb", C["dSQb"]),
                        ("dSQc", C["dSQc"]), ("dSQd", C["dSQd"]))

        Sa = E(nc.sbuf_tensor("Sa", [64, NH * HD], F32))
        Sab = E(nc.sbuf_tensor("Sab", [64, NH * HD], BF16))
        hbb = E(nc.sbuf_tensor("hbb", [128, 16, 128], BF16))
        xtkb = [E(nc.sbuf_tensor(f"xtkb{i}", [128, DI], BF16)) for i in range(2)]
        ubf = E(nc.sbuf_tensor("ubf", [128, 8, BS], F32))
        castb = E(nc.sbuf_tensor("castb", [128, 8, BS], BF16))
        lnx = E(nc.sbuf_tensor("lnx", [128, 8, BS], BF16))
        u2h0s = E(nc.sbuf_tensor("u2h0s", [128, 4, 256], F32))

        with ExitStack() as s3:
            E3 = s3.enter_context
            bga = E3(nc.sbuf_tensor("bga", [128, NH * HD], BF16, side="right"))
            agst = E3(nc.sbuf_tensor("agst", [64, DI], F32, side="right"))
            gth = E3(nc.sbuf_tensor("gth", [64, 4 * DI], F32, side="right"))

            prevT2 = [None, None]
            for c in range(NCHK):
                bk = 5 + c % 2

                @blk.tensor
                def _(e, c=c, bk=bk, pv=prevT2[c % 2]):
                    W(e, convV, K_ALL)
                    if pv is not None:
                        W(e, pv)
                    e.transpose(pb[bk][0:64, 0:128], bctok[:, c, 0:64], i128_f[:])
                    I(e.transpose(pb[bk][0:64, 128:256], bctok[:, c, 64:128],
                                  i128_f[:]), "mT")
                trT = tok("mT")

                @blk.vector
                def _(e, c=c, bk=bk, trT=trT):
                    W(e, trT)
                    e.tensor_copy(bfeat[:, c, :], pb[bk][0:64, 0:128])
                    I(e.tensor_copy(cfeat[:, c, :], pb[bk][0:64, 128:256]), "mV")
                prevT2[c % 2] = tok("mV")
            bcV = prevT2[1]

            prevS2 = [None, None]
            for c in range(NCHK):
                bk = 5 + c % 2

                @blk.tensor
                def _(e, c=c, bk=bk, pv=prevS2[c % 2]):
                    W(e, bcV)
                    if pv is not None:
                        W(e, pv)
                    I(e.matmul(pb[bk][:, 0:128], bfeat[:, c, :], cfeat[:, c, :],
                               start=True, stop=True), "mT")
                scT = tok("mT")

                @blk.vector
                def _(e, c=c, bk=bk, scT=scT):
                    W(e, scT)
                    I(e.tensor_tensor(scoresM[:, c, :], pb[bk][:, 0:128],
                                      triu_f[:], ALU.mult), "mV")
                prevS2[c % 2] = tok("mV")
            scoresV = prevS2[1]
            dbg_tap("scoresM", scoresM[:].rearrange("p c l -> p (c l)"), [scoresV])

            expT0 = C["mT"]
            expV0 = C["mV"]

            @blk.sync
            def _(e):
                for c in range(NCHK):
                    if c < 2:
                        W(e, *xtspD)
                        W(e, convV)
                    else:
                        e.wait_ge(S["mT"], expT0 + (c - 1))
                    par = "dXTa" if c % 2 == 0 else "dXTb"
                    e.dma_start(out=xtkb[c % 2][:], in_=xt_sp.ap()[c]).then_inc(S[par], 16)
                    C[par] += 16

            bgav2 = [bga[:].rearrange("p (h d) -> p h d", h=NH),
                     hbb[:].rearrange("p a b -> p (a b)").rearrange("p (h d) -> p h d", h=NH)]

            @blk.vector
            def _(e):
                for c in range(NCHK):
                    if c >= 2:
                        e.wait_ge(S["mT"], expT0 + c - 1)
                    W(e, convV)
                    I(e.tensor_tensor(bgav2[c % 2],
                                      bctok[:, c, 0:64].unsqueeze(1).broadcast_to([128, NH, 64]),
                                      dtegt[:, c, :].unsqueeze(2).broadcast_to([128, NH, 64]),
                                      ALU.mult), "mV")
            assert C["mV"] == expV0 + NCHK

            @blk.tensor
            def _(e):
                inst = None
                for c in range(NCHK):
                    par = "dXTa" if c % 2 == 0 else "dXTb"
                    e.wait_ge(S[par], 16 * (c // 2 + 1))
                    e.wait_ge(S["mV"], expV0 + c + 1)
                    for h in range(NH):
                        bank = h // 8
                        col = (h % 8) * 64
                        inst = e.matmul(pb[bank][0:64, col:col + 64], bgav2[c % 2][:, h, :],
                                        xtkb[c % 2][:, h * 64:(h + 1) * 64],
                                        start=(c == 0 and h % 8 == 0),
                                        stop=(c == NCHK - 1 and h % 8 == 7))
                    I(inst, "mT")
            assert C["mT"] == expT0 + NCHK
            expT = tok("mT")
            dbg_tap("xt0", xtkb[0][:], [expT])

            @blk.vector
            def _(e):
                W(e, expT)
                inst = None
                for bank in range(4):
                    inst = e.tensor_copy(agst[:, bank * 512:(bank + 1) * 512], pb[bank][0:64, 0:512])
                I(inst, "mV")
            agV = tok("mV")
            dbg_tap("sfin", agst[:], [agV])

            @blk.sync
            def _(e):
                W(e, agV)
                e.dma_start(out=ag_in.ap(), in_=agst[:]).then_inc(S["dX"], 16)
                C["dX"] += 16

            w2b = [w3buf[i][:].rearrange("p (k c) -> p k c", k=8) for i in range(3)]
            NS = {"GW": 0, "AW": 0, "F1": 0, "F2": 0}

            @blk.sync
            def _(e):
                W(e, *xtspD)
                e.dma_start(out=ubf[:], in_=uview[:, :, 3:3 + BS]).then_inc(S["dUBa"], 16)
                C["dUBa"] += 16
            ubTok0 = tok("dUBa")

            @blk.gpsimd
            def _(e):
                W(e, *xtspD)
                W(e, *qspD)
                W(e, convV)
                e.dma_start(out=lnx[:], in_=uview[:, :, 3:3 + BS]).then_inc(S["dUCa"], 16)
                C["dUCa"] += 16
            ubTok20 = tok("dUCa")

            lnv0 = ln_fm(ubf, lnx, lnx, 8, [(0, 256), (256, 256)],
                         (ubTok0, ubTok20, agV, agV) + tuple(qspD))

            u2T0h = C["mT"]
            gw0h = {p: C[p] for p in ("dGWa", "dGWb", "dGWc")}

            @blk.gpsimd
            def _(e):
                for m in range(16):
                    NS["GW"] += 1
                    if m < 3:
                        W(e, bias3T)
                    else:
                        e.wait_ge(S["mT"], u2T0h + (m - 2))
                    par = ("dGWa", "dGWb", "dGWc")[m % 3]
                    e.dma_start(out=w2b[m % 3],
                                in_=w2_t.ap()[m].rearrange("k f c -> f k c")
                                ).then_inc(S[par], 16)
                    C[par] += 16

            sumV0 = C["mV"]

            @blk.gpsimd
            def _(e):
                e.wait_ge(S["dX"], 16)
                e.collective_compute("AllGather", ALU.bypass,
                                     replica_groups=[list(range(ncores))],
                                     ins=[ag_in.ap().opt()],
                                     outs=[ag_out.ap().opt()]).then_inc(S["cc"], 1)
                C["cc"] += 1
                e.wait_ge(S["cc"], 1)
                nb1 = min(4, ncores)
                for r in range(nb1):
                    e.dma_start(out=gth[0:64, r * DI:(r + 1) * DI],
                                in_=ag_out.ap()[r * 64:(r + 1) * 64, :]).then_inc(S["dG2"], 16)
                    C["dG2"] += 16
                if ncores > 4:
                    e.wait_ge(S["mV"], sumV0 + 4)
                    for r in range(4, ncores):
                        e.dma_start(out=gth[0:64, (r - 4) * DI:(r - 3) * DI],
                                    in_=ag_out.ap()[r * 64:(r + 1) * 64, :]).then_inc(S["dG2"], 16)
                        C["dG2"] += 16

            @blk.vector
            def _(e):
                W(e, ("dG2", 16 * min(4, ncores)), K_ALL, agV)
                prev = None
                for r in range(ncores):
                    if r == 4:
                        e.wait_ge(S["dG2"], 128)
                    W(e, prev)
                    slot = (r % 4) * DI
                    if r == 0:
                        prev = I(e.tensor_scalar(Sa[:], gth[0:64, 0:DI],
                                                 selc[:, 0:1], None, ALU.mult), "mV")
                    else:
                        prev = I(e.scalar_tensor_tensor(Sa[:], gth[0:64, slot:slot + DI],
                                                        selc[:, r:r + 1], Sa[:],
                                                        ALU.mult, ALU.add), "mV")
            sinitV = tok("mV")
            dbg_tap("sinit", Sa[:], [sinitV])

        with ExitStack() as s5:
            E5 = s5.enter_context
            wq2b = [E5(nc.sbuf_tensor(f"wq2b{i}", [128, 16, 128], BF16)) for i in range(2)]
            attn_s = E5(nc.sbuf_tensor("attn_s", [128, 8, BS], F32))
            x1b = E5(nc.sbuf_tensor("x1b", [128, 8, BS], F32))
            w3gv = [w3gb[0][:].rearrange("p (k c) -> p k c", k=8),
                    w3gb[1][:].rearrange("p (k c) -> p k c", k=8),
                    hbb[:, 0:8, :]]
            wq1v = w2b
            znv = castb[:].rearrange("p a b -> p (a b)").rearrange("p (k w) -> p k w", k=16)
            dsubv2 = [t1s[:, 0:512], t1s[:, 512:1024]]
            dexpv2 = [sqs[:, 0:512].rearrange("p (a b) -> p a b", a=4),
                      sqs[:, 512:1024].rearrange("p (a b) -> p a b", a=4)]

            tok_pb01 = agV
            tok_pb23 = agV
            tok_pb45 = convV
            tok_pb67 = scoresV
            tok_castb = None
            tok_ubf = None
            tok_chainEvac = sinitV
            tok_sab_pe = None
            tok_xt_pe = {-2: expT, -1: expT}
            tok_u2_read = None
            tok_znb_read = None
            tok_hsb_read = None
            tok_lnx_read = None
            tok_wbuf = bias3T
            tok_w3gbuf = bias3V
            tok_wq2b = sinitV
            tok_hbb = None

            for j in range(NBLK):
                ucol0 = 3 + j * BS
                upar = "dUBa" if j % 2 == 0 else "dUBb"

                with ExitStack() as sB:
                    EB = sB.enter_context
                    u2 = EB(nc.sbuf_tensor(f"u2_{j}", [128, 16, 256], F32, side="right"))
                    hsb = EB(nc.sbuf_tensor(f"hsb_{j}", [128, 16, 128], F32, side="right"))
                    decS = EB(nc.sbuf_tensor(f"decS_{j}", [128, 16, 128], BF16, side="right"))
                    cdec = EB(nc.sbuf_tensor(f"cdec_{j}", [64, 16, 128], BF16, side="right"))
                    bdv = EB(nc.sbuf_tensor(f"bdv_{j}", [128, 16, HD], BF16, side="right"))
                    lamI = EB(nc.sbuf_tensor(f"lamI_{j}", [64, 16, HD], BF16, side="right"))

                    ucpar = "dUCa" if j % 2 == 0 else "dUCb"
                    tok_castb_blk = tok_castb
                    if j == 0:
                        lnv = lnv0
                        ubTok, ubTok2 = ubTok0, ubTok20
                    else:
                        @blk.sync
                        def _(e, j=j, ucol0=ucol0):
                            W(e, tok_ubf)
                            e.dma_start(out=ubf[:], in_=uview[:, :, ucol0:ucol0 + BS]).then_inc(S[upar], 16)
                            C[upar] += 16
                        ubTok = tok(upar)

                        @blk.gpsimd
                        def _(e, j=j, ucol0=ucol0, ucpar=ucpar):
                            W(e, tok_lnx_u)
                            e.dma_start(out=lnx[:], in_=uview[:, :, ucol0:ucol0 + BS]).then_inc(S[ucpar], 16)
                            C[ucpar] += 16
                        ubTok2 = tok(ucpar)

                        lnv = ln_fm(ubf, lnx, lnx, 8, [(0, 256), (256, 256)],
                                    (ubTok, ubTok2, tok_pb01, tok_pb23))
                    tok_pb01 = lnv
                    tok_pb23 = lnv
                    lnx_free_half = {0: None, 1: None}

                    for half in range(2):
                        hc0 = half * 256
                        u2T0 = C["mT"]
                        u2A0 = C["mA"]

                        gw0 = gw0h if (j == 0 and half == 0) else \
                            {p: C[p] for p in ("dGWa", "dGWb", "dGWc")}
                        if not (j == 0 and half == 0):
                            @blk.gpsimd
                            def _(e, half=half, u2T0=u2T0, tok_wbuf=tok_wbuf):
                                for m in range(16):
                                    NS["GW"] += 1
                                    if m < 3:
                                        W(e, tok_wbuf)
                                    else:
                                        e.wait_ge(S["mT"], u2T0 + (m - 2))
                                    par = ("dGWa", "dGWb", "dGWc")[m % 3]
                                    e.dma_start(out=w2b[m % 3],
                                                in_=w2_t.ap()[m].rearrange("k f c -> f k c")
                                                ).then_inc(S[par], 16)
                                    C[par] += 16
                        else:
                            assert u2T0 == u2T0h

                        @blk.tensor
                        def _(e, half=half, hc0=hc0, u2T0=u2T0, u2A0=u2A0, lnv=lnv):
                            W(e, lnv, tok_pb01, tok_pb23)
                            for m in range(16):
                                par = ("dGWa", "dGWb", "dGWc")[m % 3]
                                e.wait_ge(S[par], gw0[par] + 16 * (m // 3 + 1))
                                if m >= 4:
                                    e.wait_ge(S["mA"], u2A0 + (m - 3))
                                inst = None
                                for k in range(8):
                                    inst = e.matmul(pb[m % 4][:, 0:256],
                                                    w2b[m % 3][:, k, :],
                                                    lnx[:, k, hc0:hc0 + 256],
                                                    start=(k == 0), stop=(k == 7))
                                I(inst, "mT")
                        assert C["mT"] == u2T0 + 16

                        @blk.scalar
                        def _(e, j=j, half=half, u2T0=u2T0, u2A0=u2A0, u2=u2):
                            act_set(e, 10)
                            if j == 0 and half == 0:
                                W(e, ("dX", 16))
                            for m in range(16):
                                e.wait_ge(S["mT"], u2T0 + m + 1)
                                dst = u2[:, m, :]
                                if j == 0 and half == 0 and m < 4:
                                    dst = u2h0s[:, m, :]
                                I(e.activation(dst, pb[m % 4][:, 0:256], AF.Gelu), "mA")
                            act_set(e, 6)
                        assert C["mA"] == u2A0 + 16
                        u2V = tok("mA")
                        tok_pb01 = u2V
                        tok_pb23 = u2V
                        tok_wbuf = ("mT", u2T0 + 16)
                        lnx_free_half[half] = ("mT", u2T0 + 16)

                        for c2 in range(2):
                            ci = j * 4 + half * 2 + c2
                            coff = c2 * 128
                            xpar = "dXTa" if ci % 2 == 0 else "dXTb"

                            @blk.sync
                            def _(e, ci=ci, xpar=xpar):
                                W(e, tok_xt_pe.get(ci - 2, expT))
                                e.dma_start(out=xtkb[ci % 2][:], in_=xt_sp.ap()[ci]).then_inc(S[xpar], 16)
                                C[xpar] += 16
                            xtTok = tok(xpar)

                            @blk.vector
                            def _(e, ci=ci):
                                W(e, tok_chainEvac, tok_sab_pe)
                                I(e.tensor_copy(Sab[:], Sa[:]), "mV")
                            sabV = tok("mV")

                            hh_done = None
                            yTs = []
                            for hh in range(2):
                                hbase = hh * 16
                                pbY0 = 4 + 2 * hh

                                @blk.vector
                                def _(e, ci=ci, hh=hh, hbase=hbase, hh_done=hh_done):
                                    W(e, lamVV, KG_ALL, hh_done)
                                    if hh == 0:
                                        W(e, tok_chainEvac)
                                    I(e.tensor_tensor(
                                        lamI[:],
                                        i64_b[:].unsqueeze(1).broadcast_to([64, 16, 64]),
                                        lam64[:, ci * 32 + hbase:ci * 32 + hbase + 16].unsqueeze(2)
                                        .broadcast_to([64, 16, 64]),
                                        ALU.mult), "mV")
                                lamIV = tok("mV")

                                @blk.vector
                                def _(e, ci=ci, hh=hh, hh_done=hh_done):
                                    W(e, hh_done)
                                    I(e.tensor_tensor(
                                        bdv[:],
                                        bctok[:, ci, 0:64].unsqueeze(1).broadcast_to([128, 16, 64]),
                                        dtet[:, ci, hh * 16:hh * 16 + 16].unsqueeze(2)
                                        .broadcast_to([128, 16, 64]),
                                        ALU.mult), "mV")
                                bdV = tok("mV")

                                prevDec = [hh_done, hh_done]
                                for hg in range(4):
                                    bank = 2 + hg % 2
                                    hgg = hbase + hg * 4

                                    @blk.tensor
                                    def _(e, ci=ci, hgg=hgg, bank=bank, pv=prevDec[hg % 2]):
                                        W(e, tok_pb23, pv, K_ALL)
                                        inst = None
                                        for hi in range(4):
                                            h = hgg + hi
                                            inst = e.matmul(pb[bank][:, hi * 128:(hi + 1) * 128],
                                                            i32_f[:, h:h + 1].broadcast_to([32, 128]),
                                                            aclt[:, ci, :], start=True, stop=True)
                                        I(inst, "mT")
                                    dbT = tok("mT")

                                    @blk.vector
                                    def _(e, ci=ci, hg=hg, hgg=hgg, bank=bank, dbT=dbT):
                                        W(e, dbT)
                                        inst = None
                                        for hi in range(4):
                                            h = hgg + hi
                                            inst = e.tensor_scalar(dsubv2[hg % 2][:, hi * 128:(hi + 1) * 128],
                                                                   pb[bank][:, hi * 128:(hi + 1) * 128],
                                                                   actok[:, ci, h:h + 1], 0.0,
                                                                   ALU.subtract, ALU.min)
                                        I(inst, "mV")
                                    dsV = tok("mV")

                                    @blk.scalar
                                    def _(e, hg=hg, dsV=dsV, pv=prevDec[hg % 2]):
                                        W(e, dsV, pv)
                                        I(e.activation(dexpv2[hg % 2].rearrange("p a b -> p (a b)"),
                                                       dsubv2[hg % 2], AF.Exp), "mA")
                                    dxA = tok("mA")

                                    @blk.vector
                                    def _(e, ci=ci, hg=hg, dxA=dxA):
                                        W(e, dxA)
                                        I(e.tensor_tensor(
                                            decS[:, hg * 4:(hg + 1) * 4, :],
                                            dexpv2[hg % 2],
                                            scoresM[:, ci, :].unsqueeze(1).broadcast_to([128, 4, 128]),
                                            ALU.mult), "mV")
                                    prevDec[hg % 2] = tok("mV")
                                decSV = prevDec[1]

                                prevCd = [prevDec[0], prevDec[1]]
                                for hg in range(4):
                                    bank = 2 + hg % 2
                                    hgg = hbase + hg * 4

                                    @blk.tensor
                                    def _(e, ci=ci, hgg=hgg, bank=bank, pv=prevCd[hg % 2]):
                                        W(e, pv, eacA, KG_ALL)
                                        inst = None
                                        for hi in range(4):
                                            h = hgg + hi
                                            inst = e.matmul(pb[bank][0:64, hi * 128:(hi + 1) * 128],
                                                            i32_b[:, h:h + 1].broadcast_to([32, 64]),
                                                            eac[:, ci, :], start=True, stop=True)
                                        I(inst, "mT")
                                    eaT = tok("mT")

                                    @blk.vector
                                    def _(e, ci=ci, hg=hg, bank=bank, eaT=eaT):
                                        W(e, eaT)
                                        I(e.tensor_tensor(
                                            cdec[:, hg * 4:(hg + 1) * 4, :],
                                            cfeat[:, ci, :].unsqueeze(1).broadcast_to([64, 4, 128]),
                                            pb[bank][0:64, 0:512].rearrange("p (a b) -> p a b", a=4),
                                            ALU.mult), "mV")
                                    prevCd[hg % 2] = tok("mV")
                                cdecV = prevCd[1]
                                tok_pb23 = cdecV

                                pbXf = tok_pb45 if hh == 0 else tok_pb67

                                @blk.tensor
                                def _(e, ci=ci, hh=hh, hbase=hbase, pbY0=pbY0, xtTok=xtTok,
                                      sabV=sabV, decSV=decSV, cdecV=cdecV, pbXf=pbXf):
                                    W(e, xtTok, sabV, decSV, cdecV, pbXf)
                                    inst = None
                                    for hp in range(8):
                                        bank = pbY0 + hp // 4
                                        col = (hp % 4) * 128
                                        for sub in range(2):
                                            h16 = 2 * hp + sub
                                            h = hbase + h16
                                            pr = slice(64 * sub, 64 * (sub + 1))
                                            inst = e.matmul(pb[bank][pr, col:col + 128],
                                                            xtkb[ci % 2][:, h * 64:(h + 1) * 64],
                                                            decS[:, h16, :], start=True, stop=False)
                                            inst = e.matmul(pb[bank][pr, col:col + 128],
                                                            Sab[:, h * 64:(h + 1) * 64],
                                                            cdec[:, h16, :], start=False, stop=True)
                                    I(inst, "mT")
                                yT = tok("mT")
                                yTs.append(yT)

                                @blk.tensor
                                def _(e, ci=ci, hh=hh, hbase=hbase, bdV=bdV, lamIV=lamIV,
                                      sabV=sabV, xtTok=xtTok):
                                    W(e, bdV, lamIV, sabV, xtTok, tok_pb01)
                                    inst = None
                                    for h16 in range(16):
                                        h = hbase + h16
                                        bank = h16 // 8
                                        col = (h16 % 8) * 64
                                        inst = e.matmul(pb[bank][0:64, col:col + 64],
                                                        lamI[:, h16, :], Sab[:, h * 64:(h + 1) * 64],
                                                        start=True, stop=False)
                                        inst = e.matmul(pb[bank][0:64, col:col + 64],
                                                        bdv[:, h16, :], xtkb[ci % 2][:, h * 64:(h + 1) * 64],
                                                        start=False, stop=True)
                                    I(inst, "mT")
                                cwT = tok("mT")

                                @blk.vector
                                def _(e, hh=hh, cwT=cwT):
                                    W(e, cwT)
                                    e.tensor_copy(Sa[:, hh * 1024:hh * 1024 + 512], pb[0][0:64, 0:512])
                                    I(e.tensor_copy(Sa[:, hh * 1024 + 512:hh * 1024 + 1024],
                                                    pb[1][0:64, 0:512]), "mV")
                                chainEv = tok("mV")
                                tok_pb01 = chainEv
                                hh_done = chainEv

                            tok_chainEvac = hh_done
                            tok_sab_pe = hh_done
                            tok_xt_pe[ci] = hh_done

                            @blk.vector
                            def _(e, ci=ci, coff=coff, yTs=tuple(yTs), u2V=u2V, u2=u2):
                                W(e, yTs[0], yTs[1], u2V, tok_hsb_read, tok_hbb)
                                inst = None
                                for b4 in range(4):
                                    src = u2[:, b4 * 4:(b4 + 1) * 4, coff:coff + 128]
                                    if ci < 2 and b4 == 0:
                                        src = u2h0s[:, 0:4, coff:coff + 128]
                                    e.tensor_tensor(
                                        hsb[:, b4 * 4:(b4 + 1) * 4, :],
                                        src,
                                        pb[4 + b4][:, 0:512].rearrange("p (a b) -> p a b", a=4),
                                        ALU.mult)
                                    inst = e.tensor_tensor(
                                        hbb[:, b4 * 4:(b4 + 1) * 4, :],
                                        src,
                                        pb[4 + b4][:, 0:512].rearrange("p (a b) -> p a b", a=4),
                                        ALU.mult)
                                I(inst, "mV")
                            hV = tok("mV")
                            tok_pb45 = hV
                            tok_pb67 = hV
                            tok_u2_read = hV
                            if dbg == "hsb0" and ci == 0:
                                dbg_tap("hsb0", hsb[:].rearrange("p a b -> p (a b)"), [hV])

                            hbV = hV

                            gv = ln_fm(hsb, hbb, znv[:, :, coff:coff + 128], 16, [(0, 128)],
                                       (hV, hbV, tok_pb01, tok_pb23, tok_znb_read,
                                        tok_castb_blk))
                            tok_hsb_read = gv
                            tok_pb01 = gv
                            tok_pb23 = gv
                            znReady = gv
                            if dbg == "zn0" and ci == 1:
                                dbg_tap("zn0", castb[:].rearrange("p a b -> p (a b)"), [gv])

                        atT0 = C["mT"]
                        atV0 = C["mV"]
                        aw0 = {p: C[p] for p in ("dAWa", "dAWb", "dAWc")}

                        @blk.sync
                        def _(e, half=half, atT0=atT0, tok_w3gbuf=tok_w3gbuf):
                            for m in range(8):
                                for piece in range(2):
                                    idx = m * 2 + piece
                                    NS["AW"] += 1
                                    if idx < 3:
                                        W(e, *w3spD)
                                        W(e, tok_w3gbuf, znReady)
                                    else:
                                        e.wait_ge(S["mT"], atT0 + (idx - 3) // 2 + 1)
                                    par = ("dAWa", "dAWb", "dAWc")[idx % 3]
                                    e.dma_start(out=w3gv[idx % 3],
                                                in_=w3g_sp.ap()[m, piece * 8:(piece + 1) * 8]
                                                .rearrange("k f c -> f k c")).then_inc(S[par], 16)
                                    C[par] += 16

                        @blk.tensor
                        def _(e, half=half, hc0=hc0, atT0=atT0, atV0=atV0, znReady=znReady,
                              aw0=aw0):
                            W(e, tok_pb45, tok_pb67, znReady)
                            for m in range(8):
                                if m >= 4:
                                    e.wait_ge(S["mV"], atV0 + (m - 3))
                                inst = None
                                for piece in range(2):
                                    idx = m * 2 + piece
                                    par = ("dAWa", "dAWb", "dAWc")[idx % 3]
                                    e.wait_ge(S[par], aw0[par] + 16 * (idx // 3 + 1))
                                    for k8 in range(8):
                                        k = piece * 8 + k8
                                        inst = e.matmul(pb[4 + m % 4][:, 0:256],
                                                        w3gv[idx % 3][:, k8, :], znv[:, k, :],
                                                        start=(k == 0), stop=(k == 15))
                                I(inst, "mT")
                        assert C["mT"] == atT0 + 8

                        @blk.vector
                        def _(e, j=j, half=half, hc0=hc0, atT0=atT0, atV0=atV0,
                              lnx_free=lnx_free_half[half]):
                            if half == 0:
                                W(e, ("dO", 16 * j))
                            for m in range(8):
                                e.wait_ge(S["mT"], atT0 + m + 1)
                                W(e, bias3V)
                                ev = I(e.tensor_scalar(attn_s[:, m, hc0:hc0 + 256],
                                                       pb[4 + m % 4][:, 0:256],
                                                       bias3[:, m:m + 1], None, ALU.add), "mV")
                                W(e, ev)
                                if m == 0:
                                    W(e, lnx_free)
                                I(e.tensor_copy(lnx[:, m, hc0:hc0 + 256],
                                                attn_s[:, m, hc0:hc0 + 256]), "mP")
                        assert C["mV"] == atV0 + 8
                        attnV = tok("mV")
                        acpV = tok("mP")
                        tok_pb45 = attnV
                        tok_pb67 = attnV
                        tok_znb_read = attnV
                        tok_hbb = ("mT", atT0 + 8)
                        tok_w3gbuf = ("mT", atT0 + 8)

                    if dbg == "attn0" and j == 0:
                        dbg_tap("attn0", attn_s[:].rearrange("p a b -> p (a b)"), [acpV])

                    x1V, x1cV = ln_fm(attn_s, lnx, None, 8, [(0, 256), (256, 256)],
                                      (attnV, acpV, tok_pb01, tok_pb23, ubTok),
                                      res_out=x1b, res_in=ubf,
                                      copy_out=castb, copy_gate=tok_w3gbuf)
                    tok_castb = x1V
                    tok_pb01 = x1V
                    tok_pb23 = x1V
                    tok_ubf = x1V
                    if dbg == "x1b0" and j == 0:
                        dbg_tap("x1b0", x1b[:].rearrange("p a b -> p (a b)"), [x1V])

                    ln2V = ln_fm(x1b, castb, lnx, 8, [(0, 256), (256, 256)],
                                 (x1V, x1cV, tok_pb01, tok_pb23))
                    tok_castb = ln2V
                    tok_pb01 = ln2V
                    tok_pb23 = ln2V
                    tok_lnx_read = None

                with ExitStack() as sF:
                    h1b = sF.enter_context(nc.sbuf_tensor(f"h1b_{j}", [128, 32, BS], BF16, side="right"))
                    wq2b2 = sF.enter_context(nc.sbuf_tensor(f"wq2b2_{j}", [128, 16, 128], BF16, side="right"))
                    wq2b3 = [wq2b[0], wq2b[1], wq2b2]

                    f1T0 = C["mT"]
                    f1A0 = C["mA"]

                    f10 = {p: C[p] for p in ("dF1a", "dF1b", "dF1c")}

                    @blk.sync
                    def _(e, j=j, f1T0=f1T0, tok_wbuf=tok_wbuf):
                        for m in range(32):
                            NS["F1"] += 1
                            if m < 3:
                                W(e, *qspD)
                                W(e, ln2V, tok_wbuf)
                            else:
                                e.wait_ge(S["mT"], f1T0 + (m - 2))
                            par = ("dF1a", "dF1b", "dF1c")[m % 3]
                            e.dma_start(out=wq1v[m % 3],
                                        in_=wq1_sp.ap()[m].rearrange("k f c -> f k c")
                                        ).then_inc(S[par], 16)
                            C[par] += 16

                    @blk.tensor
                    def _(e, f1T0=f1T0, f1A0=f1A0, ln2V=ln2V, f10=f10):
                        W(e, ln2V, tok_pb45, tok_pb67)
                        for m in range(32):
                            par = ("dF1a", "dF1b", "dF1c")[m % 3]
                            e.wait_ge(S[par], f10[par] + 16 * (m // 3 + 1))
                            if m >= 4:
                                e.wait_ge(S["mA"], f1A0 + (m - 3))
                            inst = None
                            for k in range(8):
                                inst = e.matmul(pb[4 + m % 4][:, 0:512], wq1v[m % 3][:, k, :],
                                                lnx[:, k, :], start=(k == 0), stop=(k == 7))
                            I(inst, "mT")
                    assert C["mT"] == f1T0 + 32

                    @blk.scalar
                    def _(e, f1T0=f1T0, f1A0=f1A0):
                        for m in range(32):
                            e.wait_ge(S["mT"], f1T0 + m + 1)
                            W(e, K_ALL)
                            I(e.activation(h1b[:, m, :], pb[4 + m % 4][:, 0:512], AF.Relu,
                                           bias=b1c[:, m:m + 1]), "mA")
                    assert C["mA"] == f1A0 + 32
                    h1V = tok("mA")
                    tok_pb45 = h1V
                    tok_pb67 = h1V
                    tok_wbuf = ("mT", f1T0 + 32)
                    tok_lnx_u = ("mT", f1T0 + 32)
                    if dbg == "h1b0" and j == 0:
                        dbg_tap("h1b0", h1b[:].rearrange("p a b -> p (a b)"), [h1V])

                    f2T0 = C["mT"]
                    f2V0 = C["mV"]

                    @blk.sync
                    def _(e, f2T0=f2T0, tok_wq2b=tok_wq2b, wq2b3=wq2b3):
                        for m in range(8):
                            for piece in range(2):
                                idx = m * 2 + piece
                                NS["F2"] += 1
                                n = NS["F2"] - 1
                                if idx < 3:
                                    W(e, *qspD)
                                    W(e, *xtspD)
                                    W(e, tok_wq2b)
                                else:
                                    e.wait_ge(S["mT"], f2T0 + (idx - 3) // 2 + 1)
                                par = ("dF2a", "dF2b", "dF2c")[n % 3]
                                e.dma_start(out=wq2b3[n % 3][:],
                                            in_=wq2_sp.ap()[m, piece * 16:(piece + 1) * 16]
                                            .rearrange("k f c -> f k c")).then_inc(S[par], 16)
                                C[par] += 16

                    @blk.tensor
                    def _(e, f2T0=f2T0, f2V0=f2V0, h1V=h1V, wq2b3=wq2b3):
                        W(e, h1V, tok_pb67, tok_pb45)
                        nbase = NS["F2"] - 16
                        for m in range(8):
                            if m >= 4:
                                e.wait_ge(S["mV"], f2V0 + (m - 3))
                            inst = None
                            for piece in range(2):
                                n = nbase + m * 2 + piece
                                par = ("dF2a", "dF2b", "dF2c")[n % 3]
                                e.wait_ge(S[par], 16 * (n // 3 + 1))
                                for k16 in range(16):
                                    k = piece * 16 + k16
                                    inst = e.matmul(pb[4 + m % 4][:, 0:512], wq2b3[n % 3][:, k16, :],
                                                    h1b[:, k, :], start=(k == 0), stop=(k == 31))
                            I(inst, "mT")
                    assert C["mT"] == f2T0 + 8

                    @blk.vector
                    def _(e, f2T0=f2T0, f2V0=f2V0, ln2V=ln2V):
                        for m in range(8):
                            e.wait_ge(S["mT"], f2T0 + m + 1)
                            W(e, K_ALL)
                            ev = I(e.tensor_scalar(attn_s[:, m, :], pb[4 + m % 4][:, 0:512],
                                                   b2c[:, m:m + 1], None, ALU.add), "mV")
                            W(e, ev)
                            I(e.tensor_copy(castb[:, m, :], attn_s[:, m, :]), "mP")
                    assert C["mV"] == f2V0 + 8
                    ffoV = tok("mV")
                    fcpV = tok("mP")
                    tok_pb67 = ffoV
                    tok_pb45 = ffoV
                    tok_wq2b = ("mT", f2T0 + 8)
                    if dbg == "ffn0" and j == 0:
                        dbg_tap("ffn0", attn_s[:].rearrange("p a b -> p (a b)"), [ffoV])

                    x2V = ln_fm(attn_s, castb, None, 8, [(0, 256), (256, 256)],
                                (ffoV, fcpV, tok_pb01, tok_pb23),
                                res_out=attn_s, res_in=x1b)
                    tok_castb = x2V
                    tok_pb01 = x2V
                    tok_pb23 = x2V
                    tok_lnx_read = None

                    @blk.scalar
                    def _(e, j=j, x2V=x2V):
                        W(e, x2V)
                        e.dma_start(out=out_ft.ap().rearrange("(a f) t -> f a t", f=128)
                                    [:, :, j * BS:(j + 1) * BS],
                                    in_=attn_s[:]).then_inc(S["dO"], 16)
                        C["dO"] += 16

            @blk.gpsimd
            def _(e):
                W(e, ("dO", C["dO"]))
                if dbg:
                    W(e, ("dDBG", C["dDBG"]))
    return nc



def _host_prepare(inputs):
    W1p = np.ascontiguousarray(np.asarray(inputs["L0_attn_W_in1"], np.float32))
    w1x = np.ascontiguousarray(W1p.T[:, 2048:4256])
    w1x_t = np.zeros((18, 8, 128, 128), np.float32)
    for m in range(18):
        cwd = min(128, 2208 - m * 128)
        w1x_t[m, :, :, :cwd] = w1x[:, m * 128:m * 128 + cwd].reshape(8, 128, cwd)
    convw_t = np.ascontiguousarray(
        np.asarray(inputs["L0_attn_conv_w"], np.float32).transpose(2, 1, 0))
    w2 = np.ascontiguousarray(np.asarray(inputs["L0_attn_W_in2"], np.float32).T)
    w2_t = np.zeros((16, 8, 128, 128), np.float32)
    for m in range(16):
        w2_t[m] = w2[:, m * 128:(m + 1) * 128].reshape(8, 128, 128)
    shared = dict(
        w1x_t=w1x_t, convw_t=convw_t, w2_t=w2_t,
        w3_t=np.ascontiguousarray(np.asarray(inputs["L0_attn_W_in3"], np.float32).T),
        fw1_t=np.ascontiguousarray(np.asarray(inputs["L0_ffn_W1"], np.float32).T),
        fw2_t=np.ascontiguousarray(np.asarray(inputs["L0_ffn_W2"], np.float32).T),
        cb_row=np.asarray(inputs["L0_attn_conv_b"], np.float32).reshape(1, CH),
        b1_pm=np.ascontiguousarray(np.asarray(inputs["L0_ffn_b1"], np.float32).reshape(32, 128).T),
        b2_pm=np.ascontiguousarray(np.asarray(inputs["L0_ffn_b2"], np.float32).reshape(8, 128).T),
        lng_pm=np.ascontiguousarray(np.asarray(inputs["L0_attn_ln_g"], np.float32).reshape(16, 128).T),
        lnb_pm=np.ascontiguousarray(np.asarray(inputs["L0_attn_ln_b"], np.float32).reshape(16, 128).T),
        alog_c=np.asarray(inputs["L0_attn_A_log"], np.float32).reshape(32, 1),
        prelu_pm=np.broadcast_to(np.asarray(inputs["L0_attn_prelu_a"], np.float32)
                                 .reshape(1, 1), (128, 1)).copy(),
        triu_h=np.triu(np.ones((128, 128), np.float32)),
        i128_h=np.eye(128, dtype=np.float32),
        i32_h=np.eye(32, dtype=np.float32),
        i64_h=np.eye(64, dtype=np.float32),
        ones_h=np.ones((128, 128), np.float32),
        onesc_h=np.ones((128, 1), np.float32),
    )
    shared = {k: np.ascontiguousarray(v, dtype=np.float32) for k, v in shared.items()}

    u = np.asarray(inputs["L0_attn_u"], np.float32)
    in_maps = []
    for c in range(NCORES):
        b, hhalf = c // 2, c % 2
        t0 = hhalf * 2048
        useg = np.zeros((TH, D), np.float32)
        useg[3:] = u[b, t0:t0 + 2048]
        if hhalf == 1:
            useg[:3] = u[b, t0 - 3:t0]
        sel = np.zeros((64, 8), np.float32)
        if hhalf == 1:
            sel[:, c - 1] = 1.0
        m = dict(shared)
        m["u_ft"] = np.ascontiguousarray(useg.T)
        m["sel_pm"] = sel
        in_maps.append(m)
    return in_maps


def kernel(**inputs):
    in_maps = _host_prepare(inputs)
    nc = build(DBG)
    res = run_bass_kernel_spmd(nc, in_maps, core_ids=list(range(NCORES)))
    u = np.asarray(inputs["L0_attn_u"])
    out = np.empty((4, 4096, D), np.float32)
    for c in range(NCORES):
        b, hhalf = c // 2, c % 2
        out[b, hhalf * 2048:(hhalf + 1) * 2048] = res.results[c]["out_ft"].T
    if DBG:
        np.save("/tmp/kdbg.npy", np.stack([res.results[c]["dbg"] for c in range(NCORES)]))
    return out


# version 19
# speedup vs baseline: 1.0113x; 1.0113x over previous
import os
import numpy as np
import ml_dtypes
import concourse.bass as bass
import concourse.mybir as mybir
from concourse.bass_utils import run_bass_kernel_spmd
from contextlib import ExitStack

F32 = mybir.dt.float32
BF16 = mybir.dt.bfloat16
I32 = mybir.dt.int32
AF = mybir.ActivationFunctionType
ALU = mybir.AluOpType

D, DI, NH, HD, DS, CH = 1024, 2048, 32, 64, 64, 2176
T, TH, Q, NCHK, BS, NBLK, FF = 2048, 2051, 128, 16, 512, 4, 4096
EPS = 1e-5
RM = 12582912.0
NCORES = 8

DBG = os.environ.get("KDBG", "")
DBG_SHAPES = {
    "ln1": (128, 8 * TH), "xbc": (128, 17 * TH), "dt": (32, TH), "AC": (32, T),
    "bctok": (128, NCHK * 128), "scoresM": (128, NCHK * 128),
    "sfin": (64, DI), "sinit": (64, DI), "xt0": (128, DI),
    "hsb0": (128, 16 * 128), "zn0": (128, 16 * 256),
    "attn0": (128, 8 * BS), "x1b0": (128, 8 * BS),
    "h1b0": (128, 32 * BS), "ffn0": (128, 8 * BS),
}


def build(dbg="", ncores=NCORES):
    nc = bass.Bass("TRN2", target_bir_lowering=False)

    u_ft = nc.dram_tensor("u_ft", [D, TH], F32, kind="ExternalInput")
    w1x_t = nc.dram_tensor("w1x_t", [18, 8, 128, 128], F32, kind="ExternalInput")
    convw_t = nc.dram_tensor("convw_t", [4, CH, CH], F32, kind="ExternalInput")
    w2_t = nc.dram_tensor("w2_t", [16, 8, 128, 128], F32, kind="ExternalInput")
    w3_t = nc.dram_tensor("w3_t", [DI, D], F32, kind="ExternalInput")
    fw1_t = nc.dram_tensor("fw1_t", [D, FF], F32, kind="ExternalInput")
    fw2_t = nc.dram_tensor("fw2_t", [FF, D], F32, kind="ExternalInput")
    cb_row = nc.dram_tensor("cb_row", [1, CH], F32, kind="ExternalInput")
    b1_pm = nc.dram_tensor("b1_pm", [128, 32], F32, kind="ExternalInput")
    b2_pm = nc.dram_tensor("b2_pm", [128, 8], F32, kind="ExternalInput")
    lng_pm = nc.dram_tensor("lng_pm", [128, 16], F32, kind="ExternalInput")
    lnb_pm = nc.dram_tensor("lnb_pm", [128, 16], F32, kind="ExternalInput")
    alog_c = nc.dram_tensor("alog_c", [32, 1], F32, kind="ExternalInput")
    prelu_pm = nc.dram_tensor("prelu_pm", [128, 1], F32, kind="ExternalInput")
    triu_h = nc.dram_tensor("triu_h", [128, 128], F32, kind="ExternalInput")
    i128_h = nc.dram_tensor("i128_h", [128, 128], F32, kind="ExternalInput")
    i32_h = nc.dram_tensor("i32_h", [32, 32], F32, kind="ExternalInput")
    i64_h = nc.dram_tensor("i64_h", [64, 64], F32, kind="ExternalInput")
    ones_h = nc.dram_tensor("ones_h", [128, 128], F32, kind="ExternalInput")
    onesc_h = nc.dram_tensor("onesc_h", [128, 1], F32, kind="ExternalInput")
    sel_pm = nc.dram_tensor("sel_pm", [64, 8], F32, kind="ExternalInput")
    out_ft = nc.dram_tensor("out_ft", [D, T], F32, kind="ExternalOutput")

    w3g_sp = nc.dram_tensor("w3g_sp", [8, 16, 128, 128], BF16)
    wq1_sp = nc.dram_tensor("wq1_sp", [32, 8, 128, 128], BF16)
    wq2_sp = nc.dram_tensor("wq2_sp", [8, 32, 128, 128], BF16)
    xt_sp = nc.dram_tensor("xt_sp", [NCHK, 128, DI], BF16)
    ag_in = nc.dram_tensor("ag_in", [64, DI], F32)
    if ncores % 2 == 0:
        ag_out = nc.dram_tensor("ag_out", [ncores * 64, DI], F32, addr_space="Shared")
    else:
        ag_out = nc.dram_tensor("ag_out", [ncores * 64, DI], F32)

    dbg_t = nc.dram_tensor("dbg", list(DBG_SHAPES[dbg]), F32, kind="ExternalOutput") if dbg else None

    with ExitStack() as st:
        E = st.enter_context
        semnames = ("mT", "mV", "mA", "mP",
                    "dK", "dKg", "dUa", "dUb", "dUga", "dUgb",
                    "dW3a", "dW3b", "dSPa", "dSPb", "dSQa", "dSQb", "dSPXa", "dSPXb",
                    "dIWa", "dIWb", "dQa", "dQb",
                    "dCWa", "dCWb", "dCWc", "dCWd", "fCW", "dF2c",
                    "dQc", "dQd", "dSQc", "dSQd",
                    "dGWc", "dGWd", "dAWc", "dAWd", "dF1c", "dF1d",
                    "dXTa", "dXTb",
                    "dGWa", "dGWb", "dAWa", "dAWb", "dF1a", "dF1b", "dF2a", "dF2b",
                    "dUBa", "dUBb", "dUCa", "dUCb", "dX", "cc", "dG2", "dO", "dDBG")
        S = {n: E(nc.semaphore(n)) for n in semnames}
        C = {n: 0 for n in semnames}

        def W(e, *deps):
            for dep in deps:
                if dep is None:
                    continue
                sname, val = dep
                if val > 0:
                    e.wait_ge(S[sname], val)

        def I(inst, sname, n=1):
            C[sname] += n
            inst.then_inc(S[sname], n)
            return (sname, C[sname])

        def tok(sname):
            return (sname, C[sname])

        pb = [E(nc.psum_tensor(f"pb{i}", [128, 512], F32)) for i in range(8)]

        triu_f = E(nc.sbuf_tensor("triu_f", [128, 128], F32))
        i128_f = E(nc.sbuf_tensor("i128_f", [128, 128], F32))
        i32_f = E(nc.sbuf_tensor("i32_f", [32, 32], F32))
        i64_b = E(nc.sbuf_tensor("i64_b", [64, 64], BF16))
        i32_b = E(nc.sbuf_tensor("i32_b", [32, 32], BF16))
        ones_f = E(nc.sbuf_tensor("ones_f", [128, 1], F32))
        ones_b = E(nc.sbuf_tensor("ones_b", [128, 128], BF16))
        cbrow_b = E(nc.sbuf_tensor("cbrow_b", [1, CH], BF16))
        b1c = E(nc.sbuf_tensor("b1c", [128, 32], F32))
        b2c = E(nc.sbuf_tensor("b2c", [128, 8], F32))
        lngc = E(nc.sbuf_tensor("lngc", [128, 16], F32))
        lnbc_b = E(nc.sbuf_tensor("lnbc_b", [128, 16], BF16))
        alogc = E(nc.sbuf_tensor("alogc", [32, 1], F32))
        preluc = E(nc.sbuf_tensor("preluc", [128, 1], F32))
        selc = E(nc.sbuf_tensor("selc", [64, 8], F32))
        aneg = E(nc.sbuf_tensor("aneg", [32, 1], F32))
        c1c = E(nc.sbuf_tensor("c1c", [128, 1], F32))
        c2c = E(nc.sbuf_tensor("c2c", [128, 1], F32))
        bias3 = E(nc.sbuf_tensor("bias3", [128, 8], F32))
        gam1 = E(nc.sbuf_tensor("gam1", [128, 1], F32))
        gam2 = E(nc.sbuf_tensor("gam2", [128, 1], F32))
        rowsc = E(nc.sbuf_tensor("rowsc", [1, 1032], F32))
        sqs = E(nc.sbuf_tensor("sqs", [128, 2048], BF16))
        t1s = E(nc.sbuf_tensor("t1s", [128, 2048], F32))
        w3buf = [E(nc.sbuf_tensor(f"w3buf{i}", [128, D], BF16)) for i in range(3)]
        w3gb = [E(nc.sbuf_tensor(f"w3gb{i}", [128, D], BF16)) for i in range(2)]

        blk = E(nc.Block())

        actset_n = [0]

        def act_set(e, sid):
            actset_n[0] += 1
            e.add_instruction(mybir.InstLoadActFuncSet(
                name=f"actset{sid}_{actset_n[0]}", act_func_set_id=sid, ins=[], outs=[]))

        def dbg_tap(name, src_ap, wait_toks):
            if dbg != name:
                return

            @blk.gpsimd
            def _(e):
                W(e, *wait_toks)
                e.dma_start(out=dbg_t.ap(), in_=src_ap).then_inc(S["dDBG"], 16)
                C["dDBG"] += 16

        @blk.sync
        def _(e):
            for dst, src in ((triu_f, triu_h.ap()), (i128_f, i128_h.ap()),
                             (i32_f, i32_h.ap()), (ones_f, onesc_h.ap()),
                             (b1c, b1_pm.ap()), (b2c, b2_pm.ap()),
                             (lngc, lng_pm.ap()),
                             (alogc, alog_c.ap()), (preluc, prelu_pm.ap()),
                             (selc, sel_pm.ap())):
                e.dma_start(out=dst[:], in_=src).then_inc(S["dK"], 16)
                C["dK"] += 16

        @blk.gpsimd
        def _(e):
            for dst, src in ((i64_b, i64_h.ap()), (i32_b, i32_h.ap()),
                             (ones_b, ones_h.ap()), (cbrow_b, cb_row.ap()),
                             (lnbc_b, lnb_pm.ap())):
                e.dma_start(out=dst[:], in_=src).then_inc(S["dKg"], 16)
                C["dKg"] += 16
        K_ALL, KG_ALL = tok("dK"), tok("dKg")

        @blk.scalar
        def _(e):
            act_set(e, 6)
            W(e, K_ALL)
            I(e.activation(aneg[:], alogc[:], AF.Exp), "mA")
        anegA = tok("mA")

        @blk.vector
        def _(e):
            W(e, anegA)
            e.tensor_scalar(aneg[:], aneg[:], -1.0, None, ALU.mult)
            e.tensor_scalar(c1c[:], preluc[:], 1.0, 0.5, ALU.add, ALU.mult)
            I(e.tensor_scalar(c2c[:], preluc[:], -0.5, 0.5, ALU.mult, ALU.add), "mV")
        P0V = tok("mV")

        def ln_fm(x_f32, x_b16, out_b16, nk, widths, deps, res_out=None, res_in=None,
                  copy_out=None, copy_gate=None):
            sqv = sqs[:].rearrange("p (k w) -> p k w", k=nk)
            t1v = t1s[:].rearrange("p (k w) -> p k w", k=nk)
            nf = float(nk * 128)
            NW = len(widths)
            assert NW <= 2
            sqT = [None] * NW
            tsT = [None] * NW
            vrT = [None] * NW
            exT = [None] * NW
            bcT = [None] * NW
            mulT = [None] * NW
            cpT = [None] * NW

            stT = [None] * NW
            for i, (c0, wdt) in enumerate(widths):
                @blk.vector
                def _(e, i=i, c0=c0, wdt=wdt):
                    W(e, *deps)
                    if i >= 1:
                        W(e, stT[i - 1])
                    inst = None
                    for k in range(nk):
                        inst = e.tensor_tensor(sqv[:, k, 0:wdt], x_b16[:, k, c0:c0 + wdt],
                                               x_b16[:, k, c0:c0 + wdt], ALU.mult)
                    I(inst, "mV")
                sqT[i] = tok("mV")

                b0 = 2 * i

                @blk.tensor
                def _(e, i=i, c0=c0, wdt=wdt, b0=b0):
                    W(e, *deps, sqT[i], KG_ALL)
                    inst = None
                    for k in range(nk):
                        e.matmul(pb[b0][0:1, 0:wdt], ones_b[:, 0:1], x_b16[:, k, c0:c0 + wdt],
                                 start=(k == 0), stop=(k == nk - 1))
                    for k in range(nk):
                        inst = e.matmul(pb[b0 + 1][0:1, 0:wdt], ones_b[:, 0:1],
                                        sqv[:, k, 0:wdt],
                                        start=(k == 0), stop=(k == nk - 1))
                    I(inst, "mT")
                stT[i] = tok("mT")

            for i, (c0, wdt) in enumerate(widths):
                m0 = i * 256
                v0 = 512 + i * 256
                b0 = 2 * i

                @blk.vector
                def _(e, i=i, wdt=wdt, m0=m0, v0=v0, b0=b0):
                    W(e, stT[i])
                    t0 = I(e.tensor_scalar(rowsc[0:1, m0:m0 + wdt], pb[b0][0:1, 0:wdt],
                                           1.0 / nf, None, ALU.mult), "mV")
                    e.tensor_scalar(rowsc[0:1, v0:v0 + wdt], pb[b0 + 1][0:1, 0:wdt], 1.0 / nf, EPS,
                                    ALU.mult, ALU.add)
                    W(e, t0)
                    t1 = I(e.tensor_tensor(t1s[0:1, m0:m0 + wdt], rowsc[0:1, m0:m0 + wdt],
                                           rowsc[0:1, m0:m0 + wdt], ALU.mult), "mV")
                    W(e, t1)
                    I(e.tensor_tensor(rowsc[0:1, v0:v0 + wdt], rowsc[0:1, v0:v0 + wdt],
                                      t1s[0:1, m0:m0 + wdt], ALU.subtract), "mV")
                vrT[i] = tok("mV")
                tsT[i] = vrT[i]

            for i, (c0, wdt) in enumerate(widths):
                v0 = 512 + i * 256

                @blk.scalar
                def _(e, i=i, wdt=wdt, v0=v0):
                    W(e, vrT[i])
                    I(e.activation(rowsc[0:1, v0:v0 + wdt], rowsc[0:1, v0:v0 + wdt], AF.Ln), "mA")
                lnT = tok("mA")

                @blk.scalar
                def _(e, i=i, wdt=wdt, v0=v0, lnT=lnT):
                    W(e, lnT)
                    I(e.activation(rowsc[0:1, v0:v0 + wdt], rowsc[0:1, v0:v0 + wdt],
                                   AF.Exp, scale=-0.5), "mA")
                exT[i] = tok("mA")

            for i, (c0, wdt) in enumerate(widths):
                m0 = i * 256
                v0 = 512 + i * 256
                bb = 2 * ((i + 1) % NW) if NW == 2 else 2

                @blk.tensor
                def _(e, i=i, wdt=wdt, m0=m0, v0=v0, bb=bb):
                    W(e, exT[i], tsT[(i + 1) % NW] if NW == 2 else tsT[i], K_ALL)
                    e.matmul(pb[bb][:, 0:wdt], triu_f[0:1, :], rowsc[0:1, m0:m0 + wdt],
                             start=True, stop=True)
                    I(e.matmul(pb[bb + 1][:, 0:wdt], triu_f[0:1, :], rowsc[0:1, v0:v0 + wdt],
                               start=True, stop=True), "mT")
                bcT[i] = tok("mT")

            out_tok = None
            for i, (c0, wdt) in enumerate(widths):
                bb = 2 * ((i + 1) % NW) if NW == 2 else 2

                @blk.vector
                def _(e, i=i, c0=c0, wdt=wdt, bb=bb):
                    W(e, bcT[i])
                    if i >= 1:
                        W(e, mulT[i - 1])
                    inst = None
                    for k in range(nk):
                        inst = e.tensor_tensor(t1v[:, k, 0:wdt], x_f32[:, k, c0:c0 + wdt],
                                               pb[bb][:, 0:wdt], ALU.subtract)
                    I(inst, "mV")
                subT = tok("mV")

                @blk.vector
                def _(e, i=i, c0=c0, wdt=wdt, bb=bb, subT=subT):
                    W(e, subT)
                    inst = None
                    for k in range(nk):
                        if res_out is None:
                            inst = e.tensor_tensor(out_b16[:, k, c0:c0 + wdt], t1v[:, k, 0:wdt],
                                                   pb[bb + 1][:, 0:wdt], ALU.mult)
                        else:
                            inst = e.tensor_tensor(t1v[:, k, 0:wdt], t1v[:, k, 0:wdt],
                                                   pb[bb + 1][:, 0:wdt], ALU.mult)
                    I(inst, "mV")
                mulT[i] = tok("mV")

                if res_out is not None:
                    @blk.vector
                    def _(e, i=i, c0=c0, wdt=wdt):
                        W(e, mulT[i])
                        inst = None
                        for k in range(nk):
                            inst = e.tensor_tensor(res_out[:, k, c0:c0 + wdt], t1v[:, k, 0:wdt],
                                                   res_in[:, k, c0:c0 + wdt], ALU.add)
                        I(inst, "mV")
                    mulT[i] = tok("mV")
                if copy_out is not None:
                    @blk.vector
                    def _(e, i=i, c0=c0, wdt=wdt):
                        W(e, mulT[i], copy_gate)
                        inst = None
                        for k in range(nk):
                            inst = e.tensor_copy(copy_out[:, k, c0:c0 + wdt],
                                                 res_out[:, k, c0:c0 + wdt])
                        I(inst, "mP")
                    cpT[i] = tok("mP")
                out_tok = mulT[i]
            if copy_out is not None:
                return out_tok, cpT[NW - 1]
            return out_tok

        v0 = C["mV"]
        b3T0 = C["mT"]

        @blk.gpsimd
        def _(e):
            for k in range(16):
                if k >= 2:
                    e.wait_ge(S["mV"], v0 + (k - 1))
                    e.wait_ge(S["mT"], b3T0 + (k - 1))
                par = "dW3a" if k % 2 == 0 else "dW3b"
                e.dma_start(out=w3buf[k % 2][:],
                            in_=w3_t.ap()[k * 128:(k + 1) * 128, :]).then_inc(S[par], 16)
                C[par] += 16

        @blk.tensor
        def _(e):
            inst = None
            for k in range(16):
                par = "dW3a" if k % 2 == 0 else "dW3b"
                e.wait_ge(S[par], 16 * (k // 2 + 1))
                W(e, KG_ALL)
                for m in range(8):
                    inst = e.matmul(pb[m][:, 0:1], w3buf[k % 2][:, m * 128:(m + 1) * 128],
                                    lnbc_b[:, k:k + 1], start=(k == 0), stop=(k == 15))
                I(inst, "mT")
        assert C["mT"] == b3T0 + 16
        bias3T = tok("mT")

        @blk.vector
        def _(e):
            for k in range(16):
                par = "dW3a" if k % 2 == 0 else "dW3b"
                e.wait_ge(S[par], 16 * (k // 2 + 1))
                if k >= 2:
                    e.wait_ge(S["dSPa" if k % 2 == 0 else "dSPb"], 16 * (k // 2))
                W(e, K_ALL)
                I(e.tensor_scalar(w3gb[k % 2][:], w3buf[k % 2][:], lngc[:, k:k + 1], None, ALU.mult), "mV")
            W(e, bias3T)
            inst = None
            for m in range(8):
                inst = e.tensor_copy(bias3[:, m:m + 1], pb[m][:, 0:1])
            I(inst, "mV")
        bias3V = tok("mV")

        @blk.sync
        def _(e):
            for k in range(16):
                e.wait_ge(S["mV"], v0 + k + 1)
                par = "dSPa" if k % 2 == 0 else "dSPb"
                e.dma_start(out=w3g_sp.ap()[:, k].rearrange("m f c -> f m c"),
                            in_=w3gb[k % 2][:].rearrange("f (m c) -> f m c", m=8)).then_inc(S[par], 16)
                C[par] += 16
        w3spD = (("dSPa", C["dSPa"]), ("dSPb", C["dSPb"]))

        uview = u_ft.ap().rearrange("(a f) t -> f a t", f=128)

        with ExitStack() as s1:
            E1 = s1.enter_context
            xbc = E1(nc.sbuf_tensor("xbc", [128, 17, TH], BF16, side="right"))
            wip = [E1(nc.sbuf_tensor(f"wip{i}", [128, 8, 128], BF16, side="right")) for i in range(2)]
            lamr_f = E1(nc.sbuf_tensor("lamr_f", [1, NCHK * 32], F32, side="right"))
            lamr_b = E1(nc.sbuf_tensor("lamr_b", [1, NCHK * 32], BF16, side="right"))

            with ExitStack() as sD:
                ED = sD.enter_context
                dtr = ED(nc.sbuf_tensor("dtr", [32, TH], F32, side="right"))
                ATt = ED(nc.sbuf_tensor("ATt", [32, T], F32, side="right"))
                ACt = ED(nc.sbuf_tensor("ACt", [32, T], F32, side="right"))
                sp_sb = ED(nc.sbuf_tensor("sp_sb", [32, 2 * T], F32, side="right"))

                with ExitStack() as sL:
                    EL = sL.enter_context
                    ln1 = EL(nc.sbuf_tensor("ln1", [128, 8, TH], BF16, side="right"))
                    ubuf = [EL(nc.sbuf_tensor(f"ubuf{i}", [128, 8, 256], F32, side="right")) for i in range(2)]
                    ubbl = [EL(nc.sbuf_tensor(f"ubbl{i}", [128, 8, 256], BF16, side="right")) for i in range(2)]

                    UB = [(c, 256) for c in range(0, 2048, 256)] + [(2048, 3)]
                    lnToks = []
                    for wi, (c0, wdt) in enumerate(UB):
                        pa = "dUa" if wi % 2 == 0 else "dUb"
                        pg = "dUga" if wi % 2 == 0 else "dUgb"

                        @blk.sync
                        def _(e, wi=wi, c0=c0, wdt=wdt, pa=pa):
                            if wi >= 2:
                                W(e, lnToks[wi - 2])
                            e.dma_start(out=ubuf[wi % 2][:, :, 0:wdt],
                                        in_=uview[:, :, c0:c0 + wdt]).then_inc(S[pa], 16)
                            C[pa] += 16
                        uTok = tok(pa)

                        @blk.gpsimd
                        def _(e, wi=wi, c0=c0, wdt=wdt, pg=pg):
                            if wi >= 2:
                                W(e, lnToks[wi - 2])
                            e.dma_start(out=ubbl[wi % 2][:, :, 0:wdt],
                                        in_=uview[:, :, c0:c0 + wdt]).then_inc(S[pg], 16)
                            C[pg] += 16
                        ugTok = tok(pg)

                        lt = ln_fm(ubuf[wi % 2], ubbl[wi % 2],
                                   ln1[:, :, c0:c0 + wdt], 8, [(0, wdt)],
                                   (uTok, ugTok, P0V, bias3V))
                        lnToks.append(lt)
                    ln1V = lnToks[-1]
                    dbg_tap("ln1", ln1[:].rearrange("p a t -> p (a t)"), [ln1V])

                    IPB = [(0, 512, 0), (512, 512, 1), (1024, 512, 2), (1536, 512, 3), (2048, 3, 4)]
                    ipT0 = C["mT"]
                    vIP0 = C["mV"]

                    @blk.gpsimd
                    def _(e):
                        for m in range(18):
                            if m >= 2:
                                e.wait_ge(S["mT"], ipT0 + (m - 1))
                            par = "dIWa" if m % 2 == 0 else "dIWb"
                            e.dma_start(out=wip[m % 2][:],
                                        in_=w1x_t.ap()[m].rearrange("k f c -> f k c")).then_inc(S[par], 16)
                            C[par] += 16

                    @blk.tensor
                    def _(e):
                        W(e, ln1V, bias3V)
                        for m in range(18):
                            par = "dIWa" if m % 2 == 0 else "dIWb"
                            e.wait_ge(S[par], 16 * (m // 2 + 1))
                            if m >= 1:
                                e.wait_ge(S["mV"], vIP0 + 2 * m - 1)
                            inst = None
                            for k in range(8):
                                for bi, (c0, wdt, _) in enumerate(IPB):
                                    inst = e.matmul(pb[(5 * m + bi) % 8][:, 0:wdt],
                                                    wip[m % 2][:, k, :],
                                                    ln1[:, k, c0:c0 + wdt],
                                                    start=(k == 0), stop=(k == 7))
                            I(inst, "mT")
                    assert C["mT"] == ipT0 + 18
                    ipT = tok("mT")

                    @blk.vector
                    def _(e):
                        for m in range(18):
                            e.wait_ge(S["mT"], ipT0 + m + 1)
                            for phase in range(2):
                                inst = None
                                for bi, (c0, wdt, _) in enumerate(IPB):
                                    if (phase == 0) != (bi < 2):
                                        continue
                                    bank = (5 * m + bi) % 8
                                    if m < 17:
                                        inst = e.tensor_copy(xbc[:, m, c0:c0 + wdt],
                                                             pb[bank][:, 0:wdt])
                                    else:
                                        inst = e.tensor_copy(dtr[0:32, c0:c0 + wdt],
                                                             pb[bank][0:32, 0:wdt])
                                I(inst, "mV")
                    assert C["mV"] == vIP0 + 36
                    ipV = tok("mV")
                    dbg_tap("xbc", xbc[:].rearrange("p m t -> p (m t)"), [ipV])
                    dbg_tap("dt", dtr[:], [ipV])

                bctok = E(nc.sbuf_tensor("bctok", [128, NCHK, 128], F32))
                bfeat = E(nc.sbuf_tensor("bfeat", [64, NCHK, 128], BF16))
                cfeat = E(nc.sbuf_tensor("cfeat", [64, NCHK, 128], BF16))
                scoresM = E(nc.sbuf_tensor("scoresM", [128, NCHK, 128], BF16))
                aclt = E(nc.sbuf_tensor("aclt", [32, NCHK, 128], F32))
                eac = E(nc.sbuf_tensor("eac", [32, NCHK, 128], BF16))
                actok = E(nc.sbuf_tensor("actok", [128, NCHK, 32], F32))
                dtet = E(nc.sbuf_tensor("dtet", [128, NCHK, 32], F32))
                dtegt = E(nc.sbuf_tensor("dtegt", [128, NCHK, 32], F32))
                lam64 = E(nc.sbuf_tensor("lam64", [64, NCHK * 32], BF16))

                @blk.scalar
                def _(e):
                    W(e, ipV, P0V)
                    I(e.activation(sp_sb[:, 0:T], dtr[:, 3:3 + T], AF.Abs), "mA")
                absV = tok("mA")

                @blk.scalar
                def _(e):
                    W(e, absV)
                    I(e.activation(sp_sb[:, T:2 * T], sp_sb[:, 0:T], AF.Exp, scale=-1.0), "mA")
                spE = tok("mA")

                @blk.vector
                def _(e):
                    W(e, spE)
                    I(e.tensor_scalar(sp_sb[:, T:2 * T], sp_sb[:, T:2 * T], 1.0, None, ALU.add), "mV")
                spP1 = tok("mV")

                @blk.scalar
                def _(e):
                    W(e, spP1)
                    I(e.activation(sp_sb[:, T:2 * T], sp_sb[:, T:2 * T], AF.Ln), "mA")
                spA = tok("mA")

                @blk.vector
                def _(e):
                    W(e, spA)
                    t0 = I(e.tensor_scalar(sp_sb[:, 0:T], dtr[:, 3:3 + T], 0.0, None, ALU.max), "mV")
                    W(e, t0)
                    t1 = I(e.tensor_tensor(ATt[:], sp_sb[:, 0:T], sp_sb[:, T:2 * T], ALU.add), "mV")
                    W(e, t1)
                    t2 = I(e.tensor_scalar(sp_sb[:, 0:T], ATt[:], aneg[:, 0:1], None, ALU.mult), "mV")
                    W(e, t2)
                    I(e.tensor_tensor_scan(ACt[:], sp_sb[:, 0:T], sp_sb[:, 0:T], 0.0, ALU.add, ALU.bypass), "mV")
                acV = tok("mV")
                dbg_tap("AC", ACt[:], [acV])

                @blk.vector
                def _(e):
                    W(e, acV)
                    inst = None
                    for c in range(NCHK):
                        if c == 0:
                            inst = e.tensor_copy(aclt[:, 0, :], ACt[:, 0:128])
                        else:
                            inst = e.tensor_scalar(aclt[:, c, :], ACt[:, c * 128:(c + 1) * 128],
                                                   ACt[:, c * 128 - 1:c * 128], None, ALU.subtract)
                    inst = e.tensor_scalar(sp_sb[:, 0:T], ACt[:], ACt[:, T - 1:T], -1.0,
                                           ALU.subtract, ALU.mult)
                    I(inst, "mV")
                aclV = tok("mV")

                @blk.scalar
                def _(e):
                    W(e, aclV)
                    inst = None
                    for c in range(NCHK):
                        inst = e.activation(eac[:, c, :], aclt[:, c, :], AF.Exp)
                    inst = e.activation(sp_sb[:, T:2 * T], sp_sb[:, 0:T], AF.Exp)
                    I(inst, "mA")
                eacA = tok("mA")

                prevCp = None
                for c in range(NCHK):
                    @blk.vector
                    def _(e, c=c, pv=prevCp):
                        W(e, eacA)
                        if pv is not None:
                            W(e, pv)
                        I(e.tensor_scalar(sp_sb[:, 0:128], aclt[:, c, :],
                                          aclt[:, c, 127:128], -1.0,
                                          ALU.subtract, ALU.mult), "mV")
                    subV = tok("mV")

                    @blk.scalar
                    def _(e, subV=subV):
                        W(e, subV)
                        I(e.activation(sp_sb[:, 128:256], sp_sb[:, 0:128], AF.Exp), "mA")
                    dteA = tok("mA")

                    @blk.tensor
                    def _(e, c=c, dteA=dteA, pv=prevCp):
                        W(e, dteA, eacA, K_ALL)
                        if pv is not None:
                            W(e, pv)
                        e.transpose(pb[5][0:128, 0:32], aclt[:, c, :], i32_f[:])
                        e.transpose(pb[5][0:128, 32:64], sp_sb[:, 128:256], i32_f[:])
                        e.transpose(pb[5][0:128, 64:96], sp_sb[:, T + c * 128:T + (c + 1) * 128], i32_f[:])
                        I(e.transpose(pb[5][0:1, 96:128], aclt[:, c, 127:128], i32_f[:]), "mT")
                    trT = tok("mT")

                    @blk.vector
                    def _(e, c=c, trT=trT):
                        W(e, trT)
                        e.tensor_copy(actok[:, c, :], pb[5][0:128, 0:32])
                        e.tensor_copy(dtet[:, c, :], pb[5][0:128, 32:64])
                        e.tensor_copy(dtegt[:, c, :], pb[5][0:128, 64:96])
                        I(e.tensor_copy(lamr_f[0:1, c * 32:(c + 1) * 32], pb[5][0:1, 96:128]), "mV")
                    prevCp = tok("mV")
                prepV = prevCp

                @blk.scalar
                def _(e):
                    W(e, prepV)
                    I(e.activation(lamr_f[:], lamr_f[:], AF.Exp), "mA")
                lamA = tok("mA")

                @blk.vector
                def _(e):
                    W(e, lamA)
                    I(e.tensor_copy(lamr_b[:], lamr_f[:]), "mV")
                lamV = tok("mV")

                @blk.tensor
                def _(e):
                    W(e, lamV, KG_ALL)
                    inst = None
                    for c in range(NCHK):
                        inst = e.matmul(pb[6][0:64, c * 32:(c + 1) * 32], ones_b[0:1, 0:64],
                                        lamr_b[0:1, c * 32:(c + 1) * 32], start=True, stop=True)
                    I(inst, "mT")
                lamT = tok("mT")

                @blk.vector
                def _(e):
                    W(e, lamT)
                    I(e.tensor_copy(lam64[:], pb[6][0:64, 0:512]), "mV")
                lamVV = tok("mV")

            with ExitStack() as s2:
                E2 = s2.enter_context
                qbuf = [E2(nc.sbuf_tensor(f"qbuf{i}", [128, 512], F32, side="right")) for i in range(4)]
                qacc = E2(nc.sbuf_tensor("qacc", [128, 132], F32, side="right"))
                qw1p = [E2(nc.sbuf_tensor(f"qw1p{i}", [128, 512], F32, side="right")) for i in range(2)]
                qi1p = [E2(nc.sbuf_tensor(f"qi1p{i}", [128, 512], I32, side="right")) for i in range(2)]
                qo1p = [E2(nc.sbuf_tensor(f"qo1p{i}", [128, 512], BF16, side="right")) for i in range(4)]
                qw1 = qw1p[0]
                cw = [E2(nc.sbuf_tensor(f"cw{i}", [128, 512], BF16, side="right")) for i in range(4)]
                cst = E2(nc.sbuf_tensor("cst", [128, 8, 512], BF16, side="right"))

                QT = ([("w1", k, p, fw1_t.ap()[k * 128:(k + 1) * 128, p * 512:(p + 1) * 512])
                       for k in range(8) for p in range(8)] +
                      [("w2", k, p, fw2_t.ap()[k * 128:(k + 1) * 128, p * 512:(p + 1) * 512])
                       for k in range(32) for p in range(2)])
                NQ = len(QT)

                aQ0 = C["mA"]

                @blk.sync
                def _(e):
                    for i in range(NQ):
                        if i < 2:
                            W(e, ipT, prepV, lamVV)
                        else:
                            e.wait_ge(S["mA"], aQ0 + (i - 1))
                        par = "dQa" if i % 2 == 0 else "dQb"
                        e.dma_start(out=qbuf[i % 2][:], in_=QT[i][3]).then_inc(S[par], 16)
                        C[par] += 16
                qaTotA, qaTotB = C["dQa"], C["dQb"]

                @blk.scalar
                def _(e):
                    for i in range(NQ):
                        par = "dQa" if i % 2 == 0 else "dQb"
                        e.wait_ge(S[par], 16 * (i // 2 + 1))
                        if i >= 1:
                            e.wait_ge(S["mA"], aQ0 + i)
                        I(e.activation(qw1[:], qbuf[i % 2][:], AF.Abs,
                                       accum_out=qacc[:, i:i + 1]), "mA")
                assert C["mA"] == aQ0 + NQ
                qaccA = tok("mA")

                OBLK = [(0, 512), (512, 512), (1024, 512), (1536, 512), (2048, 128)]
                KI = [(kk, ii) for kk in range(4) for ii in range(17)]
                convT0 = C["mT"]
                convV0 = C["mV"]


                @blk.gpsimd
                def _(e):
                    n = 0
                    gi = 0
                    for (o0, ow) in OBLK:
                        for tg in range(2):
                            si = gi - 1
                            if 0 <= si < 8:
                                e.wait_ge(S["mV"], convV0 + 4 * (si + 1))
                                spar = "dSPXa" if si % 2 == 0 else "dSPXb"
                                so0 = OBLK[si // 2][0]
                                stg = si % 2
                                dst = xt_sp.ap()[stg * 8:(stg + 1) * 8, :, so0:so0 + 512] \
                                    .rearrange("c p o -> p c o")
                                e.dma_start(out=dst, in_=cst[:, :, 0:512]).then_inc(S[spar], 16)
                                C[spar] += 16
                            for (kk, ii) in KI:
                                if n < 4:
                                    W(e, ipV, prepV, lamVV)
                                else:
                                    e.wait_ge(S["fCW"], n - 3)
                                par = ("dCWa", "dCWb", "dCWc", "dCWd")[n % 4]
                                e.dma_start(out=cw[n % 4][:, 0:ow],
                                            in_=convw_t.ap()[kk, ii * 128:(ii + 1) * 128,
                                                             o0:o0 + ow]).then_inc(S[par], 16)
                                C[par] += 16
                                n += 1
                            gi += 1
                xtspD = (("dSPXa", C["dSPXa"]), ("dSPXb", C["dSPXb"]))

                @blk.tensor
                def _(e):
                    W(e, ipV, KG_ALL)
                    n = 0
                    gi = 0
                    for (o0, ow) in OBLK:
                        for tg in range(2):
                            if gi == 3:
                                e.wait_ge(S["mP"], 1)
                                e.wait_ge(S["mV"], convV0 + 4 * 3)
                                e.matmul(pb[7][0:1, 0:1], ones_f[:, 0:1], qacc[:, 128:129],
                                         start=True, stop=True).then_inc(S["mP"], 1)
                                e.matmul(pb[7][0:1, 1:2], ones_f[:, 0:1], qacc[:, 129:130],
                                         start=True, stop=True).then_inc(S["mP"], 1)
                                e.wait_ge(S["mP"], 6)
                                e.matmul(pb[7][:, 2:3], triu_f[0:1, :], rowsc[0:1, 1024:1025],
                                         start=True, stop=True).then_inc(S["mP"], 1)
                                e.matmul(pb[7][:, 3:4], triu_f[0:1, :], rowsc[0:1, 1025:1026],
                                         start=True, stop=True).then_inc(S["mP"], 1)
                                e.wait_ge(S["mP"], 10)
                            if gi >= 1:
                                e.wait_ge(S["mV"], convV0 + 4 * gi)
                            inst = None
                            for ki, (kk, ii) in enumerate(KI):
                                par = ("dCWa", "dCWb", "dCWc", "dCWd")[n % 4]
                                e.wait_ge(S[par], 16 * (n // 4 + 1))
                                for tc in range(8):
                                    tci = tg * 8 + tc
                                    tcol = tci * 128 + kk
                                    inst = e.matmul(pb[tc][:, 0:ow], xbc[:, ii, tcol:tcol + 128],
                                                    cw[n % 4][:, 0:ow], start=(ki == 0), stop=False)
                                I(inst, "fCW")
                                n += 1
                            for tc in range(8):
                                inst = e.matmul(pb[tc][:, 0:ow], ones_b[0:1, :],
                                                cbrow_b[0:1, o0:o0 + ow], start=False, stop=True)
                            I(inst, "mT")
                            gi += 1

                @blk.scalar
                def _(e):
                    e.wait_ge(S["mP"], 4)
                    e.activation(rowsc[0:1, 1024:1026], rowsc[0:1, 1024:1026],
                                 AF.Ln).then_inc(S["mP"], 1)
                    e.wait_ge(S["mP"], 5)
                    e.activation(rowsc[0:1, 1024:1026], rowsc[0:1, 1024:1026],
                                 AF.Exp, scale=-1.0).then_inc(S["mP"], 1)

                qd = {-2: 10, -1: 10}

                def emit_qtiles(e, i0, i1):
                    ids = list(range(i0, i1))
                    for p in range(0, len(ids), 2):
                        pair = ids[p:p + 2]
                        toks = {}
                        for step in range(4):
                            for t in pair:
                                b = t % 2
                                if step == 0:
                                    par = ("dQa", "dQb", "dQc", "dQd")[t % 4]
                                    base = (qaTotA, qaTotB, 0, 0)[t % 4]
                                    e.wait_ge(S[par], base + 16 * (t // 4 + 1))
                                    if t >= 4:
                                        e.wait_ge(S[("dSQa", "dSQb", "dSQc", "dSQd")[t % 4]],
                                                  16 * ((t - 4) // 4 + 1))
                                    e.wait_ge(S["mP"], qd[t - 2])
                                    g = gam1 if QT[t][0] == "w1" else gam2
                                    toks[t] = I(e.tensor_scalar(qw1p[b][:], qbuf[t % 4][:],
                                                                g[:, 0:1], RM,
                                                                ALU.mult, ALU.add), "mP")[1]
                                elif step == 1:
                                    e.wait_ge(S["mP"], toks[t])
                                    toks[t] = I(e.tensor_copy(qi1p[b][:], qw1p[b][:]), "mP")[1]
                                elif step == 2:
                                    e.wait_ge(S["mP"], toks[t])
                                    toks[t] = I(e.tensor_scalar(qw1p[b][:], qi1p[b][:],
                                                                -RM, 1.0,
                                                                ALU.add, ALU.min), "mP")[1]
                                else:
                                    e.wait_ge(S["mP"], toks[t])
                                    toks[t] = I(e.tensor_scalar(qo1p[t % 4][:], qw1p[b][:],
                                                                -1.0, None,
                                                                ALU.max), "mP")[1]
                                    qd[t] = toks[t]

                QBATCH = {4: (0, 22), 5: (22, 44), 6: (44, 66), 7: (66, 88),
                          8: (88, 108), 9: (108, 128)}

                @blk.vector
                def _(e):
                    gi = 0
                    prevStt = None
                    for obi, (o0, ow) in enumerate(OBLK):
                        for tg in range(2):
                            if gi == 3:
                                e.wait_ge(S["mP"], 3)
                                e.tensor_scalar(rowsc[0:1, 1024:1025], pb[7][0:1, 0:1],
                                                1.0 / (FF * D), EPS, ALU.mult, ALU.add)
                                e.tensor_scalar(rowsc[0:1, 1025:1026], pb[7][0:1, 1:2],
                                                1.0 / (FF * D), EPS,
                                                ALU.mult, ALU.add).then_inc(S["mP"], 1)
                                e.wait_ge(S["mP"], 8)
                                e.tensor_copy(gam1[:], pb[7][:, 2:3]).then_inc(S["mP"], 1)
                                e.tensor_copy(gam2[:], pb[7][:, 3:4]).then_inc(S["mP"], 1)
                                C["mP"] = 10
                            W(e, P0V, lamVV)
                            if 1 <= gi <= 8 and obi <= 4:
                                si = gi - 1
                                if si < 8:
                                    e.wait_ge(S["dSPXa" if si % 2 == 0 else "dSPXb"],
                                              16 * (si // 2 + 1))
                            for half in range(2):
                                e.wait_ge(S["mT"], convT0 + gi + 1)
                                W(e, prevStt)
                                inst = None
                                for tc2 in range(4):
                                    tc = half * 4 + tc2
                                    inst = e.scalar_tensor_tensor(
                                        t1s[:, tc2 * 512:tc2 * 512 + ow], pb[tc][:, 0:ow],
                                        0.0, preluc[:, 0:1].broadcast_to([128, ow]),
                                        ALU.min, ALU.mult)
                                negT = I(inst, "mV")
                                W(e, negT)
                                inst = None
                                for tc2 in range(4):
                                    tc = half * 4 + tc2
                                    tci = tg * 8 + tc
                                    if o0 < DI:
                                        dst = cst[:, tc, 0:ow]
                                    else:
                                        dst = bctok[:, tci, 0:ow]
                                    inst = e.scalar_tensor_tensor(dst, pb[tc][:, 0:ow], 0.0,
                                                                  t1s[:, tc2 * 512:tc2 * 512 + ow],
                                                                  ALU.max, ALU.add)
                                prevStt = I(inst, "mV")
                            if gi == 2:
                                W(e, qaccA)
                                e.tensor_reduce(qacc[:, 128:129], qacc[:, 0:64],
                                                mybir.AxisListType.X, ALU.add)
                                e.tensor_reduce(qacc[:, 129:130], qacc[:, 64:128],
                                                mybir.AxisListType.X, ALU.add).then_inc(S["mP"], 1)
                            if gi in QBATCH:
                                emit_qtiles(e, *QBATCH[gi])
                            gi += 1
                assert C["mV"] == convV0 + 4 * 10
                assert C["mP"] == 10 + 4 * NQ
                convV = tok("mV")
                qpassB = tok("mP")

                dbg_tap("bctok", bctok[:].rearrange("p c d -> p (c d)"), [convV])

                @blk.sync
                def _(e):
                    def spill(i):
                        e.wait_ge(S["mP"], qd[i])
                        wname, k, p = QT[i][0], QT[i][1], QT[i][2]
                        if wname == "w1":
                            dst = wq1_sp.ap()[4 * p:4 * p + 4, k].rearrange("m f c -> f m c")
                        else:
                            dst = wq2_sp.ap()[4 * p:4 * p + 4, k].rearrange("m f c -> f m c")
                        par = ("dSQa", "dSQb", "dSQc", "dSQd")[i % 4]
                        e.dma_start(out=dst,
                                    in_=qo1p[i % 4][:].rearrange("f (m c) -> f m c",
                                                                 m=4)).then_inc(S[par], 16)
                        C[par] += 16
                    W(e, qaccA)
                    for i in range(NQ):
                        if i >= 4:
                            spill(i - 4)
                        par = ("dQa", "dQb", "dQc", "dQd")[i % 4]
                        e.dma_start(out=qbuf[i % 4][:], in_=QT[i][3]).then_inc(S[par], 16)
                        C[par] += 16
                    for i in range(NQ - 4, NQ):
                        spill(i)
                qspD = (("dSQa", C["dSQa"]), ("dSQb", C["dSQb"]),
                        ("dSQc", C["dSQc"]), ("dSQd", C["dSQd"]))

        Sa = E(nc.sbuf_tensor("Sa", [64, NH * HD], F32))
        Sab = E(nc.sbuf_tensor("Sab", [64, NH * HD], BF16))
        hbb = E(nc.sbuf_tensor("hbb", [128, 16, 128], BF16))
        xtkb = [E(nc.sbuf_tensor(f"xtkb{i}", [128, DI], BF16)) for i in range(2)]
        ubf = E(nc.sbuf_tensor("ubf", [128, 8, BS], F32))
        castb = E(nc.sbuf_tensor("castb", [128, 8, BS], BF16))
        lnx = E(nc.sbuf_tensor("lnx", [128, 8, BS], BF16))
        u2h0s = E(nc.sbuf_tensor("u2h0s", [128, 4, 256], F32))

        with ExitStack() as s3:
            E3 = s3.enter_context
            bga = E3(nc.sbuf_tensor("bga", [128, NH * HD], BF16, side="right"))
            agst = E3(nc.sbuf_tensor("agst", [64, DI], F32, side="right"))
            gth = E3(nc.sbuf_tensor("gth", [64, 4 * DI], F32, side="right"))

            prevT2 = [None, None]
            for c in range(NCHK):
                bk = 5 + c % 2

                @blk.tensor
                def _(e, c=c, bk=bk, pv=prevT2[c % 2]):
                    W(e, convV, K_ALL)
                    if pv is not None:
                        W(e, pv)
                    e.transpose(pb[bk][0:64, 0:128], bctok[:, c, 0:64], i128_f[:])
                    I(e.transpose(pb[bk][0:64, 128:256], bctok[:, c, 64:128],
                                  i128_f[:]), "mT")
                trT = tok("mT")

                @blk.vector
                def _(e, c=c, bk=bk, trT=trT):
                    W(e, trT)
                    e.tensor_copy(bfeat[:, c, :], pb[bk][0:64, 0:128])
                    I(e.tensor_copy(cfeat[:, c, :], pb[bk][0:64, 128:256]), "mV")
                prevT2[c % 2] = tok("mV")
            bcV = prevT2[1]

            prevS2 = [None, None]
            for c in range(NCHK):
                bk = 5 + c % 2

                @blk.tensor
                def _(e, c=c, bk=bk, pv=prevS2[c % 2]):
                    W(e, bcV)
                    if pv is not None:
                        W(e, pv)
                    I(e.matmul(pb[bk][:, 0:128], bfeat[:, c, :], cfeat[:, c, :],
                               start=True, stop=True), "mT")
                scT = tok("mT")

                @blk.vector
                def _(e, c=c, bk=bk, scT=scT):
                    W(e, scT)
                    I(e.tensor_tensor(scoresM[:, c, :], pb[bk][:, 0:128],
                                      triu_f[:], ALU.mult), "mV")
                prevS2[c % 2] = tok("mV")
            scoresV = prevS2[1]
            dbg_tap("scoresM", scoresM[:].rearrange("p c l -> p (c l)"), [scoresV])

            expT0 = C["mT"]
            expV0 = C["mV"]

            @blk.sync
            def _(e):
                for c in range(NCHK):
                    if c < 2:
                        W(e, *xtspD)
                        W(e, convV)
                    else:
                        e.wait_ge(S["mT"], expT0 + (c - 1))
                    par = "dXTa" if c % 2 == 0 else "dXTb"
                    e.dma_start(out=xtkb[c % 2][:], in_=xt_sp.ap()[c]).then_inc(S[par], 16)
                    C[par] += 16

            bgav2 = [bga[:].rearrange("p (h d) -> p h d", h=NH),
                     hbb[:].rearrange("p a b -> p (a b)").rearrange("p (h d) -> p h d", h=NH)]

            @blk.vector
            def _(e):
                for c in range(NCHK):
                    if c >= 2:
                        e.wait_ge(S["mT"], expT0 + c - 1)
                    W(e, convV)
                    I(e.tensor_tensor(bgav2[c % 2],
                                      bctok[:, c, 0:64].unsqueeze(1).broadcast_to([128, NH, 64]),
                                      dtegt[:, c, :].unsqueeze(2).broadcast_to([128, NH, 64]),
                                      ALU.mult), "mV")
            assert C["mV"] == expV0 + NCHK

            @blk.tensor
            def _(e):
                inst = None
                for c in range(NCHK):
                    par = "dXTa" if c % 2 == 0 else "dXTb"
                    e.wait_ge(S[par], 16 * (c // 2 + 1))
                    e.wait_ge(S["mV"], expV0 + c + 1)
                    for h in range(NH):
                        bank = h // 8
                        col = (h % 8) * 64
                        inst = e.matmul(pb[bank][0:64, col:col + 64], bgav2[c % 2][:, h, :],
                                        xtkb[c % 2][:, h * 64:(h + 1) * 64],
                                        start=(c == 0 and h % 8 == 0),
                                        stop=(c == NCHK - 1 and h % 8 == 7))
                    I(inst, "mT")
            assert C["mT"] == expT0 + NCHK
            expT = tok("mT")
            dbg_tap("xt0", xtkb[0][:], [expT])

            @blk.vector
            def _(e):
                W(e, expT)
                inst = None
                for bank in range(4):
                    inst = e.tensor_copy(agst[:, bank * 512:(bank + 1) * 512], pb[bank][0:64, 0:512])
                I(inst, "mV")
            agV = tok("mV")
            dbg_tap("sfin", agst[:], [agV])

            @blk.sync
            def _(e):
                W(e, agV)
                e.dma_start(out=ag_in.ap(), in_=agst[:]).then_inc(S["dX"], 16)
                C["dX"] += 16

            w2b = [w3buf[i][:].rearrange("p (k c) -> p k c", k=8) for i in range(3)]
            NS = {"GW": 0, "AW": 0, "F1": 0, "F2": 0}

            @blk.sync
            def _(e):
                W(e, *xtspD)
                e.dma_start(out=ubf[:], in_=uview[:, :, 3:3 + BS]).then_inc(S["dUBa"], 16)
                C["dUBa"] += 16
            ubTok0 = tok("dUBa")

            @blk.gpsimd
            def _(e):
                W(e, *xtspD)
                W(e, *qspD)
                W(e, convV)
                e.dma_start(out=lnx[:], in_=uview[:, :, 3:3 + BS]).then_inc(S["dUCa"], 16)
                C["dUCa"] += 16
            ubTok20 = tok("dUCa")

            lnv0 = ln_fm(ubf, lnx, lnx, 8, [(0, 256), (256, 256)],
                         (ubTok0, ubTok20, agV, agV) + tuple(qspD))

            u2T0h = C["mT"]
            gw0h = {p: C[p] for p in ("dGWa", "dGWb", "dGWc")}

            @blk.gpsimd
            def _(e):
                for m in range(16):
                    NS["GW"] += 1
                    if m < 3:
                        W(e, bias3T)
                    else:
                        e.wait_ge(S["mT"], u2T0h + (m - 2))
                    par = ("dGWa", "dGWb", "dGWc")[m % 3]
                    e.dma_start(out=w2b[m % 3],
                                in_=w2_t.ap()[m].rearrange("k f c -> f k c")
                                ).then_inc(S[par], 16)
                    C[par] += 16

            sumV0 = C["mV"]

            @blk.gpsimd
            def _(e):
                e.wait_ge(S["dX"], 16)
                e.collective_compute("AllGather", ALU.bypass,
                                     replica_groups=[list(range(ncores))],
                                     ins=[ag_in.ap().opt()],
                                     outs=[ag_out.ap().opt()]).then_inc(S["cc"], 1)
                C["cc"] += 1
                e.wait_ge(S["cc"], 1)
                nb1 = min(4, ncores)
                for r in range(nb1):
                    e.dma_start(out=gth[0:64, r * DI:(r + 1) * DI],
                                in_=ag_out.ap()[r * 64:(r + 1) * 64, :]).then_inc(S["dG2"], 16)
                    C["dG2"] += 16
                if ncores > 4:
                    e.wait_ge(S["mV"], sumV0 + 4)
                    for r in range(4, ncores):
                        e.dma_start(out=gth[0:64, (r - 4) * DI:(r - 3) * DI],
                                    in_=ag_out.ap()[r * 64:(r + 1) * 64, :]).then_inc(S["dG2"], 16)
                        C["dG2"] += 16

            @blk.vector
            def _(e):
                W(e, ("dG2", 16 * min(4, ncores)), K_ALL, agV)
                prev = None
                for r in range(ncores):
                    if r == 4:
                        e.wait_ge(S["dG2"], 128)
                    W(e, prev)
                    slot = (r % 4) * DI
                    if r == 0:
                        prev = I(e.tensor_scalar(Sa[:], gth[0:64, 0:DI],
                                                 selc[:, 0:1], None, ALU.mult), "mV")
                    else:
                        prev = I(e.scalar_tensor_tensor(Sa[:], gth[0:64, slot:slot + DI],
                                                        selc[:, r:r + 1], Sa[:],
                                                        ALU.mult, ALU.add), "mV")
            sinitV = tok("mV")
            dbg_tap("sinit", Sa[:], [sinitV])

        with ExitStack() as s5:
            E5 = s5.enter_context
            wq2b = [E5(nc.sbuf_tensor(f"wq2b{i}", [128, 16, 128], BF16)) for i in range(2)]
            attn_s = E5(nc.sbuf_tensor("attn_s", [128, 8, BS], F32))
            x1b = E5(nc.sbuf_tensor("x1b", [128, 8, BS], F32))
            w3gv = [w3gb[0][:].rearrange("p (k c) -> p k c", k=8),
                    w3gb[1][:].rearrange("p (k c) -> p k c", k=8),
                    hbb[:, 0:8, :]]
            wq1v = w2b
            znv = castb[:].rearrange("p a b -> p (a b)").rearrange("p (k w) -> p k w", k=16)
            dsubv2 = [t1s[:, 0:512], t1s[:, 512:1024]]
            dexpv2 = [sqs[:, 0:512].rearrange("p (a b) -> p a b", a=4),
                      sqs[:, 512:1024].rearrange("p (a b) -> p a b", a=4)]

            tok_pb01 = agV
            tok_pb23 = agV
            tok_pb45 = convV
            tok_pb67 = scoresV
            tok_castb = None
            tok_ubf = None
            tok_chainEvac = sinitV
            tok_sab_pe = None
            tok_xt_pe = {-2: expT, -1: expT}
            tok_u2_read = None
            tok_znb_read = None
            tok_hsb_read = None
            tok_lnx_read = None
            tok_wbuf = bias3T
            tok_w3gbuf = bias3V
            tok_wq2b = sinitV
            tok_hbb = None

            for j in range(NBLK):
                ucol0 = 3 + j * BS
                upar = "dUBa" if j % 2 == 0 else "dUBb"

                with ExitStack() as sB:
                    EB = sB.enter_context
                    u2 = EB(nc.sbuf_tensor(f"u2_{j}", [128, 16, 256], F32, side="right"))
                    hsb = EB(nc.sbuf_tensor(f"hsb_{j}", [128, 16, 128], F32, side="right"))
                    decS = EB(nc.sbuf_tensor(f"decS_{j}", [128, 16, 128], BF16, side="right"))
                    cdec = EB(nc.sbuf_tensor(f"cdec_{j}", [64, 16, 128], BF16, side="right"))
                    bdv = EB(nc.sbuf_tensor(f"bdv_{j}", [128, 16, HD], BF16, side="right"))
                    lamI = EB(nc.sbuf_tensor(f"lamI_{j}", [64, 16, HD], BF16, side="right"))

                    ucpar = "dUCa" if j % 2 == 0 else "dUCb"
                    tok_castb_blk = tok_castb
                    if j == 0:
                        lnv = lnv0
                        ubTok, ubTok2 = ubTok0, ubTok20
                    else:
                        @blk.sync
                        def _(e, j=j, ucol0=ucol0):
                            W(e, tok_ubf)
                            e.dma_start(out=ubf[:], in_=uview[:, :, ucol0:ucol0 + BS]).then_inc(S[upar], 16)
                            C[upar] += 16
                        ubTok = tok(upar)

                        @blk.gpsimd
                        def _(e, j=j, ucol0=ucol0, ucpar=ucpar):
                            W(e, tok_lnx_u)
                            e.dma_start(out=lnx[:], in_=uview[:, :, ucol0:ucol0 + BS]).then_inc(S[ucpar], 16)
                            C[ucpar] += 16
                        ubTok2 = tok(ucpar)

                        lnv = ln_fm(ubf, lnx, lnx, 8, [(0, 256), (256, 256)],
                                    (ubTok, ubTok2, tok_pb01, tok_pb23))
                    tok_pb01 = lnv
                    tok_pb23 = lnv
                    lnx_free_half = {0: None, 1: None}

                    for half in range(2):
                        hc0 = half * 256
                        u2T0 = C["mT"]
                        u2A0 = C["mA"]

                        gw0 = gw0h if (j == 0 and half == 0) else \
                            {p: C[p] for p in ("dGWa", "dGWb", "dGWc")}
                        if not (j == 0 and half == 0):
                            @blk.gpsimd
                            def _(e, half=half, u2T0=u2T0, tok_wbuf=tok_wbuf):
                                for m in range(16):
                                    NS["GW"] += 1
                                    if m < 3:
                                        W(e, tok_wbuf)
                                    else:
                                        e.wait_ge(S["mT"], u2T0 + (m - 2))
                                    par = ("dGWa", "dGWb", "dGWc")[m % 3]
                                    e.dma_start(out=w2b[m % 3],
                                                in_=w2_t.ap()[m].rearrange("k f c -> f k c")
                                                ).then_inc(S[par], 16)
                                    C[par] += 16
                        else:
                            assert u2T0 == u2T0h

                        @blk.tensor
                        def _(e, half=half, hc0=hc0, u2T0=u2T0, u2A0=u2A0, lnv=lnv):
                            W(e, lnv, tok_pb01, tok_pb23)
                            for m in range(16):
                                par = ("dGWa", "dGWb", "dGWc")[m % 3]
                                e.wait_ge(S[par], gw0[par] + 16 * (m // 3 + 1))
                                if m >= 4:
                                    e.wait_ge(S["mA"], u2A0 + (m - 3))
                                inst = None
                                for k in range(8):
                                    inst = e.matmul(pb[m % 4][:, 0:256],
                                                    w2b[m % 3][:, k, :],
                                                    lnx[:, k, hc0:hc0 + 256],
                                                    start=(k == 0), stop=(k == 7))
                                I(inst, "mT")
                        assert C["mT"] == u2T0 + 16

                        @blk.scalar
                        def _(e, j=j, half=half, u2T0=u2T0, u2A0=u2A0, u2=u2):
                            act_set(e, 10)
                            if j == 0 and half == 0:
                                W(e, ("dX", 16))
                            for m in range(16):
                                e.wait_ge(S["mT"], u2T0 + m + 1)
                                dst = u2[:, m, :]
                                if j == 0 and half == 0 and m < 4:
                                    dst = u2h0s[:, m, :]
                                I(e.activation(dst, pb[m % 4][:, 0:256], AF.Gelu), "mA")
                            act_set(e, 6)
                        assert C["mA"] == u2A0 + 16
                        u2V = tok("mA")
                        tok_pb01 = u2V
                        tok_pb23 = u2V
                        tok_wbuf = ("mT", u2T0 + 16)
                        lnx_free_half[half] = ("mT", u2T0 + 16)

                        for c2 in range(2):
                            ci = j * 4 + half * 2 + c2
                            coff = c2 * 128
                            xpar = "dXTa" if ci % 2 == 0 else "dXTb"

                            @blk.sync
                            def _(e, ci=ci, xpar=xpar):
                                W(e, tok_xt_pe.get(ci - 2, expT))
                                e.dma_start(out=xtkb[ci % 2][:], in_=xt_sp.ap()[ci]).then_inc(S[xpar], 16)
                                C[xpar] += 16
                            xtTok = tok(xpar)

                            @blk.vector
                            def _(e, ci=ci):
                                W(e, tok_chainEvac, tok_sab_pe)
                                I(e.tensor_copy(Sab[:], Sa[:]), "mV")
                            sabV = tok("mV")

                            hh_done = None
                            yTs = []
                            for hh in range(2):
                                hbase = hh * 16
                                pbY0 = 4 + 2 * hh

                                @blk.vector
                                def _(e, ci=ci, hh=hh, hbase=hbase, hh_done=hh_done):
                                    W(e, lamVV, KG_ALL, hh_done)
                                    if hh == 0:
                                        W(e, tok_chainEvac)
                                    I(e.tensor_tensor(
                                        lamI[:],
                                        i64_b[:].unsqueeze(1).broadcast_to([64, 16, 64]),
                                        lam64[:, ci * 32 + hbase:ci * 32 + hbase + 16].unsqueeze(2)
                                        .broadcast_to([64, 16, 64]),
                                        ALU.mult), "mV")
                                lamIV = tok("mV")

                                @blk.vector
                                def _(e, ci=ci, hh=hh, hh_done=hh_done):
                                    W(e, hh_done)
                                    I(e.tensor_tensor(
                                        bdv[:],
                                        bctok[:, ci, 0:64].unsqueeze(1).broadcast_to([128, 16, 64]),
                                        dtet[:, ci, hh * 16:hh * 16 + 16].unsqueeze(2)
                                        .broadcast_to([128, 16, 64]),
                                        ALU.mult), "mV")
                                bdV = tok("mV")

                                prevDec = [hh_done, hh_done]
                                for hg in range(4):
                                    bank = 2 + hg % 2
                                    hgg = hbase + hg * 4

                                    @blk.tensor
                                    def _(e, ci=ci, hgg=hgg, bank=bank, pv=prevDec[hg % 2]):
                                        W(e, tok_pb23, pv, K_ALL)
                                        inst = None
                                        for hi in range(4):
                                            h = hgg + hi
                                            inst = e.matmul(pb[bank][:, hi * 128:(hi + 1) * 128],
                                                            i32_f[:, h:h + 1].broadcast_to([32, 128]),
                                                            aclt[:, ci, :], start=True, stop=True)
                                        I(inst, "mT")
                                    dbT = tok("mT")

                                    @blk.vector
                                    def _(e, ci=ci, hg=hg, hgg=hgg, bank=bank, dbT=dbT):
                                        W(e, dbT)
                                        inst = None
                                        for hi in range(4):
                                            h = hgg + hi
                                            inst = e.tensor_scalar(dsubv2[hg % 2][:, hi * 128:(hi + 1) * 128],
                                                                   pb[bank][:, hi * 128:(hi + 1) * 128],
                                                                   actok[:, ci, h:h + 1], 0.0,
                                                                   ALU.subtract, ALU.min)
                                        I(inst, "mV")
                                    dsV = tok("mV")

                                    @blk.scalar
                                    def _(e, hg=hg, dsV=dsV, pv=prevDec[hg % 2]):
                                        W(e, dsV, pv)
                                        I(e.activation(dexpv2[hg % 2].rearrange("p a b -> p (a b)"),
                                                       dsubv2[hg % 2], AF.Exp), "mA")
                                    dxA = tok("mA")

                                    @blk.vector
                                    def _(e, ci=ci, hg=hg, dxA=dxA):
                                        W(e, dxA)
                                        I(e.tensor_tensor(
                                            decS[:, hg * 4:(hg + 1) * 4, :],
                                            dexpv2[hg % 2],
                                            scoresM[:, ci, :].unsqueeze(1).broadcast_to([128, 4, 128]),
                                            ALU.mult), "mV")
                                    prevDec[hg % 2] = tok("mV")
                                decSV = prevDec[1]

                                prevCd = [prevDec[0], prevDec[1]]
                                for hg in range(4):
                                    bank = 2 + hg % 2
                                    hgg = hbase + hg * 4

                                    @blk.tensor
                                    def _(e, ci=ci, hgg=hgg, bank=bank, pv=prevCd[hg % 2]):
                                        W(e, pv, eacA, KG_ALL)
                                        inst = None
                                        for hi in range(4):
                                            h = hgg + hi
                                            inst = e.matmul(pb[bank][0:64, hi * 128:(hi + 1) * 128],
                                                            i32_b[:, h:h + 1].broadcast_to([32, 64]),
                                                            eac[:, ci, :], start=True, stop=True)
                                        I(inst, "mT")
                                    eaT = tok("mT")

                                    @blk.vector
                                    def _(e, ci=ci, hg=hg, bank=bank, eaT=eaT):
                                        W(e, eaT)
                                        I(e.tensor_tensor(
                                            cdec[:, hg * 4:(hg + 1) * 4, :],
                                            cfeat[:, ci, :].unsqueeze(1).broadcast_to([64, 4, 128]),
                                            pb[bank][0:64, 0:512].rearrange("p (a b) -> p a b", a=4),
                                            ALU.mult), "mV")
                                    prevCd[hg % 2] = tok("mV")
                                cdecV = prevCd[1]
                                tok_pb23 = cdecV

                                pbXf = tok_pb45 if hh == 0 else tok_pb67

                                @blk.tensor
                                def _(e, ci=ci, hh=hh, hbase=hbase, pbY0=pbY0, xtTok=xtTok,
                                      sabV=sabV, decSV=decSV, cdecV=cdecV, pbXf=pbXf):
                                    W(e, xtTok, sabV, decSV, cdecV, pbXf)
                                    inst = None
                                    for hp in range(8):
                                        bank = pbY0 + hp // 4
                                        col = (hp % 4) * 128
                                        for sub in range(2):
                                            h16 = 2 * hp + sub
                                            h = hbase + h16
                                            pr = slice(64 * sub, 64 * (sub + 1))
                                            inst = e.matmul(pb[bank][pr, col:col + 128],
                                                            xtkb[ci % 2][:, h * 64:(h + 1) * 64],
                                                            decS[:, h16, :], start=True, stop=False)
                                            inst = e.matmul(pb[bank][pr, col:col + 128],
                                                            Sab[:, h * 64:(h + 1) * 64],
                                                            cdec[:, h16, :], start=False, stop=True)
                                    I(inst, "mT")
                                yT = tok("mT")
                                yTs.append(yT)

                                @blk.tensor
                                def _(e, ci=ci, hh=hh, hbase=hbase, bdV=bdV, lamIV=lamIV,
                                      sabV=sabV, xtTok=xtTok):
                                    W(e, bdV, lamIV, sabV, xtTok, tok_pb01)
                                    inst = None
                                    for h16 in range(16):
                                        h = hbase + h16
                                        bank = h16 // 8
                                        col = (h16 % 8) * 64
                                        inst = e.matmul(pb[bank][0:64, col:col + 64],
                                                        lamI[:, h16, :], Sab[:, h * 64:(h + 1) * 64],
                                                        start=True, stop=False)
                                        inst = e.matmul(pb[bank][0:64, col:col + 64],
                                                        bdv[:, h16, :], xtkb[ci % 2][:, h * 64:(h + 1) * 64],
                                                        start=False, stop=True)
                                    I(inst, "mT")
                                cwT = tok("mT")

                                @blk.vector
                                def _(e, hh=hh, cwT=cwT):
                                    W(e, cwT)
                                    e.tensor_copy(Sa[:, hh * 1024:hh * 1024 + 512], pb[0][0:64, 0:512])
                                    I(e.tensor_copy(Sa[:, hh * 1024 + 512:hh * 1024 + 1024],
                                                    pb[1][0:64, 0:512]), "mV")
                                chainEv = tok("mV")
                                tok_pb01 = chainEv
                                hh_done = chainEv

                            tok_chainEvac = hh_done
                            tok_sab_pe = hh_done
                            tok_xt_pe[ci] = hh_done

                            @blk.vector
                            def _(e, ci=ci, coff=coff, yTs=tuple(yTs), u2V=u2V, u2=u2):
                                W(e, yTs[0], yTs[1], u2V, tok_hsb_read, tok_hbb)
                                inst = None
                                for b4 in range(4):
                                    src = u2[:, b4 * 4:(b4 + 1) * 4, coff:coff + 128]
                                    if ci < 2 and b4 == 0:
                                        src = u2h0s[:, 0:4, coff:coff + 128]
                                    e.tensor_tensor(
                                        hsb[:, b4 * 4:(b4 + 1) * 4, :],
                                        src,
                                        pb[4 + b4][:, 0:512].rearrange("p (a b) -> p a b", a=4),
                                        ALU.mult)
                                    inst = e.tensor_tensor(
                                        hbb[:, b4 * 4:(b4 + 1) * 4, :],
                                        src,
                                        pb[4 + b4][:, 0:512].rearrange("p (a b) -> p a b", a=4),
                                        ALU.mult)
                                I(inst, "mV")
                            hV = tok("mV")
                            tok_pb45 = hV
                            tok_pb67 = hV
                            tok_u2_read = hV
                            if dbg == "hsb0" and ci == 0:
                                dbg_tap("hsb0", hsb[:].rearrange("p a b -> p (a b)"), [hV])

                            hbV = hV

                            gv = ln_fm(hsb, hbb, znv[:, :, coff:coff + 128], 16, [(0, 128)],
                                       (hV, hbV, tok_pb01, tok_pb23, tok_znb_read,
                                        tok_castb_blk))
                            tok_hsb_read = gv
                            tok_pb01 = gv
                            tok_pb23 = gv
                            znReady = gv
                            if dbg == "zn0" and ci == 1:
                                dbg_tap("zn0", castb[:].rearrange("p a b -> p (a b)"), [gv])

                        atT0 = C["mT"]
                        atV0 = C["mV"]
                        aw0 = {p: C[p] for p in ("dAWa", "dAWb", "dAWc")}

                        @blk.sync
                        def _(e, half=half, atT0=atT0, tok_w3gbuf=tok_w3gbuf):
                            for m in range(8):
                                for piece in range(2):
                                    idx = m * 2 + piece
                                    NS["AW"] += 1
                                    if idx < 3:
                                        W(e, *w3spD)
                                        W(e, tok_w3gbuf, znReady)
                                    else:
                                        e.wait_ge(S["mT"], atT0 + (idx - 3) // 2 + 1)
                                    par = ("dAWa", "dAWb", "dAWc")[idx % 3]
                                    e.dma_start(out=w3gv[idx % 3],
                                                in_=w3g_sp.ap()[m, piece * 8:(piece + 1) * 8]
                                                .rearrange("k f c -> f k c")).then_inc(S[par], 16)
                                    C[par] += 16

                        @blk.tensor
                        def _(e, half=half, hc0=hc0, atT0=atT0, atV0=atV0, znReady=znReady,
                              aw0=aw0):
                            W(e, tok_pb45, tok_pb67, znReady)
                            for m in range(8):
                                if m >= 4:
                                    e.wait_ge(S["mV"], atV0 + (m - 3))
                                inst = None
                                for piece in range(2):
                                    idx = m * 2 + piece
                                    par = ("dAWa", "dAWb", "dAWc")[idx % 3]
                                    e.wait_ge(S[par], aw0[par] + 16 * (idx // 3 + 1))
                                    for k8 in range(8):
                                        k = piece * 8 + k8
                                        inst = e.matmul(pb[4 + m % 4][:, 0:256],
                                                        w3gv[idx % 3][:, k8, :], znv[:, k, :],
                                                        start=(k == 0), stop=(k == 15))
                                I(inst, "mT")
                        assert C["mT"] == atT0 + 8

                        @blk.vector
                        def _(e, j=j, half=half, hc0=hc0, atT0=atT0, atV0=atV0,
                              lnx_free=lnx_free_half[half]):
                            if half == 0:
                                W(e, ("dO", 16 * j))
                            for m in range(8):
                                e.wait_ge(S["mT"], atT0 + m + 1)
                                W(e, bias3V)
                                ev = I(e.tensor_scalar(attn_s[:, m, hc0:hc0 + 256],
                                                       pb[4 + m % 4][:, 0:256],
                                                       bias3[:, m:m + 1], None, ALU.add), "mV")
                                W(e, ev)
                                if m == 0:
                                    W(e, lnx_free)
                                I(e.tensor_copy(lnx[:, m, hc0:hc0 + 256],
                                                attn_s[:, m, hc0:hc0 + 256]), "mP")
                        assert C["mV"] == atV0 + 8
                        attnV = tok("mV")
                        acpV = tok("mP")
                        tok_pb45 = attnV
                        tok_pb67 = attnV
                        tok_znb_read = attnV
                        tok_hbb = ("mT", atT0 + 8)
                        tok_w3gbuf = ("mT", atT0 + 8)

                    if dbg == "attn0" and j == 0:
                        dbg_tap("attn0", attn_s[:].rearrange("p a b -> p (a b)"), [acpV])

                    x1V, x1cV = ln_fm(attn_s, lnx, None, 8, [(0, 256), (256, 256)],
                                      (attnV, acpV, tok_pb01, tok_pb23, ubTok),
                                      res_out=x1b, res_in=ubf,
                                      copy_out=castb, copy_gate=tok_w3gbuf)
                    tok_castb = x1V
                    tok_pb01 = x1V
                    tok_pb23 = x1V
                    tok_ubf = x1V
                    if dbg == "x1b0" and j == 0:
                        dbg_tap("x1b0", x1b[:].rearrange("p a b -> p (a b)"), [x1V])

                    ln2V = ln_fm(x1b, castb, lnx, 8, [(0, 256), (256, 256)],
                                 (x1V, x1cV, tok_pb01, tok_pb23))
                    tok_castb = ln2V
                    tok_pb01 = ln2V
                    tok_pb23 = ln2V
                    tok_lnx_read = None

                with ExitStack() as sF:
                    h1b = sF.enter_context(nc.sbuf_tensor(f"h1b_{j}", [128, 32, BS], BF16, side="right"))
                    wq2b2 = sF.enter_context(nc.sbuf_tensor(f"wq2b2_{j}", [128, 16, 128], BF16, side="right"))
                    wq2b3 = [wq2b[0], wq2b[1], wq2b2]

                    f1T0 = C["mT"]
                    f1A0 = C["mA"]

                    f10 = {p: C[p] for p in ("dF1a", "dF1b", "dF1c")}

                    @blk.sync
                    def _(e, j=j, f1T0=f1T0, tok_wbuf=tok_wbuf):
                        for m in range(32):
                            NS["F1"] += 1
                            if m < 3:
                                W(e, *qspD)
                                W(e, ln2V, tok_wbuf)
                            else:
                                e.wait_ge(S["mT"], f1T0 + (m - 2))
                            par = ("dF1a", "dF1b", "dF1c")[m % 3]
                            e.dma_start(out=wq1v[m % 3],
                                        in_=wq1_sp.ap()[m].rearrange("k f c -> f k c")
                                        ).then_inc(S[par], 16)
                            C[par] += 16

                    @blk.tensor
                    def _(e, f1T0=f1T0, f1A0=f1A0, ln2V=ln2V, f10=f10):
                        W(e, ln2V, tok_pb45, tok_pb67)
                        for m in range(32):
                            par = ("dF1a", "dF1b", "dF1c")[m % 3]
                            e.wait_ge(S[par], f10[par] + 16 * (m // 3 + 1))
                            if m >= 4:
                                e.wait_ge(S["mA"], f1A0 + (m - 3))
                            inst = None
                            for k in range(8):
                                inst = e.matmul(pb[4 + m % 4][:, 0:512], wq1v[m % 3][:, k, :],
                                                lnx[:, k, :], start=(k == 0), stop=(k == 7))
                            I(inst, "mT")
                    assert C["mT"] == f1T0 + 32

                    @blk.scalar
                    def _(e, f1T0=f1T0, f1A0=f1A0):
                        for m in range(32):
                            e.wait_ge(S["mT"], f1T0 + m + 1)
                            W(e, K_ALL)
                            I(e.activation(h1b[:, m, :], pb[4 + m % 4][:, 0:512], AF.Relu,
                                           bias=b1c[:, m:m + 1]), "mA")
                    assert C["mA"] == f1A0 + 32
                    h1V = tok("mA")
                    tok_pb45 = h1V
                    tok_pb67 = h1V
                    tok_wbuf = ("mT", f1T0 + 32)
                    tok_lnx_u = ("mT", f1T0 + 32)
                    if dbg == "h1b0" and j == 0:
                        dbg_tap("h1b0", h1b[:].rearrange("p a b -> p (a b)"), [h1V])

                    f2T0 = C["mT"]
                    f2V0 = C["mV"]

                    @blk.sync
                    def _(e, f2T0=f2T0, tok_wq2b=tok_wq2b, wq2b3=wq2b3):
                        for m in range(8):
                            for piece in range(2):
                                idx = m * 2 + piece
                                NS["F2"] += 1
                                n = NS["F2"] - 1
                                if idx < 3:
                                    W(e, *qspD)
                                    W(e, *xtspD)
                                    W(e, tok_wq2b)
                                else:
                                    e.wait_ge(S["mT"], f2T0 + (idx - 3) // 2 + 1)
                                par = ("dF2a", "dF2b", "dF2c")[n % 3]
                                e.dma_start(out=wq2b3[n % 3][:],
                                            in_=wq2_sp.ap()[m, piece * 16:(piece + 1) * 16]
                                            .rearrange("k f c -> f k c")).then_inc(S[par], 16)
                                C[par] += 16

                    @blk.tensor
                    def _(e, f2T0=f2T0, f2V0=f2V0, h1V=h1V, wq2b3=wq2b3):
                        W(e, h1V, tok_pb67, tok_pb45)
                        nbase = NS["F2"] - 16
                        for m in range(8):
                            if m >= 4:
                                e.wait_ge(S["mV"], f2V0 + (m - 3))
                            inst = None
                            for piece in range(2):
                                n = nbase + m * 2 + piece
                                par = ("dF2a", "dF2b", "dF2c")[n % 3]
                                e.wait_ge(S[par], 16 * (n // 3 + 1))
                                for k16 in range(16):
                                    k = piece * 16 + k16
                                    inst = e.matmul(pb[4 + m % 4][:, 0:512], wq2b3[n % 3][:, k16, :],
                                                    h1b[:, k, :], start=(k == 0), stop=(k == 31))
                            I(inst, "mT")
                    assert C["mT"] == f2T0 + 8

                    @blk.vector
                    def _(e, f2T0=f2T0, f2V0=f2V0, ln2V=ln2V):
                        for m in range(8):
                            e.wait_ge(S["mT"], f2T0 + m + 1)
                            W(e, K_ALL)
                            ev = I(e.tensor_scalar(attn_s[:, m, :], pb[4 + m % 4][:, 0:512],
                                                   b2c[:, m:m + 1], None, ALU.add), "mV")
                            W(e, ev)
                            I(e.tensor_copy(castb[:, m, :], attn_s[:, m, :]), "mP")
                    assert C["mV"] == f2V0 + 8
                    ffoV = tok("mV")
                    fcpV = tok("mP")
                    tok_pb67 = ffoV
                    tok_pb45 = ffoV
                    tok_wq2b = ("mT", f2T0 + 8)
                    if dbg == "ffn0" and j == 0:
                        dbg_tap("ffn0", attn_s[:].rearrange("p a b -> p (a b)"), [ffoV])

                    x2V = ln_fm(attn_s, castb, None, 8, [(0, 256), (256, 256)],
                                (ffoV, fcpV, tok_pb01, tok_pb23),
                                res_out=attn_s, res_in=x1b)
                    tok_castb = x2V
                    tok_pb01 = x2V
                    tok_pb23 = x2V
                    tok_lnx_read = None

                    @blk.scalar
                    def _(e, j=j, x2V=x2V):
                        W(e, x2V)
                        e.dma_start(out=out_ft.ap().rearrange("(a f) t -> f a t", f=128)
                                    [:, :, j * BS:(j + 1) * BS],
                                    in_=attn_s[:]).then_inc(S["dO"], 16)
                        C["dO"] += 16

            @blk.gpsimd
            def _(e):
                W(e, ("dO", C["dO"]))
                if dbg:
                    W(e, ("dDBG", C["dDBG"]))
    return nc



def _host_prepare(inputs):
    W1p = np.ascontiguousarray(np.asarray(inputs["L0_attn_W_in1"], np.float32))
    w1x = np.ascontiguousarray(W1p.T[:, 2048:4256])
    w1x_t = np.zeros((18, 8, 128, 128), np.float32)
    for m in range(18):
        cwd = min(128, 2208 - m * 128)
        w1x_t[m, :, :, :cwd] = w1x[:, m * 128:m * 128 + cwd].reshape(8, 128, cwd)
    convw_t = np.ascontiguousarray(
        np.asarray(inputs["L0_attn_conv_w"], np.float32).transpose(2, 1, 0))
    w2 = np.ascontiguousarray(np.asarray(inputs["L0_attn_W_in2"], np.float32).T)
    w2_t = np.zeros((16, 8, 128, 128), np.float32)
    for m in range(16):
        w2_t[m] = w2[:, m * 128:(m + 1) * 128].reshape(8, 128, 128)
    shared = dict(
        w1x_t=w1x_t, convw_t=convw_t, w2_t=w2_t,
        w3_t=np.ascontiguousarray(np.asarray(inputs["L0_attn_W_in3"], np.float32).T),
        fw1_t=np.ascontiguousarray(np.asarray(inputs["L0_ffn_W1"], np.float32).T),
        fw2_t=np.ascontiguousarray(np.asarray(inputs["L0_ffn_W2"], np.float32).T),
        cb_row=np.asarray(inputs["L0_attn_conv_b"], np.float32).reshape(1, CH),
        b1_pm=np.ascontiguousarray(np.asarray(inputs["L0_ffn_b1"], np.float32).reshape(32, 128).T),
        b2_pm=np.ascontiguousarray(np.asarray(inputs["L0_ffn_b2"], np.float32).reshape(8, 128).T),
        lng_pm=np.ascontiguousarray(np.asarray(inputs["L0_attn_ln_g"], np.float32).reshape(16, 128).T),
        lnb_pm=np.ascontiguousarray(np.asarray(inputs["L0_attn_ln_b"], np.float32).reshape(16, 128).T),
        alog_c=np.asarray(inputs["L0_attn_A_log"], np.float32).reshape(32, 1),
        prelu_pm=np.broadcast_to(np.asarray(inputs["L0_attn_prelu_a"], np.float32)
                                 .reshape(1, 1), (128, 1)).copy(),
        triu_h=np.triu(np.ones((128, 128), np.float32)),
        i128_h=np.eye(128, dtype=np.float32),
        i32_h=np.eye(32, dtype=np.float32),
        i64_h=np.eye(64, dtype=np.float32),
        ones_h=np.ones((128, 128), np.float32),
        onesc_h=np.ones((128, 1), np.float32),
    )
    shared = {k: np.ascontiguousarray(v, dtype=np.float32) for k, v in shared.items()}

    u = np.asarray(inputs["L0_attn_u"], np.float32)
    in_maps = []
    for c in range(NCORES):
        b, hhalf = c // 2, c % 2
        t0 = hhalf * 2048
        useg = np.zeros((TH, D), np.float32)
        useg[3:] = u[b, t0:t0 + 2048]
        if hhalf == 1:
            useg[:3] = u[b, t0 - 3:t0]
        sel = np.zeros((64, 8), np.float32)
        if hhalf == 1:
            sel[:, c - 1] = 1.0
        m = dict(shared)
        m["u_ft"] = np.ascontiguousarray(useg.T)
        m["sel_pm"] = sel
        in_maps.append(m)
    return in_maps


def kernel(**inputs):
    in_maps = _host_prepare(inputs)
    nc = build(DBG)
    res = run_bass_kernel_spmd(nc, in_maps, core_ids=list(range(NCORES)))
    u = np.asarray(inputs["L0_attn_u"])
    out = np.empty((4, 4096, D), np.float32)
    for c in range(NCORES):
        b, hhalf = c // 2, c % 2
        out[b, hhalf * 2048:(hhalf + 1) * 2048] = res.results[c]["out_ft"].T
    if DBG:
        np.save("/tmp/kdbg.npy", np.stack([res.results[c]["dbg"] for c in range(NCORES)]))
    return out


# version 20
# speedup vs baseline: 1.0263x; 1.0148x over previous
import os
import numpy as np
import ml_dtypes
import concourse.bass as bass
import concourse.mybir as mybir
from concourse.bass_utils import run_bass_kernel_spmd
from contextlib import ExitStack

F32 = mybir.dt.float32
BF16 = mybir.dt.bfloat16
I32 = mybir.dt.int32
AF = mybir.ActivationFunctionType
ALU = mybir.AluOpType

D, DI, NH, HD, DS, CH = 1024, 2048, 32, 64, 64, 2176
T, TH, Q, NCHK, BS, NBLK, FF = 2048, 2051, 128, 16, 512, 4, 4096
EPS = 1e-5
RM = 12582912.0
NCORES = 8

DBG = os.environ.get("KDBG", "")
DBG_SHAPES = {
    "ln1": (128, 8 * TH), "xbc": (128, 17 * TH), "dt": (32, TH), "AC": (32, T),
    "bctok": (128, NCHK * 128), "scoresM": (128, NCHK * 128),
    "sfin": (64, DI), "sinit": (64, DI), "xt0": (128, DI),
    "hsb0": (128, 16 * 128), "zn0": (128, 16 * 256),
    "attn0": (128, 8 * BS), "x1b0": (128, 8 * BS),
    "h1b0": (128, 32 * BS), "ffn0": (128, 8 * BS),
}


def build(dbg="", ncores=NCORES):
    nc = bass.Bass("TRN2", target_bir_lowering=False)

    u_ft = nc.dram_tensor("u_ft", [D, TH], F32, kind="ExternalInput")
    w1x_t = nc.dram_tensor("w1x_t", [18, 8, 128, 128], F32, kind="ExternalInput")
    convw_t = nc.dram_tensor("convw_t", [4, CH, CH], F32, kind="ExternalInput")
    w2_t = nc.dram_tensor("w2_t", [16, 8, 128, 128], F32, kind="ExternalInput")
    w3_t = nc.dram_tensor("w3_t", [DI, D], F32, kind="ExternalInput")
    fw1_t = nc.dram_tensor("fw1_t", [D, FF], F32, kind="ExternalInput")
    fw2_t = nc.dram_tensor("fw2_t", [FF, D], F32, kind="ExternalInput")
    cb_row = nc.dram_tensor("cb_row", [1, CH], F32, kind="ExternalInput")
    b1_pm = nc.dram_tensor("b1_pm", [128, 32], F32, kind="ExternalInput")
    b2_pm = nc.dram_tensor("b2_pm", [128, 8], F32, kind="ExternalInput")
    lng_pm = nc.dram_tensor("lng_pm", [128, 16], F32, kind="ExternalInput")
    lnb_pm = nc.dram_tensor("lnb_pm", [128, 16], F32, kind="ExternalInput")
    alog_c = nc.dram_tensor("alog_c", [32, 1], F32, kind="ExternalInput")
    prelu_pm = nc.dram_tensor("prelu_pm", [128, 1], F32, kind="ExternalInput")
    triu_h = nc.dram_tensor("triu_h", [128, 128], F32, kind="ExternalInput")
    i128_h = nc.dram_tensor("i128_h", [128, 128], F32, kind="ExternalInput")
    i32_h = nc.dram_tensor("i32_h", [32, 32], F32, kind="ExternalInput")
    i64_h = nc.dram_tensor("i64_h", [64, 64], F32, kind="ExternalInput")
    ones_h = nc.dram_tensor("ones_h", [128, 128], F32, kind="ExternalInput")
    onesc_h = nc.dram_tensor("onesc_h", [128, 1], F32, kind="ExternalInput")
    sel_pm = nc.dram_tensor("sel_pm", [64, 8], F32, kind="ExternalInput")
    out_ft = nc.dram_tensor("out_ft", [D, T], F32, kind="ExternalOutput")

    w3g_sp = nc.dram_tensor("w3g_sp", [8, 16, 128, 128], BF16)
    wq1_sp = nc.dram_tensor("wq1_sp", [32, 8, 128, 128], BF16)
    wq2_sp = nc.dram_tensor("wq2_sp", [8, 32, 128, 128], BF16)
    xt_sp = nc.dram_tensor("xt_sp", [NCHK, 128, DI], BF16)
    ag_in = nc.dram_tensor("ag_in", [64, DI], F32)
    if ncores % 2 == 0:
        ag_out = nc.dram_tensor("ag_out", [ncores * 64, DI], F32, addr_space="Shared")
    else:
        ag_out = nc.dram_tensor("ag_out", [ncores * 64, DI], F32)

    dbg_t = nc.dram_tensor("dbg", list(DBG_SHAPES[dbg]), F32, kind="ExternalOutput") if dbg else None

    with ExitStack() as st:
        E = st.enter_context
        semnames = ("mT", "mV", "mA", "mP",
                    "dK", "dKg", "dUa", "dUb", "dUga", "dUgb",
                    "dW3a", "dW3b", "dSPa", "dSPb", "dSQa", "dSQb", "dSPXa", "dSPXb",
                    "dIWa", "dIWb", "dQa", "dQb",
                    "dCWa", "dCWb", "dCWc", "dCWd", "dCWe", "dCWf", "fCW", "dF2c",
                    "dQc", "dQd", "dSQc", "dSQd",
                    "dGWc", "dGWd", "dAWc", "dAWd", "dF1c", "dF1d",
                    "dXTa", "dXTb",
                    "dGWa", "dGWb", "dAWa", "dAWb", "dF1a", "dF1b", "dF2a", "dF2b",
                    "dUBa", "dUBb", "dUCa", "dUCb", "dX", "cc", "dG2", "dO", "dDBG")
        S = {n: E(nc.semaphore(n)) for n in semnames}
        C = {n: 0 for n in semnames}

        def W(e, *deps):
            for dep in deps:
                if dep is None:
                    continue
                sname, val = dep
                if val > 0:
                    e.wait_ge(S[sname], val)

        def I(inst, sname, n=1):
            C[sname] += n
            inst.then_inc(S[sname], n)
            return (sname, C[sname])

        def tok(sname):
            return (sname, C[sname])

        pb = [E(nc.psum_tensor(f"pb{i}", [128, 512], F32)) for i in range(8)]

        triu_f = E(nc.sbuf_tensor("triu_f", [128, 128], F32))
        i128_f = E(nc.sbuf_tensor("i128_f", [128, 128], F32))
        i32_f = E(nc.sbuf_tensor("i32_f", [32, 32], F32))
        i64_b = E(nc.sbuf_tensor("i64_b", [64, 64], BF16))
        i32_b = E(nc.sbuf_tensor("i32_b", [32, 32], BF16))
        ones_f = E(nc.sbuf_tensor("ones_f", [128, 1], F32))
        ones_b = E(nc.sbuf_tensor("ones_b", [128, 128], BF16))
        cbrow_b = E(nc.sbuf_tensor("cbrow_b", [1, CH], BF16))
        b1c = E(nc.sbuf_tensor("b1c", [128, 32], F32))
        b2c = E(nc.sbuf_tensor("b2c", [128, 8], F32))
        lngc = E(nc.sbuf_tensor("lngc", [128, 16], F32))
        lnbc_b = E(nc.sbuf_tensor("lnbc_b", [128, 16], BF16))
        alogc = E(nc.sbuf_tensor("alogc", [32, 1], F32))
        preluc = E(nc.sbuf_tensor("preluc", [128, 1], F32))
        selc = E(nc.sbuf_tensor("selc", [64, 8], F32))
        aneg = E(nc.sbuf_tensor("aneg", [32, 1], F32))
        c1c = E(nc.sbuf_tensor("c1c", [128, 1], F32))
        c2c = E(nc.sbuf_tensor("c2c", [128, 1], F32))
        bias3 = E(nc.sbuf_tensor("bias3", [128, 8], F32))
        gam1 = E(nc.sbuf_tensor("gam1", [128, 1], F32))
        gam2 = E(nc.sbuf_tensor("gam2", [128, 1], F32))
        rowsc = E(nc.sbuf_tensor("rowsc", [1, 1032], F32))
        sqs = E(nc.sbuf_tensor("sqs", [128, 2048], BF16))
        t1s = E(nc.sbuf_tensor("t1s", [128, 2048], F32))
        w3buf = [E(nc.sbuf_tensor(f"w3buf{i}", [128, D], BF16)) for i in range(3)]
        w3gb = [E(nc.sbuf_tensor(f"w3gb{i}", [128, D], BF16)) for i in range(2)]

        blk = E(nc.Block())

        actset_n = [0]

        def act_set(e, sid):
            actset_n[0] += 1
            e.add_instruction(mybir.InstLoadActFuncSet(
                name=f"actset{sid}_{actset_n[0]}", act_func_set_id=sid, ins=[], outs=[]))

        def dbg_tap(name, src_ap, wait_toks):
            if dbg != name:
                return

            @blk.gpsimd
            def _(e):
                W(e, *wait_toks)
                e.dma_start(out=dbg_t.ap(), in_=src_ap).then_inc(S["dDBG"], 16)
                C["dDBG"] += 16

        @blk.sync
        def _(e):
            for dst, src in ((triu_f, triu_h.ap()), (i128_f, i128_h.ap()),
                             (i32_f, i32_h.ap()), (ones_f, onesc_h.ap()),
                             (b1c, b1_pm.ap()), (b2c, b2_pm.ap()),
                             (lngc, lng_pm.ap()),
                             (alogc, alog_c.ap()), (preluc, prelu_pm.ap()),
                             (selc, sel_pm.ap())):
                e.dma_start(out=dst[:], in_=src).then_inc(S["dK"], 16)
                C["dK"] += 16

        @blk.gpsimd
        def _(e):
            for dst, src in ((i64_b, i64_h.ap()), (i32_b, i32_h.ap()),
                             (ones_b, ones_h.ap()), (cbrow_b, cb_row.ap()),
                             (lnbc_b, lnb_pm.ap())):
                e.dma_start(out=dst[:], in_=src).then_inc(S["dKg"], 16)
                C["dKg"] += 16
        K_ALL, KG_ALL = tok("dK"), tok("dKg")

        @blk.scalar
        def _(e):
            act_set(e, 6)
            W(e, K_ALL)
            I(e.activation(aneg[:], alogc[:], AF.Exp), "mA")
        anegA = tok("mA")

        @blk.vector
        def _(e):
            W(e, anegA)
            e.tensor_scalar(aneg[:], aneg[:], -1.0, None, ALU.mult)
            e.tensor_scalar(c1c[:], preluc[:], 1.0, 0.5, ALU.add, ALU.mult)
            I(e.tensor_scalar(c2c[:], preluc[:], -0.5, 0.5, ALU.mult, ALU.add), "mV")
        P0V = tok("mV")

        def ln_fm(x_f32, x_b16, out_b16, nk, widths, deps, res_out=None, res_in=None,
                  copy_out=None, copy_gate=None):
            sqv = sqs[:].rearrange("p (k w) -> p k w", k=nk)
            t1v = t1s[:].rearrange("p (k w) -> p k w", k=nk)
            nf = float(nk * 128)
            NW = len(widths)
            assert NW <= 2
            sqT = [None] * NW
            tsT = [None] * NW
            vrT = [None] * NW
            exT = [None] * NW
            bcT = [None] * NW
            mulT = [None] * NW
            cpT = [None] * NW

            stT = [None] * NW
            for i, (c0, wdt) in enumerate(widths):
                @blk.vector
                def _(e, i=i, c0=c0, wdt=wdt):
                    W(e, *deps)
                    if i >= 1:
                        W(e, stT[i - 1])
                    inst = None
                    for k in range(nk):
                        inst = e.tensor_tensor(sqv[:, k, 0:wdt], x_b16[:, k, c0:c0 + wdt],
                                               x_b16[:, k, c0:c0 + wdt], ALU.mult)
                    I(inst, "mV")
                sqT[i] = tok("mV")

                b0 = 2 * i

                @blk.tensor
                def _(e, i=i, c0=c0, wdt=wdt, b0=b0):
                    W(e, *deps, sqT[i], KG_ALL)
                    inst = None
                    for k in range(nk):
                        e.matmul(pb[b0][0:1, 0:wdt], ones_b[:, 0:1], x_b16[:, k, c0:c0 + wdt],
                                 start=(k == 0), stop=(k == nk - 1))
                    for k in range(nk):
                        inst = e.matmul(pb[b0 + 1][0:1, 0:wdt], ones_b[:, 0:1],
                                        sqv[:, k, 0:wdt],
                                        start=(k == 0), stop=(k == nk - 1))
                    I(inst, "mT")
                stT[i] = tok("mT")

            for i, (c0, wdt) in enumerate(widths):
                m0 = i * 256
                v0 = 512 + i * 256
                b0 = 2 * i

                @blk.vector
                def _(e, i=i, wdt=wdt, m0=m0, v0=v0, b0=b0):
                    W(e, stT[i])
                    t0 = I(e.tensor_scalar(rowsc[0:1, m0:m0 + wdt], pb[b0][0:1, 0:wdt],
                                           1.0 / nf, None, ALU.mult), "mV")
                    e.tensor_scalar(rowsc[0:1, v0:v0 + wdt], pb[b0 + 1][0:1, 0:wdt], 1.0 / nf, EPS,
                                    ALU.mult, ALU.add)
                    W(e, t0)
                    t1 = I(e.tensor_tensor(t1s[0:1, m0:m0 + wdt], rowsc[0:1, m0:m0 + wdt],
                                           rowsc[0:1, m0:m0 + wdt], ALU.mult), "mV")
                    W(e, t1)
                    I(e.tensor_tensor(rowsc[0:1, v0:v0 + wdt], rowsc[0:1, v0:v0 + wdt],
                                      t1s[0:1, m0:m0 + wdt], ALU.subtract), "mV")
                vrT[i] = tok("mV")
                tsT[i] = vrT[i]

            for i, (c0, wdt) in enumerate(widths):
                v0 = 512 + i * 256

                @blk.scalar
                def _(e, i=i, wdt=wdt, v0=v0):
                    W(e, vrT[i])
                    I(e.activation(rowsc[0:1, v0:v0 + wdt], rowsc[0:1, v0:v0 + wdt], AF.Ln), "mA")
                lnT = tok("mA")

                @blk.scalar
                def _(e, i=i, wdt=wdt, v0=v0, lnT=lnT):
                    W(e, lnT)
                    I(e.activation(rowsc[0:1, v0:v0 + wdt], rowsc[0:1, v0:v0 + wdt],
                                   AF.Exp, scale=-0.5), "mA")
                exT[i] = tok("mA")

            for i, (c0, wdt) in enumerate(widths):
                m0 = i * 256
                v0 = 512 + i * 256
                bb = 2 * ((i + 1) % NW) if NW == 2 else 2

                @blk.tensor
                def _(e, i=i, wdt=wdt, m0=m0, v0=v0, bb=bb):
                    W(e, exT[i], tsT[(i + 1) % NW] if NW == 2 else tsT[i], K_ALL)
                    e.matmul(pb[bb][:, 0:wdt], triu_f[0:1, :], rowsc[0:1, m0:m0 + wdt],
                             start=True, stop=True)
                    I(e.matmul(pb[bb + 1][:, 0:wdt], triu_f[0:1, :], rowsc[0:1, v0:v0 + wdt],
                               start=True, stop=True), "mT")
                bcT[i] = tok("mT")

            out_tok = None
            for i, (c0, wdt) in enumerate(widths):
                bb = 2 * ((i + 1) % NW) if NW == 2 else 2

                @blk.vector
                def _(e, i=i, c0=c0, wdt=wdt, bb=bb):
                    W(e, bcT[i])
                    if i >= 1:
                        W(e, mulT[i - 1])
                    inst = None
                    for k in range(nk):
                        inst = e.tensor_tensor(t1v[:, k, 0:wdt], x_f32[:, k, c0:c0 + wdt],
                                               pb[bb][:, 0:wdt], ALU.subtract)
                    I(inst, "mV")
                subT = tok("mV")

                @blk.vector
                def _(e, i=i, c0=c0, wdt=wdt, bb=bb, subT=subT):
                    W(e, subT)
                    inst = None
                    for k in range(nk):
                        if res_out is None:
                            inst = e.tensor_tensor(out_b16[:, k, c0:c0 + wdt], t1v[:, k, 0:wdt],
                                                   pb[bb + 1][:, 0:wdt], ALU.mult)
                        else:
                            inst = e.tensor_tensor(t1v[:, k, 0:wdt], t1v[:, k, 0:wdt],
                                                   pb[bb + 1][:, 0:wdt], ALU.mult)
                    I(inst, "mV")
                mulT[i] = tok("mV")

                if res_out is not None:
                    @blk.vector
                    def _(e, i=i, c0=c0, wdt=wdt):
                        W(e, mulT[i])
                        inst = None
                        for k in range(nk):
                            inst = e.tensor_tensor(res_out[:, k, c0:c0 + wdt], t1v[:, k, 0:wdt],
                                                   res_in[:, k, c0:c0 + wdt], ALU.add)
                        I(inst, "mV")
                    mulT[i] = tok("mV")
                if copy_out is not None:
                    @blk.vector
                    def _(e, i=i, c0=c0, wdt=wdt):
                        W(e, mulT[i], copy_gate)
                        inst = None
                        for k in range(nk):
                            inst = e.tensor_copy(copy_out[:, k, c0:c0 + wdt],
                                                 res_out[:, k, c0:c0 + wdt])
                        I(inst, "mP")
                    cpT[i] = tok("mP")
                out_tok = mulT[i]
            if copy_out is not None:
                return out_tok, cpT[NW - 1]
            return out_tok

        v0 = C["mV"]
        b3T0 = C["mT"]

        @blk.gpsimd
        def _(e):
            for k in range(16):
                if k >= 2:
                    e.wait_ge(S["mV"], v0 + (k - 1))
                    e.wait_ge(S["mT"], b3T0 + (k - 1))
                par = "dW3a" if k % 2 == 0 else "dW3b"
                e.dma_start(out=w3buf[k % 2][:],
                            in_=w3_t.ap()[k * 128:(k + 1) * 128, :]).then_inc(S[par], 16)
                C[par] += 16

        @blk.tensor
        def _(e):
            inst = None
            for k in range(16):
                par = "dW3a" if k % 2 == 0 else "dW3b"
                e.wait_ge(S[par], 16 * (k // 2 + 1))
                W(e, KG_ALL)
                for m in range(8):
                    inst = e.matmul(pb[m][:, 0:1], w3buf[k % 2][:, m * 128:(m + 1) * 128],
                                    lnbc_b[:, k:k + 1], start=(k == 0), stop=(k == 15))
                I(inst, "mT")
        assert C["mT"] == b3T0 + 16
        bias3T = tok("mT")

        @blk.vector
        def _(e):
            for k in range(16):
                par = "dW3a" if k % 2 == 0 else "dW3b"
                e.wait_ge(S[par], 16 * (k // 2 + 1))
                if k >= 2:
                    e.wait_ge(S["dSPa" if k % 2 == 0 else "dSPb"], 16 * (k // 2))
                W(e, K_ALL)
                I(e.tensor_scalar(w3gb[k % 2][:], w3buf[k % 2][:], lngc[:, k:k + 1], None, ALU.mult), "mV")
            W(e, bias3T)
            inst = None
            for m in range(8):
                inst = e.tensor_copy(bias3[:, m:m + 1], pb[m][:, 0:1])
            I(inst, "mV")
        bias3V = tok("mV")

        @blk.sync
        def _(e):
            for k in range(16):
                e.wait_ge(S["mV"], v0 + k + 1)
                par = "dSPa" if k % 2 == 0 else "dSPb"
                e.dma_start(out=w3g_sp.ap()[:, k].rearrange("m f c -> f m c"),
                            in_=w3gb[k % 2][:].rearrange("f (m c) -> f m c", m=8)).then_inc(S[par], 16)
                C[par] += 16
        w3spD = (("dSPa", C["dSPa"]), ("dSPb", C["dSPb"]))

        uview = u_ft.ap().rearrange("(a f) t -> f a t", f=128)

        with ExitStack() as s1:
            E1 = s1.enter_context
            xbc = E1(nc.sbuf_tensor("xbc", [128, 17, TH], BF16, side="right"))
            wip = [E1(nc.sbuf_tensor(f"wip{i}", [128, 8, 128], BF16, side="right")) for i in range(2)]
            lamr_f = E1(nc.sbuf_tensor("lamr_f", [1, NCHK * 32], F32, side="right"))
            lamr_b = E1(nc.sbuf_tensor("lamr_b", [1, NCHK * 32], BF16, side="right"))

            with ExitStack() as sD:
                ED = sD.enter_context
                dtr = ED(nc.sbuf_tensor("dtr", [32, TH], F32, side="right"))
                ATt = ED(nc.sbuf_tensor("ATt", [32, T], F32, side="right"))
                ACt = ED(nc.sbuf_tensor("ACt", [32, T], F32, side="right"))
                sp_sb = ED(nc.sbuf_tensor("sp_sb", [32, 2 * T], F32, side="right"))

                with ExitStack() as sL:
                    EL = sL.enter_context
                    ln1 = EL(nc.sbuf_tensor("ln1", [128, 8, TH], BF16, side="right"))
                    ubuf = [EL(nc.sbuf_tensor(f"ubuf{i}", [128, 8, 256], F32, side="right")) for i in range(2)]
                    ubbl = [EL(nc.sbuf_tensor(f"ubbl{i}", [128, 8, 256], BF16, side="right")) for i in range(2)]

                    UB = [(c, 256) for c in range(0, 2048, 256)] + [(2048, 3)]
                    lnToks = []
                    for wi, (c0, wdt) in enumerate(UB):
                        pa = "dUa" if wi % 2 == 0 else "dUb"
                        pg = "dUga" if wi % 2 == 0 else "dUgb"

                        @blk.sync
                        def _(e, wi=wi, c0=c0, wdt=wdt, pa=pa):
                            if wi >= 2:
                                W(e, lnToks[wi - 2])
                            e.dma_start(out=ubuf[wi % 2][:, :, 0:wdt],
                                        in_=uview[:, :, c0:c0 + wdt]).then_inc(S[pa], 16)
                            C[pa] += 16
                        uTok = tok(pa)

                        @blk.gpsimd
                        def _(e, wi=wi, c0=c0, wdt=wdt, pg=pg):
                            if wi >= 2:
                                W(e, lnToks[wi - 2])
                            e.dma_start(out=ubbl[wi % 2][:, :, 0:wdt],
                                        in_=uview[:, :, c0:c0 + wdt]).then_inc(S[pg], 16)
                            C[pg] += 16
                        ugTok = tok(pg)

                        lt = ln_fm(ubuf[wi % 2], ubbl[wi % 2],
                                   ln1[:, :, c0:c0 + wdt], 8, [(0, wdt)],
                                   (uTok, ugTok, P0V, bias3V))
                        lnToks.append(lt)
                    ln1V = lnToks[-1]
                    dbg_tap("ln1", ln1[:].rearrange("p a t -> p (a t)"), [ln1V])

                    IPB = [(0, 512, 0), (512, 512, 1), (1024, 512, 2), (1536, 512, 3), (2048, 3, 4)]
                    ipT0 = C["mT"]
                    vIP0 = C["mV"]

                    @blk.gpsimd
                    def _(e):
                        for m in range(18):
                            if m >= 2:
                                e.wait_ge(S["mT"], ipT0 + (m - 1))
                            par = "dIWa" if m % 2 == 0 else "dIWb"
                            e.dma_start(out=wip[m % 2][:],
                                        in_=w1x_t.ap()[m].rearrange("k f c -> f k c")).then_inc(S[par], 16)
                            C[par] += 16

                    @blk.tensor
                    def _(e):
                        W(e, ln1V)
                        for m in range(18):
                            par = "dIWa" if m % 2 == 0 else "dIWb"
                            e.wait_ge(S[par], 16 * (m // 2 + 1))
                            if m >= 1:
                                e.wait_ge(S["mV"], vIP0 + m)
                            inst = None
                            for k in range(8):
                                for (c0, wdt, bank) in IPB:
                                    inst = e.matmul(pb[bank][:, 0:wdt], wip[m % 2][:, k, :],
                                                    ln1[:, k, c0:c0 + wdt], start=(k == 0), stop=(k == 7))
                            I(inst, "mT")
                    assert C["mT"] == ipT0 + 18
                    ipT = tok("mT")

                    @blk.vector
                    def _(e):
                        for m in range(18):
                            e.wait_ge(S["mT"], ipT0 + m + 1)
                            inst = None
                            for (c0, wdt, bank) in IPB:
                                if m < 17:
                                    inst = e.tensor_copy(xbc[:, m, c0:c0 + wdt], pb[bank][:, 0:wdt])
                                else:
                                    inst = e.tensor_copy(dtr[0:32, c0:c0 + wdt], pb[bank][0:32, 0:wdt])
                            I(inst, "mV")
                    assert C["mV"] == vIP0 + 18
                    ipV = tok("mV")
                    dbg_tap("xbc", xbc[:].rearrange("p m t -> p (m t)"), [ipV])
                    dbg_tap("dt", dtr[:], [ipV])

                bctok = E(nc.sbuf_tensor("bctok", [128, NCHK, 128], F32))
                bfeat = E(nc.sbuf_tensor("bfeat", [64, NCHK, 128], BF16))
                cfeat = E(nc.sbuf_tensor("cfeat", [64, NCHK, 128], BF16))
                scoresM = E(nc.sbuf_tensor("scoresM", [128, NCHK, 128], BF16))
                aclt = E(nc.sbuf_tensor("aclt", [32, NCHK, 128], F32))
                eac = E(nc.sbuf_tensor("eac", [32, NCHK, 128], BF16))
                actok = E(nc.sbuf_tensor("actok", [128, NCHK, 32], F32))
                dtet = E(nc.sbuf_tensor("dtet", [128, NCHK, 32], F32))
                dtegt = E(nc.sbuf_tensor("dtegt", [128, NCHK, 32], F32))
                lam64 = E(nc.sbuf_tensor("lam64", [64, NCHK * 32], BF16))

                @blk.scalar
                def _(e):
                    W(e, ipV, P0V)
                    I(e.activation(sp_sb[:, 0:T], dtr[:, 3:3 + T], AF.Abs), "mA")
                absV = tok("mA")

                @blk.scalar
                def _(e):
                    W(e, absV)
                    I(e.activation(sp_sb[:, T:2 * T], sp_sb[:, 0:T], AF.Exp, scale=-1.0), "mA")
                spE = tok("mA")

                @blk.vector
                def _(e):
                    W(e, spE)
                    I(e.tensor_scalar(sp_sb[:, T:2 * T], sp_sb[:, T:2 * T], 1.0, None, ALU.add), "mV")
                spP1 = tok("mV")

                @blk.scalar
                def _(e):
                    W(e, spP1)
                    I(e.activation(sp_sb[:, T:2 * T], sp_sb[:, T:2 * T], AF.Ln), "mA")
                spA = tok("mA")

                @blk.vector
                def _(e):
                    W(e, spA)
                    t0 = I(e.tensor_scalar(sp_sb[:, 0:T], dtr[:, 3:3 + T], 0.0, None, ALU.max), "mV")
                    W(e, t0)
                    t1 = I(e.tensor_tensor(ATt[:], sp_sb[:, 0:T], sp_sb[:, T:2 * T], ALU.add), "mV")
                    W(e, t1)
                    t2 = I(e.tensor_scalar(sp_sb[:, 0:T], ATt[:], aneg[:, 0:1], None, ALU.mult), "mV")
                    W(e, t2)
                    I(e.tensor_tensor_scan(ACt[:], sp_sb[:, 0:T], sp_sb[:, 0:T], 0.0, ALU.add, ALU.bypass), "mV")
                acV = tok("mV")
                dbg_tap("AC", ACt[:], [acV])

                @blk.vector
                def _(e):
                    W(e, acV)
                    inst = None
                    for c in range(NCHK):
                        if c == 0:
                            inst = e.tensor_copy(aclt[:, 0, :], ACt[:, 0:128])
                        else:
                            inst = e.tensor_scalar(aclt[:, c, :], ACt[:, c * 128:(c + 1) * 128],
                                                   ACt[:, c * 128 - 1:c * 128], None, ALU.subtract)
                    inst = e.tensor_scalar(sp_sb[:, 0:T], ACt[:], ACt[:, T - 1:T], -1.0,
                                           ALU.subtract, ALU.mult)
                    I(inst, "mV")
                aclV = tok("mV")

                @blk.scalar
                def _(e):
                    W(e, aclV)
                    inst = None
                    for c in range(NCHK):
                        inst = e.activation(eac[:, c, :], aclt[:, c, :], AF.Exp)
                    inst = e.activation(sp_sb[:, T:2 * T], sp_sb[:, 0:T], AF.Exp)
                    I(inst, "mA")
                eacA = tok("mA")

                prevCp = None
                for c in range(NCHK):
                    @blk.vector
                    def _(e, c=c, pv=prevCp):
                        W(e, eacA)
                        if pv is not None:
                            W(e, pv)
                        I(e.tensor_scalar(sp_sb[:, 0:128], aclt[:, c, :],
                                          aclt[:, c, 127:128], -1.0,
                                          ALU.subtract, ALU.mult), "mV")
                    subV = tok("mV")

                    @blk.scalar
                    def _(e, subV=subV):
                        W(e, subV)
                        I(e.activation(sp_sb[:, 128:256], sp_sb[:, 0:128], AF.Exp), "mA")
                    dteA = tok("mA")

                    @blk.tensor
                    def _(e, c=c, dteA=dteA, pv=prevCp):
                        W(e, dteA, eacA, K_ALL)
                        if pv is not None:
                            W(e, pv)
                        e.transpose(pb[5][0:128, 0:32], aclt[:, c, :], i32_f[:])
                        e.transpose(pb[5][0:128, 32:64], sp_sb[:, 128:256], i32_f[:])
                        e.transpose(pb[5][0:128, 64:96], sp_sb[:, T + c * 128:T + (c + 1) * 128], i32_f[:])
                        I(e.transpose(pb[5][0:1, 96:128], aclt[:, c, 127:128], i32_f[:]), "mT")
                    trT = tok("mT")

                    @blk.vector
                    def _(e, c=c, trT=trT):
                        W(e, trT)
                        e.tensor_copy(actok[:, c, :], pb[5][0:128, 0:32])
                        e.tensor_copy(dtet[:, c, :], pb[5][0:128, 32:64])
                        e.tensor_copy(dtegt[:, c, :], pb[5][0:128, 64:96])
                        I(e.tensor_copy(lamr_f[0:1, c * 32:(c + 1) * 32], pb[5][0:1, 96:128]), "mV")
                    prevCp = tok("mV")
                prepV = prevCp

                @blk.scalar
                def _(e):
                    W(e, prepV)
                    I(e.activation(lamr_f[:], lamr_f[:], AF.Exp), "mA")
                lamA = tok("mA")

                @blk.vector
                def _(e):
                    W(e, lamA)
                    I(e.tensor_copy(lamr_b[:], lamr_f[:]), "mV")
                lamV = tok("mV")

                @blk.tensor
                def _(e):
                    W(e, lamV, KG_ALL)
                    inst = None
                    for c in range(NCHK):
                        inst = e.matmul(pb[6][0:64, c * 32:(c + 1) * 32], ones_b[0:1, 0:64],
                                        lamr_b[0:1, c * 32:(c + 1) * 32], start=True, stop=True)
                    I(inst, "mT")
                lamT = tok("mT")

                @blk.vector
                def _(e):
                    W(e, lamT)
                    I(e.tensor_copy(lam64[:], pb[6][0:64, 0:512]), "mV")
                lamVV = tok("mV")

            with ExitStack() as s2:
                E2 = s2.enter_context
                qbuf = [E2(nc.sbuf_tensor(f"qbuf{i}", [128, 512], F32, side="right")) for i in range(4)]
                qacc = E2(nc.sbuf_tensor("qacc", [128, 132], F32, side="right"))
                qw1p = [E2(nc.sbuf_tensor(f"qw1p{i}", [128, 512], F32, side="right")) for i in range(2)]
                qi1p = [E2(nc.sbuf_tensor(f"qi1p{i}", [128, 512], I32, side="right")) for i in range(2)]
                qo1p = [E2(nc.sbuf_tensor(f"qo1p{i}", [128, 512], BF16, side="right")) for i in range(4)]
                qw1 = qw1p[0]
                cw = [E2(nc.sbuf_tensor(f"cw{i}", [128, 512], BF16, side="right")) for i in range(6)]
                cst = E2(nc.sbuf_tensor("cst", [128, 8, 512], BF16, side="right"))

                QT = ([("w1", k, p, fw1_t.ap()[k * 128:(k + 1) * 128, p * 512:(p + 1) * 512])
                       for k in range(8) for p in range(8)] +
                      [("w2", k, p, fw2_t.ap()[k * 128:(k + 1) * 128, p * 512:(p + 1) * 512])
                       for k in range(32) for p in range(2)])
                NQ = len(QT)

                aQ0 = C["mA"]

                @blk.sync
                def _(e):
                    for i in range(NQ):
                        if i < 2:
                            W(e, ipT, prepV, lamVV)
                        else:
                            e.wait_ge(S["mA"], aQ0 + (i - 1))
                        par = "dQa" if i % 2 == 0 else "dQb"
                        e.dma_start(out=qbuf[i % 2][:], in_=QT[i][3]).then_inc(S[par], 16)
                        C[par] += 16
                qaTotA, qaTotB = C["dQa"], C["dQb"]

                @blk.scalar
                def _(e):
                    for i in range(NQ):
                        par = "dQa" if i % 2 == 0 else "dQb"
                        e.wait_ge(S[par], 16 * (i // 2 + 1))
                        if i >= 1:
                            e.wait_ge(S["mA"], aQ0 + i)
                        I(e.activation(qw1[:], qbuf[i % 2][:], AF.Abs,
                                       accum_out=qacc[:, i:i + 1]), "mA")
                assert C["mA"] == aQ0 + NQ
                qaccA = tok("mA")

                OBLK = [(0, 512), (512, 512), (1024, 512), (1536, 512), (2048, 128)]
                KI = [(kk, ii) for kk in range(4) for ii in range(17)]
                convT0 = C["mT"]
                convV0 = C["mV"]


                @blk.gpsimd
                def _(e):
                    n = 0
                    gi = 0
                    for (o0, ow) in OBLK:
                        for tg in range(2):
                            si = gi - 1
                            if 0 <= si < 8:
                                e.wait_ge(S["mV"], convV0 + 4 * (si + 1))
                                spar = "dSPXa" if si % 2 == 0 else "dSPXb"
                                so0 = OBLK[si // 2][0]
                                stg = si % 2
                                dst = xt_sp.ap()[stg * 8:(stg + 1) * 8, :, so0:so0 + 512] \
                                    .rearrange("c p o -> p c o")
                                e.dma_start(out=dst, in_=cst[:, :, 0:512]).then_inc(S[spar], 16)
                                C[spar] += 16
                            for (kk, ii) in KI:
                                if n < 6:
                                    W(e, ipV, prepV, lamVV)
                                else:
                                    e.wait_ge(S["fCW"], n - 5)
                                par = ("dCWa", "dCWb", "dCWc", "dCWd", "dCWe", "dCWf")[n % 6]
                                e.dma_start(out=cw[n % 6][:, 0:ow],
                                            in_=convw_t.ap()[kk, ii * 128:(ii + 1) * 128,
                                                             o0:o0 + ow]).then_inc(S[par], 16)
                                C[par] += 16
                                n += 1
                            gi += 1
                xtspD = (("dSPXa", C["dSPXa"]), ("dSPXb", C["dSPXb"]))

                @blk.tensor
                def _(e):
                    W(e, ipV, KG_ALL)
                    n = 0
                    gi = 0
                    for (o0, ow) in OBLK:
                        for tg in range(2):
                            if gi == 3:
                                e.wait_ge(S["mP"], 1)
                                e.wait_ge(S["mV"], convV0 + 4 * 3)
                                e.matmul(pb[7][0:1, 0:1], ones_f[:, 0:1], qacc[:, 128:129],
                                         start=True, stop=True).then_inc(S["mP"], 1)
                                e.matmul(pb[7][0:1, 1:2], ones_f[:, 0:1], qacc[:, 129:130],
                                         start=True, stop=True).then_inc(S["mP"], 1)
                                e.wait_ge(S["mP"], 6)
                                e.matmul(pb[7][:, 2:3], triu_f[0:1, :], rowsc[0:1, 1024:1025],
                                         start=True, stop=True).then_inc(S["mP"], 1)
                                e.matmul(pb[7][:, 3:4], triu_f[0:1, :], rowsc[0:1, 1025:1026],
                                         start=True, stop=True).then_inc(S["mP"], 1)
                                e.wait_ge(S["mP"], 10)
                            if gi >= 1:
                                e.wait_ge(S["mV"], convV0 + 4 * gi)
                            inst = None
                            for ki, (kk, ii) in enumerate(KI):
                                par = ("dCWa", "dCWb", "dCWc", "dCWd", "dCWe", "dCWf")[n % 6]
                                e.wait_ge(S[par], 16 * (n // 6 + 1))
                                for tc in range(8):
                                    tci = tg * 8 + tc
                                    tcol = tci * 128 + kk
                                    inst = e.matmul(pb[tc][:, 0:ow], xbc[:, ii, tcol:tcol + 128],
                                                    cw[n % 6][:, 0:ow], start=(ki == 0), stop=False)
                                I(inst, "fCW")
                                n += 1
                            for tc in range(8):
                                inst = e.matmul(pb[tc][:, 0:ow], ones_b[0:1, :],
                                                cbrow_b[0:1, o0:o0 + ow], start=False, stop=True)
                            I(inst, "mT")
                            gi += 1

                @blk.scalar
                def _(e):
                    e.wait_ge(S["mP"], 4)
                    e.activation(rowsc[0:1, 1024:1026], rowsc[0:1, 1024:1026],
                                 AF.Ln).then_inc(S["mP"], 1)
                    e.wait_ge(S["mP"], 5)
                    e.activation(rowsc[0:1, 1024:1026], rowsc[0:1, 1024:1026],
                                 AF.Exp, scale=-1.0).then_inc(S["mP"], 1)

                qd = {-2: 10, -1: 10}

                def emit_qtiles(e, i0, i1):
                    ids = list(range(i0, i1))
                    for p in range(0, len(ids), 2):
                        pair = ids[p:p + 2]
                        toks = {}
                        for step in range(4):
                            for t in pair:
                                b = t % 2
                                if step == 0:
                                    par = ("dQa", "dQb", "dQc", "dQd")[t % 4]
                                    base = (qaTotA, qaTotB, 0, 0)[t % 4]
                                    e.wait_ge(S[par], base + 16 * (t // 4 + 1))
                                    if t >= 4:
                                        e.wait_ge(S[("dSQa", "dSQb", "dSQc", "dSQd")[t % 4]],
                                                  16 * ((t - 4) // 4 + 1))
                                    e.wait_ge(S["mP"], qd[t - 2])
                                    g = gam1 if QT[t][0] == "w1" else gam2
                                    toks[t] = I(e.tensor_scalar(qw1p[b][:], qbuf[t % 4][:],
                                                                g[:, 0:1], RM,
                                                                ALU.mult, ALU.add), "mP")[1]
                                elif step == 1:
                                    e.wait_ge(S["mP"], toks[t])
                                    toks[t] = I(e.tensor_copy(qi1p[b][:], qw1p[b][:]), "mP")[1]
                                elif step == 2:
                                    e.wait_ge(S["mP"], toks[t])
                                    toks[t] = I(e.tensor_scalar(qw1p[b][:], qi1p[b][:],
                                                                -RM, 1.0,
                                                                ALU.add, ALU.min), "mP")[1]
                                else:
                                    e.wait_ge(S["mP"], toks[t])
                                    toks[t] = I(e.tensor_scalar(qo1p[t % 4][:], qw1p[b][:],
                                                                -1.0, None,
                                                                ALU.max), "mP")[1]
                                    qd[t] = toks[t]

                QBATCH = {4: (0, 22), 5: (22, 44), 6: (44, 66), 7: (66, 88),
                          8: (88, 108), 9: (108, 128)}

                @blk.vector
                def _(e):
                    gi = 0
                    prevStt = None
                    for obi, (o0, ow) in enumerate(OBLK):
                        for tg in range(2):
                            if gi == 3:
                                e.wait_ge(S["mP"], 3)
                                e.tensor_scalar(rowsc[0:1, 1024:1025], pb[7][0:1, 0:1],
                                                1.0 / (FF * D), EPS, ALU.mult, ALU.add)
                                e.tensor_scalar(rowsc[0:1, 1025:1026], pb[7][0:1, 1:2],
                                                1.0 / (FF * D), EPS,
                                                ALU.mult, ALU.add).then_inc(S["mP"], 1)
                                e.wait_ge(S["mP"], 8)
                                e.tensor_copy(gam1[:], pb[7][:, 2:3]).then_inc(S["mP"], 1)
                                e.tensor_copy(gam2[:], pb[7][:, 3:4]).then_inc(S["mP"], 1)
                                C["mP"] = 10
                            W(e, P0V, lamVV)
                            if 1 <= gi <= 8 and obi <= 4:
                                si = gi - 1
                                if si < 8:
                                    e.wait_ge(S["dSPXa" if si % 2 == 0 else "dSPXb"],
                                              16 * (si // 2 + 1))
                            for half in range(2):
                                e.wait_ge(S["mT"], convT0 + gi + 1)
                                W(e, prevStt)
                                inst = None
                                for tc2 in range(4):
                                    tc = half * 4 + tc2
                                    inst = e.scalar_tensor_tensor(
                                        t1s[:, tc2 * 512:tc2 * 512 + ow], pb[tc][:, 0:ow],
                                        0.0, preluc[:, 0:1].broadcast_to([128, ow]),
                                        ALU.min, ALU.mult)
                                negT = I(inst, "mV")
                                W(e, negT)
                                inst = None
                                for tc2 in range(4):
                                    tc = half * 4 + tc2
                                    tci = tg * 8 + tc
                                    if o0 < DI:
                                        dst = cst[:, tc, 0:ow]
                                    else:
                                        dst = bctok[:, tci, 0:ow]
                                    inst = e.scalar_tensor_tensor(dst, pb[tc][:, 0:ow], 0.0,
                                                                  t1s[:, tc2 * 512:tc2 * 512 + ow],
                                                                  ALU.max, ALU.add)
                                prevStt = I(inst, "mV")
                            if gi == 2:
                                W(e, qaccA)
                                e.tensor_reduce(qacc[:, 128:129], qacc[:, 0:64],
                                                mybir.AxisListType.X, ALU.add)
                                e.tensor_reduce(qacc[:, 129:130], qacc[:, 64:128],
                                                mybir.AxisListType.X, ALU.add).then_inc(S["mP"], 1)
                            if gi in QBATCH:
                                emit_qtiles(e, *QBATCH[gi])
                            gi += 1
                assert C["mV"] == convV0 + 4 * 10
                assert C["mP"] == 10 + 4 * NQ
                convV = tok("mV")
                qpassB = tok("mP")

                dbg_tap("bctok", bctok[:].rearrange("p c d -> p (c d)"), [convV])

                @blk.sync
                def _(e):
                    def spill(i):
                        e.wait_ge(S["mP"], qd[i])
                        wname, k, p = QT[i][0], QT[i][1], QT[i][2]
                        if wname == "w1":
                            dst = wq1_sp.ap()[4 * p:4 * p + 4, k].rearrange("m f c -> f m c")
                        else:
                            dst = wq2_sp.ap()[4 * p:4 * p + 4, k].rearrange("m f c -> f m c")
                        par = ("dSQa", "dSQb", "dSQc", "dSQd")[i % 4]
                        e.dma_start(out=dst,
                                    in_=qo1p[i % 4][:].rearrange("f (m c) -> f m c",
                                                                 m=4)).then_inc(S[par], 16)
                        C[par] += 16
                    W(e, qaccA)
                    for i in range(NQ):
                        if i >= 4:
                            spill(i - 4)
                        par = ("dQa", "dQb", "dQc", "dQd")[i % 4]
                        e.dma_start(out=qbuf[i % 4][:], in_=QT[i][3]).then_inc(S[par], 16)
                        C[par] += 16
                    for i in range(NQ - 4, NQ):
                        spill(i)
                qspD = (("dSQa", C["dSQa"]), ("dSQb", C["dSQb"]),
                        ("dSQc", C["dSQc"]), ("dSQd", C["dSQd"]))

        Sa = E(nc.sbuf_tensor("Sa", [64, NH * HD], F32))
        Sab = E(nc.sbuf_tensor("Sab", [64, NH * HD], BF16))
        hbb = E(nc.sbuf_tensor("hbb", [128, 16, 128], BF16))
        xtkb = [E(nc.sbuf_tensor(f"xtkb{i}", [128, DI], BF16)) for i in range(2)]
        ubf = E(nc.sbuf_tensor("ubf", [128, 8, BS], F32))
        castb = E(nc.sbuf_tensor("castb", [128, 8, BS], BF16))
        lnx = E(nc.sbuf_tensor("lnx", [128, 8, BS], BF16))
        u2h0s = E(nc.sbuf_tensor("u2h0s", [128, 4, 256], F32))

        with ExitStack() as s3:
            E3 = s3.enter_context
            bga = E3(nc.sbuf_tensor("bga", [128, NH * HD], BF16, side="right"))
            agst = E3(nc.sbuf_tensor("agst", [64, DI], F32, side="right"))
            gth = E3(nc.sbuf_tensor("gth", [64, 4 * DI], F32, side="right"))

            prevT2 = [None, None]
            for c in range(NCHK):
                bk = 5 + c % 2

                @blk.tensor
                def _(e, c=c, bk=bk, pv=prevT2[c % 2]):
                    W(e, convV, K_ALL)
                    if pv is not None:
                        W(e, pv)
                    e.transpose(pb[bk][0:64, 0:128], bctok[:, c, 0:64], i128_f[:])
                    I(e.transpose(pb[bk][0:64, 128:256], bctok[:, c, 64:128],
                                  i128_f[:]), "mT")
                trT = tok("mT")

                @blk.vector
                def _(e, c=c, bk=bk, trT=trT):
                    W(e, trT)
                    e.tensor_copy(bfeat[:, c, :], pb[bk][0:64, 0:128])
                    I(e.tensor_copy(cfeat[:, c, :], pb[bk][0:64, 128:256]), "mV")
                prevT2[c % 2] = tok("mV")
            bcV = prevT2[1]

            prevS2 = [None, None]
            for c in range(NCHK):
                bk = 5 + c % 2

                @blk.tensor
                def _(e, c=c, bk=bk, pv=prevS2[c % 2]):
                    W(e, bcV)
                    if pv is not None:
                        W(e, pv)
                    I(e.matmul(pb[bk][:, 0:128], bfeat[:, c, :], cfeat[:, c, :],
                               start=True, stop=True), "mT")
                scT = tok("mT")

                @blk.vector
                def _(e, c=c, bk=bk, scT=scT):
                    W(e, scT)
                    I(e.tensor_tensor(scoresM[:, c, :], pb[bk][:, 0:128],
                                      triu_f[:], ALU.mult), "mV")
                prevS2[c % 2] = tok("mV")
            scoresV = prevS2[1]
            dbg_tap("scoresM", scoresM[:].rearrange("p c l -> p (c l)"), [scoresV])

            expT0 = C["mT"]
            expV0 = C["mV"]

            @blk.sync
            def _(e):
                for c in range(NCHK):
                    if c < 2:
                        W(e, *xtspD)
                        W(e, convV)
                    else:
                        e.wait_ge(S["mT"], expT0 + (c - 1))
                    par = "dXTa" if c % 2 == 0 else "dXTb"
                    e.dma_start(out=xtkb[c % 2][:], in_=xt_sp.ap()[c]).then_inc(S[par], 16)
                    C[par] += 16

            bgav2 = [bga[:].rearrange("p (h d) -> p h d", h=NH),
                     hbb[:].rearrange("p a b -> p (a b)").rearrange("p (h d) -> p h d", h=NH)]

            @blk.vector
            def _(e):
                for c in range(NCHK):
                    if c >= 2:
                        e.wait_ge(S["mT"], expT0 + c - 1)
                    W(e, convV)
                    I(e.tensor_tensor(bgav2[c % 2],
                                      bctok[:, c, 0:64].unsqueeze(1).broadcast_to([128, NH, 64]),
                                      dtegt[:, c, :].unsqueeze(2).broadcast_to([128, NH, 64]),
                                      ALU.mult), "mV")
            assert C["mV"] == expV0 + NCHK

            @blk.tensor
            def _(e):
                inst = None
                for c in range(NCHK):
                    par = "dXTa" if c % 2 == 0 else "dXTb"
                    e.wait_ge(S[par], 16 * (c // 2 + 1))
                    e.wait_ge(S["mV"], expV0 + c + 1)
                    for h in range(NH):
                        bank = h // 8
                        col = (h % 8) * 64
                        inst = e.matmul(pb[bank][0:64, col:col + 64], bgav2[c % 2][:, h, :],
                                        xtkb[c % 2][:, h * 64:(h + 1) * 64],
                                        start=(c == 0 and h % 8 == 0),
                                        stop=(c == NCHK - 1 and h % 8 == 7))
                    I(inst, "mT")
            assert C["mT"] == expT0 + NCHK
            expT = tok("mT")
            dbg_tap("xt0", xtkb[0][:], [expT])

            @blk.vector
            def _(e):
                W(e, expT)
                inst = None
                for bank in range(4):
                    inst = e.tensor_copy(agst[:, bank * 512:(bank + 1) * 512], pb[bank][0:64, 0:512])
                I(inst, "mV")
            agV = tok("mV")
            dbg_tap("sfin", agst[:], [agV])

            @blk.sync
            def _(e):
                W(e, agV)
                e.dma_start(out=ag_in.ap(), in_=agst[:]).then_inc(S["dX"], 16)
                C["dX"] += 16

            w2b = [w3buf[i][:].rearrange("p (k c) -> p k c", k=8) for i in range(3)]
            NS = {"GW": 0, "AW": 0, "F1": 0, "F2": 0}

            @blk.sync
            def _(e):
                W(e, *xtspD)
                e.dma_start(out=ubf[:], in_=uview[:, :, 3:3 + BS]).then_inc(S["dUBa"], 16)
                C["dUBa"] += 16
            ubTok0 = tok("dUBa")

            @blk.gpsimd
            def _(e):
                W(e, *xtspD)
                W(e, *qspD)
                W(e, convV)
                e.dma_start(out=lnx[:], in_=uview[:, :, 3:3 + BS]).then_inc(S["dUCa"], 16)
                C["dUCa"] += 16
            ubTok20 = tok("dUCa")

            lnv0 = ln_fm(ubf, lnx, lnx, 8, [(0, 256), (256, 256)],
                         (ubTok0, ubTok20, agV, agV) + tuple(qspD))

            u2T0h = C["mT"]
            gw0h = {p: C[p] for p in ("dGWa", "dGWb", "dGWc")}

            @blk.gpsimd
            def _(e):
                for m in range(16):
                    NS["GW"] += 1
                    if m < 3:
                        W(e, bias3T)
                    else:
                        e.wait_ge(S["mT"], u2T0h + (m - 2))
                    par = ("dGWa", "dGWb", "dGWc")[m % 3]
                    e.dma_start(out=w2b[m % 3],
                                in_=w2_t.ap()[m].rearrange("k f c -> f k c")
                                ).then_inc(S[par], 16)
                    C[par] += 16

            sumV0 = C["mV"]

            @blk.gpsimd
            def _(e):
                e.wait_ge(S["dX"], 16)
                e.collective_compute("AllGather", ALU.bypass,
                                     replica_groups=[list(range(ncores))],
                                     ins=[ag_in.ap().opt()],
                                     outs=[ag_out.ap().opt()]).then_inc(S["cc"], 1)
                C["cc"] += 1
                e.wait_ge(S["cc"], 1)
                nb1 = min(4, ncores)
                for r in range(nb1):
                    e.dma_start(out=gth[0:64, r * DI:(r + 1) * DI],
                                in_=ag_out.ap()[r * 64:(r + 1) * 64, :]).then_inc(S["dG2"], 16)
                    C["dG2"] += 16
                if ncores > 4:
                    e.wait_ge(S["mV"], sumV0 + 4)
                    for r in range(4, ncores):
                        e.dma_start(out=gth[0:64, (r - 4) * DI:(r - 3) * DI],
                                    in_=ag_out.ap()[r * 64:(r + 1) * 64, :]).then_inc(S["dG2"], 16)
                        C["dG2"] += 16

            @blk.vector
            def _(e):
                W(e, ("dG2", 16 * min(4, ncores)), K_ALL, agV)
                prev = None
                for r in range(ncores):
                    if r == 4:
                        e.wait_ge(S["dG2"], 128)
                    W(e, prev)
                    slot = (r % 4) * DI
                    if r == 0:
                        prev = I(e.tensor_scalar(Sa[:], gth[0:64, 0:DI],
                                                 selc[:, 0:1], None, ALU.mult), "mV")
                    else:
                        prev = I(e.scalar_tensor_tensor(Sa[:], gth[0:64, slot:slot + DI],
                                                        selc[:, r:r + 1], Sa[:],
                                                        ALU.mult, ALU.add), "mV")
            sinitV = tok("mV")
            dbg_tap("sinit", Sa[:], [sinitV])

        with ExitStack() as s5:
            E5 = s5.enter_context
            wq2b = [E5(nc.sbuf_tensor(f"wq2b{i}", [128, 16, 128], BF16)) for i in range(2)]
            attn_s = E5(nc.sbuf_tensor("attn_s", [128, 8, BS], F32))
            x1b = E5(nc.sbuf_tensor("x1b", [128, 8, BS], F32))
            w3gv = [w3gb[0][:].rearrange("p (k c) -> p k c", k=8),
                    w3gb[1][:].rearrange("p (k c) -> p k c", k=8),
                    hbb[:, 0:8, :]]
            wq1v = w2b
            znv = castb[:].rearrange("p a b -> p (a b)").rearrange("p (k w) -> p k w", k=16)
            dsubv2 = [t1s[:, 0:512], t1s[:, 512:1024]]
            dexpv2 = [sqs[:, 0:512].rearrange("p (a b) -> p a b", a=4),
                      sqs[:, 512:1024].rearrange("p (a b) -> p a b", a=4)]

            tok_pb01 = agV
            tok_pb23 = agV
            tok_pb45 = convV
            tok_pb67 = scoresV
            tok_castb = None
            tok_ubf = None
            tok_chainEvac = sinitV
            tok_sab_pe = None
            tok_xt_pe = {-2: expT, -1: expT}
            tok_u2_read = None
            tok_znb_read = None
            tok_hsb_read = None
            tok_lnx_read = None
            tok_wbuf = bias3T
            tok_w3gbuf = bias3V
            tok_wq2b = sinitV
            tok_hbb = None

            for j in range(NBLK):
                ucol0 = 3 + j * BS
                upar = "dUBa" if j % 2 == 0 else "dUBb"

                with ExitStack() as sB:
                    EB = sB.enter_context
                    u2 = EB(nc.sbuf_tensor(f"u2_{j}", [128, 16, 256], F32, side="right"))
                    hsb = EB(nc.sbuf_tensor(f"hsb_{j}", [128, 16, 128], F32, side="right"))
                    decS = EB(nc.sbuf_tensor(f"decS_{j}", [128, 16, 128], BF16, side="right"))
                    cdec = EB(nc.sbuf_tensor(f"cdec_{j}", [64, 16, 128], BF16, side="right"))
                    bdv = EB(nc.sbuf_tensor(f"bdv_{j}", [128, 16, HD], BF16, side="right"))
                    lamI = EB(nc.sbuf_tensor(f"lamI_{j}", [64, 16, HD], BF16, side="right"))

                    ucpar = "dUCa" if j % 2 == 0 else "dUCb"
                    tok_castb_blk = tok_castb
                    if j == 0:
                        lnv = lnv0
                        ubTok, ubTok2 = ubTok0, ubTok20
                    else:
                        @blk.sync
                        def _(e, j=j, ucol0=ucol0):
                            W(e, tok_ubf)
                            e.dma_start(out=ubf[:], in_=uview[:, :, ucol0:ucol0 + BS]).then_inc(S[upar], 16)
                            C[upar] += 16
                        ubTok = tok(upar)

                        @blk.gpsimd
                        def _(e, j=j, ucol0=ucol0, ucpar=ucpar):
                            W(e, tok_lnx_u)
                            e.dma_start(out=lnx[:], in_=uview[:, :, ucol0:ucol0 + BS]).then_inc(S[ucpar], 16)
                            C[ucpar] += 16
                        ubTok2 = tok(ucpar)

                        lnv = ln_fm(ubf, lnx, lnx, 8, [(0, 256), (256, 256)],
                                    (ubTok, ubTok2, tok_pb01, tok_pb23))
                    tok_pb01 = lnv
                    tok_pb23 = lnv
                    lnx_free_half = {0: None, 1: None}

                    for half in range(2):
                        hc0 = half * 256
                        u2T0 = C["mT"]
                        u2A0 = C["mA"]

                        gw0 = gw0h if (j == 0 and half == 0) else \
                            {p: C[p] for p in ("dGWa", "dGWb", "dGWc")}
                        if not (j == 0 and half == 0):
                            @blk.gpsimd
                            def _(e, half=half, u2T0=u2T0, tok_wbuf=tok_wbuf):
                                for m in range(16):
                                    NS["GW"] += 1
                                    if m < 3:
                                        W(e, tok_wbuf)
                                    else:
                                        e.wait_ge(S["mT"], u2T0 + (m - 2))
                                    par = ("dGWa", "dGWb", "dGWc")[m % 3]
                                    e.dma_start(out=w2b[m % 3],
                                                in_=w2_t.ap()[m].rearrange("k f c -> f k c")
                                                ).then_inc(S[par], 16)
                                    C[par] += 16
                        else:
                            assert u2T0 == u2T0h

                        @blk.tensor
                        def _(e, half=half, hc0=hc0, u2T0=u2T0, u2A0=u2A0, lnv=lnv):
                            W(e, lnv, tok_pb01, tok_pb23)
                            for m in range(16):
                                par = ("dGWa", "dGWb", "dGWc")[m % 3]
                                e.wait_ge(S[par], gw0[par] + 16 * (m // 3 + 1))
                                if m >= 4:
                                    e.wait_ge(S["mA"], u2A0 + (m - 3))
                                inst = None
                                for k in range(8):
                                    inst = e.matmul(pb[m % 4][:, 0:256],
                                                    w2b[m % 3][:, k, :],
                                                    lnx[:, k, hc0:hc0 + 256],
                                                    start=(k == 0), stop=(k == 7))
                                I(inst, "mT")
                        assert C["mT"] == u2T0 + 16

                        @blk.scalar
                        def _(e, j=j, half=half, u2T0=u2T0, u2A0=u2A0, u2=u2):
                            act_set(e, 10)
                            if j == 0 and half == 0:
                                W(e, ("dX", 16))
                            for m in range(16):
                                e.wait_ge(S["mT"], u2T0 + m + 1)
                                dst = u2[:, m, :]
                                if j == 0 and half == 0 and m < 4:
                                    dst = u2h0s[:, m, :]
                                I(e.activation(dst, pb[m % 4][:, 0:256], AF.Gelu), "mA")
                            act_set(e, 6)
                        assert C["mA"] == u2A0 + 16
                        u2V = tok("mA")
                        tok_pb01 = u2V
                        tok_pb23 = u2V
                        tok_wbuf = ("mT", u2T0 + 16)
                        lnx_free_half[half] = ("mT", u2T0 + 16)

                        for c2 in range(2):
                            ci = j * 4 + half * 2 + c2
                            coff = c2 * 128
                            xpar = "dXTa" if ci % 2 == 0 else "dXTb"

                            @blk.sync
                            def _(e, ci=ci, xpar=xpar):
                                W(e, tok_xt_pe.get(ci - 2, expT))
                                e.dma_start(out=xtkb[ci % 2][:], in_=xt_sp.ap()[ci]).then_inc(S[xpar], 16)
                                C[xpar] += 16
                            xtTok = tok(xpar)

                            @blk.vector
                            def _(e, ci=ci):
                                W(e, tok_chainEvac, tok_sab_pe)
                                I(e.tensor_copy(Sab[:], Sa[:]), "mV")
                            sabV = tok("mV")

                            hh_done = None
                            yTs = []
                            for hh in range(2):
                                hbase = hh * 16
                                pbY0 = 4 + 2 * hh

                                @blk.vector
                                def _(e, ci=ci, hh=hh, hbase=hbase, hh_done=hh_done):
                                    W(e, lamVV, KG_ALL, hh_done)
                                    if hh == 0:
                                        W(e, tok_chainEvac)
                                    I(e.tensor_tensor(
                                        lamI[:],
                                        i64_b[:].unsqueeze(1).broadcast_to([64, 16, 64]),
                                        lam64[:, ci * 32 + hbase:ci * 32 + hbase + 16].unsqueeze(2)
                                        .broadcast_to([64, 16, 64]),
                                        ALU.mult), "mV")
                                lamIV = tok("mV")

                                @blk.vector
                                def _(e, ci=ci, hh=hh, hh_done=hh_done):
                                    W(e, hh_done)
                                    I(e.tensor_tensor(
                                        bdv[:],
                                        bctok[:, ci, 0:64].unsqueeze(1).broadcast_to([128, 16, 64]),
                                        dtet[:, ci, hh * 16:hh * 16 + 16].unsqueeze(2)
                                        .broadcast_to([128, 16, 64]),
                                        ALU.mult), "mV")
                                bdV = tok("mV")

                                prevDec = [hh_done, hh_done]
                                for hg in range(4):
                                    bank = 2 + hg % 2
                                    hgg = hbase + hg * 4

                                    @blk.tensor
                                    def _(e, ci=ci, hgg=hgg, bank=bank, pv=prevDec[hg % 2]):
                                        W(e, tok_pb23, pv, K_ALL)
                                        inst = None
                                        for hi in range(4):
                                            h = hgg + hi
                                            inst = e.matmul(pb[bank][:, hi * 128:(hi + 1) * 128],
                                                            i32_f[:, h:h + 1].broadcast_to([32, 128]),
                                                            aclt[:, ci, :], start=True, stop=True)
                                        I(inst, "mT")
                                    dbT = tok("mT")

                                    @blk.vector
                                    def _(e, ci=ci, hg=hg, hgg=hgg, bank=bank, dbT=dbT):
                                        W(e, dbT)
                                        inst = None
                                        for hi in range(4):
                                            h = hgg + hi
                                            inst = e.tensor_scalar(dsubv2[hg % 2][:, hi * 128:(hi + 1) * 128],
                                                                   pb[bank][:, hi * 128:(hi + 1) * 128],
                                                                   actok[:, ci, h:h + 1], 0.0,
                                                                   ALU.subtract, ALU.min)
                                        I(inst, "mV")
                                    dsV = tok("mV")

                                    @blk.scalar
                                    def _(e, hg=hg, dsV=dsV, pv=prevDec[hg % 2]):
                                        W(e, dsV, pv)
                                        I(e.activation(dexpv2[hg % 2].rearrange("p a b -> p (a b)"),
                                                       dsubv2[hg % 2], AF.Exp), "mA")
                                    dxA = tok("mA")

                                    @blk.vector
                                    def _(e, ci=ci, hg=hg, dxA=dxA):
                                        W(e, dxA)
                                        I(e.tensor_tensor(
                                            decS[:, hg * 4:(hg + 1) * 4, :],
                                            dexpv2[hg % 2],
                                            scoresM[:, ci, :].unsqueeze(1).broadcast_to([128, 4, 128]),
                                            ALU.mult), "mV")
                                    prevDec[hg % 2] = tok("mV")
                                decSV = prevDec[1]

                                prevCd = [prevDec[0], prevDec[1]]
                                for hg in range(4):
                                    bank = 2 + hg % 2
                                    hgg = hbase + hg * 4

                                    @blk.tensor
                                    def _(e, ci=ci, hgg=hgg, bank=bank, pv=prevCd[hg % 2]):
                                        W(e, pv, eacA, KG_ALL)
                                        inst = None
                                        for hi in range(4):
                                            h = hgg + hi
                                            inst = e.matmul(pb[bank][0:64, hi * 128:(hi + 1) * 128],
                                                            i32_b[:, h:h + 1].broadcast_to([32, 64]),
                                                            eac[:, ci, :], start=True, stop=True)
                                        I(inst, "mT")
                                    eaT = tok("mT")

                                    @blk.vector
                                    def _(e, ci=ci, hg=hg, bank=bank, eaT=eaT):
                                        W(e, eaT)
                                        I(e.tensor_tensor(
                                            cdec[:, hg * 4:(hg + 1) * 4, :],
                                            cfeat[:, ci, :].unsqueeze(1).broadcast_to([64, 4, 128]),
                                            pb[bank][0:64, 0:512].rearrange("p (a b) -> p a b", a=4),
                                            ALU.mult), "mV")
                                    prevCd[hg % 2] = tok("mV")
                                cdecV = prevCd[1]
                                tok_pb23 = cdecV

                                pbXf = tok_pb45 if hh == 0 else tok_pb67

                                @blk.tensor
                                def _(e, ci=ci, hh=hh, hbase=hbase, pbY0=pbY0, xtTok=xtTok,
                                      sabV=sabV, decSV=decSV, cdecV=cdecV, pbXf=pbXf):
                                    W(e, xtTok, sabV, decSV, cdecV, pbXf)
                                    inst = None
                                    for hp in range(8):
                                        bank = pbY0 + hp // 4
                                        col = (hp % 4) * 128
                                        for sub in range(2):
                                            h16 = 2 * hp + sub
                                            h = hbase + h16
                                            pr = slice(64 * sub, 64 * (sub + 1))
                                            inst = e.matmul(pb[bank][pr, col:col + 128],
                                                            xtkb[ci % 2][:, h * 64:(h + 1) * 64],
                                                            decS[:, h16, :], start=True, stop=False)
                                            inst = e.matmul(pb[bank][pr, col:col + 128],
                                                            Sab[:, h * 64:(h + 1) * 64],
                                                            cdec[:, h16, :], start=False, stop=True)
                                    I(inst, "mT")
                                yT = tok("mT")
                                yTs.append(yT)

                                @blk.tensor
                                def _(e, ci=ci, hh=hh, hbase=hbase, bdV=bdV, lamIV=lamIV,
                                      sabV=sabV, xtTok=xtTok):
                                    W(e, bdV, lamIV, sabV, xtTok, tok_pb01)
                                    inst = None
                                    for h16 in range(16):
                                        h = hbase + h16
                                        bank = h16 // 8
                                        col = (h16 % 8) * 64
                                        inst = e.matmul(pb[bank][0:64, col:col + 64],
                                                        lamI[:, h16, :], Sab[:, h * 64:(h + 1) * 64],
                                                        start=True, stop=False)
                                        inst = e.matmul(pb[bank][0:64, col:col + 64],
                                                        bdv[:, h16, :], xtkb[ci % 2][:, h * 64:(h + 1) * 64],
                                                        start=False, stop=True)
                                    I(inst, "mT")
                                cwT = tok("mT")

                                @blk.vector
                                def _(e, hh=hh, cwT=cwT):
                                    W(e, cwT)
                                    e.tensor_copy(Sa[:, hh * 1024:hh * 1024 + 512], pb[0][0:64, 0:512])
                                    I(e.tensor_copy(Sa[:, hh * 1024 + 512:hh * 1024 + 1024],
                                                    pb[1][0:64, 0:512]), "mV")
                                chainEv = tok("mV")
                                tok_pb01 = chainEv
                                hh_done = chainEv

                            tok_chainEvac = hh_done
                            tok_sab_pe = hh_done
                            tok_xt_pe[ci] = hh_done

                            @blk.vector
                            def _(e, ci=ci, coff=coff, yTs=tuple(yTs), u2V=u2V, u2=u2):
                                W(e, yTs[0], yTs[1], u2V, tok_hsb_read, tok_hbb)
                                inst = None
                                for b4 in range(4):
                                    src = u2[:, b4 * 4:(b4 + 1) * 4, coff:coff + 128]
                                    if ci < 2 and b4 == 0:
                                        src = u2h0s[:, 0:4, coff:coff + 128]
                                    e.tensor_tensor(
                                        hsb[:, b4 * 4:(b4 + 1) * 4, :],
                                        src,
                                        pb[4 + b4][:, 0:512].rearrange("p (a b) -> p a b", a=4),
                                        ALU.mult)
                                    inst = e.tensor_tensor(
                                        hbb[:, b4 * 4:(b4 + 1) * 4, :],
                                        src,
                                        pb[4 + b4][:, 0:512].rearrange("p (a b) -> p a b", a=4),
                                        ALU.mult)
                                I(inst, "mV")
                            hV = tok("mV")
                            tok_pb45 = hV
                            tok_pb67 = hV
                            tok_u2_read = hV
                            if dbg == "hsb0" and ci == 0:
                                dbg_tap("hsb0", hsb[:].rearrange("p a b -> p (a b)"), [hV])

                            hbV = hV

                            gv = ln_fm(hsb, hbb, znv[:, :, coff:coff + 128], 16, [(0, 128)],
                                       (hV, hbV, tok_pb01, tok_pb23, tok_znb_read,
                                        tok_castb_blk))
                            tok_hsb_read = gv
                            tok_pb01 = gv
                            tok_pb23 = gv
                            znReady = gv
                            if dbg == "zn0" and ci == 1:
                                dbg_tap("zn0", castb[:].rearrange("p a b -> p (a b)"), [gv])

                        atT0 = C["mT"]
                        atV0 = C["mV"]
                        aw0 = {p: C[p] for p in ("dAWa", "dAWb", "dAWc")}

                        @blk.sync
                        def _(e, half=half, atT0=atT0, tok_w3gbuf=tok_w3gbuf):
                            for m in range(8):
                                for piece in range(2):
                                    idx = m * 2 + piece
                                    NS["AW"] += 1
                                    if idx < 3:
                                        W(e, *w3spD)
                                        W(e, tok_w3gbuf, znReady)
                                    else:
                                        e.wait_ge(S["mT"], atT0 + (idx - 3) // 2 + 1)
                                    par = ("dAWa", "dAWb", "dAWc")[idx % 3]
                                    e.dma_start(out=w3gv[idx % 3],
                                                in_=w3g_sp.ap()[m, piece * 8:(piece + 1) * 8]
                                                .rearrange("k f c -> f k c")).then_inc(S[par], 16)
                                    C[par] += 16

                        @blk.tensor
                        def _(e, half=half, hc0=hc0, atT0=atT0, atV0=atV0, znReady=znReady,
                              aw0=aw0):
                            W(e, tok_pb45, tok_pb67, znReady)
                            for m in range(8):
                                if m >= 4:
                                    e.wait_ge(S["mV"], atV0 + (m - 3))
                                inst = None
                                for piece in range(2):
                                    idx = m * 2 + piece
                                    par = ("dAWa", "dAWb", "dAWc")[idx % 3]
                                    e.wait_ge(S[par], aw0[par] + 16 * (idx // 3 + 1))
                                    for k8 in range(8):
                                        k = piece * 8 + k8
                                        inst = e.matmul(pb[4 + m % 4][:, 0:256],
                                                        w3gv[idx % 3][:, k8, :], znv[:, k, :],
                                                        start=(k == 0), stop=(k == 15))
                                I(inst, "mT")
                        assert C["mT"] == atT0 + 8

                        @blk.vector
                        def _(e, j=j, half=half, hc0=hc0, atT0=atT0, atV0=atV0,
                              lnx_free=lnx_free_half[half]):
                            if half == 0:
                                W(e, ("dO", 16 * j))
                            for m in range(8):
                                e.wait_ge(S["mT"], atT0 + m + 1)
                                W(e, bias3V)
                                ev = I(e.tensor_scalar(attn_s[:, m, hc0:hc0 + 256],
                                                       pb[4 + m % 4][:, 0:256],
                                                       bias3[:, m:m + 1], None, ALU.add), "mV")
                                W(e, ev)
                                if m == 0:
                                    W(e, lnx_free)
                                I(e.tensor_copy(lnx[:, m, hc0:hc0 + 256],
                                                attn_s[:, m, hc0:hc0 + 256]), "mP")
                        assert C["mV"] == atV0 + 8
                        attnV = tok("mV")
                        acpV = tok("mP")
                        tok_pb45 = attnV
                        tok_pb67 = attnV
                        tok_znb_read = attnV
                        tok_hbb = ("mT", atT0 + 8)
                        tok_w3gbuf = ("mT", atT0 + 8)

                    if dbg == "attn0" and j == 0:
                        dbg_tap("attn0", attn_s[:].rearrange("p a b -> p (a b)"), [acpV])

                    x1V, x1cV = ln_fm(attn_s, lnx, None, 8, [(0, 256), (256, 256)],
                                      (attnV, acpV, tok_pb01, tok_pb23, ubTok),
                                      res_out=x1b, res_in=ubf,
                                      copy_out=castb, copy_gate=tok_w3gbuf)
                    tok_castb = x1V
                    tok_pb01 = x1V
                    tok_pb23 = x1V
                    tok_ubf = x1V
                    if dbg == "x1b0" and j == 0:
                        dbg_tap("x1b0", x1b[:].rearrange("p a b -> p (a b)"), [x1V])

                    ln2V = ln_fm(x1b, castb, lnx, 8, [(0, 256), (256, 256)],
                                 (x1V, x1cV, tok_pb01, tok_pb23))
                    tok_castb = ln2V
                    tok_pb01 = ln2V
                    tok_pb23 = ln2V
                    tok_lnx_read = None

                with ExitStack() as sF:
                    h1b = sF.enter_context(nc.sbuf_tensor(f"h1b_{j}", [128, 32, BS], BF16, side="right"))
                    wq2b2 = sF.enter_context(nc.sbuf_tensor(f"wq2b2_{j}", [128, 16, 128], BF16, side="right"))
                    wq2b3 = [wq2b[0], wq2b[1], wq2b2]

                    f1T0 = C["mT"]
                    f1A0 = C["mA"]

                    f10 = {p: C[p] for p in ("dF1a", "dF1b", "dF1c")}

                    @blk.sync
                    def _(e, j=j, f1T0=f1T0, tok_wbuf=tok_wbuf):
                        for m in range(32):
                            NS["F1"] += 1
                            if m < 3:
                                W(e, *qspD)
                                W(e, ln2V, tok_wbuf)
                            else:
                                e.wait_ge(S["mT"], f1T0 + (m - 2))
                            par = ("dF1a", "dF1b", "dF1c")[m % 3]
                            e.dma_start(out=wq1v[m % 3],
                                        in_=wq1_sp.ap()[m].rearrange("k f c -> f k c")
                                        ).then_inc(S[par], 16)
                            C[par] += 16

                    @blk.tensor
                    def _(e, f1T0=f1T0, f1A0=f1A0, ln2V=ln2V, f10=f10):
                        W(e, ln2V, tok_pb45, tok_pb67)
                        for m in range(32):
                            par = ("dF1a", "dF1b", "dF1c")[m % 3]
                            e.wait_ge(S[par], f10[par] + 16 * (m // 3 + 1))
                            if m >= 4:
                                e.wait_ge(S["mA"], f1A0 + (m - 3))
                            inst = None
                            for k in range(8):
                                inst = e.matmul(pb[4 + m % 4][:, 0:512], wq1v[m % 3][:, k, :],
                                                lnx[:, k, :], start=(k == 0), stop=(k == 7))
                            I(inst, "mT")
                    assert C["mT"] == f1T0 + 32

                    @blk.scalar
                    def _(e, f1T0=f1T0, f1A0=f1A0):
                        for m in range(32):
                            e.wait_ge(S["mT"], f1T0 + m + 1)
                            W(e, K_ALL)
                            I(e.activation(h1b[:, m, :], pb[4 + m % 4][:, 0:512], AF.Relu,
                                           bias=b1c[:, m:m + 1]), "mA")
                    assert C["mA"] == f1A0 + 32
                    h1V = tok("mA")
                    tok_pb45 = h1V
                    tok_pb67 = h1V
                    tok_wbuf = ("mT", f1T0 + 32)
                    tok_lnx_u = ("mT", f1T0 + 32)
                    if dbg == "h1b0" and j == 0:
                        dbg_tap("h1b0", h1b[:].rearrange("p a b -> p (a b)"), [h1V])

                    f2T0 = C["mT"]
                    f2V0 = C["mV"]

                    @blk.sync
                    def _(e, f2T0=f2T0, tok_wq2b=tok_wq2b, wq2b3=wq2b3):
                        for m in range(8):
                            for piece in range(2):
                                idx = m * 2 + piece
                                NS["F2"] += 1
                                n = NS["F2"] - 1
                                if idx < 3:
                                    W(e, *qspD)
                                    W(e, *xtspD)
                                    W(e, tok_wq2b)
                                else:
                                    e.wait_ge(S["mT"], f2T0 + (idx - 3) // 2 + 1)
                                par = ("dF2a", "dF2b", "dF2c")[n % 3]
                                e.dma_start(out=wq2b3[n % 3][:],
                                            in_=wq2_sp.ap()[m, piece * 16:(piece + 1) * 16]
                                            .rearrange("k f c -> f k c")).then_inc(S[par], 16)
                                C[par] += 16

                    @blk.tensor
                    def _(e, f2T0=f2T0, f2V0=f2V0, h1V=h1V, wq2b3=wq2b3):
                        W(e, h1V, tok_pb67, tok_pb45)
                        nbase = NS["F2"] - 16
                        for m in range(8):
                            if m >= 4:
                                e.wait_ge(S["mV"], f2V0 + (m - 3))
                            inst = None
                            for piece in range(2):
                                n = nbase + m * 2 + piece
                                par = ("dF2a", "dF2b", "dF2c")[n % 3]
                                e.wait_ge(S[par], 16 * (n // 3 + 1))
                                for k16 in range(16):
                                    k = piece * 16 + k16
                                    inst = e.matmul(pb[4 + m % 4][:, 0:512], wq2b3[n % 3][:, k16, :],
                                                    h1b[:, k, :], start=(k == 0), stop=(k == 31))
                            I(inst, "mT")
                    assert C["mT"] == f2T0 + 8

                    @blk.vector
                    def _(e, f2T0=f2T0, f2V0=f2V0, ln2V=ln2V):
                        for m in range(8):
                            e.wait_ge(S["mT"], f2T0 + m + 1)
                            W(e, K_ALL)
                            ev = I(e.tensor_scalar(attn_s[:, m, :], pb[4 + m % 4][:, 0:512],
                                                   b2c[:, m:m + 1], None, ALU.add), "mV")
                            W(e, ev)
                            I(e.tensor_copy(castb[:, m, :], attn_s[:, m, :]), "mP")
                    assert C["mV"] == f2V0 + 8
                    ffoV = tok("mV")
                    fcpV = tok("mP")
                    tok_pb67 = ffoV
                    tok_pb45 = ffoV
                    tok_wq2b = ("mT", f2T0 + 8)
                    if dbg == "ffn0" and j == 0:
                        dbg_tap("ffn0", attn_s[:].rearrange("p a b -> p (a b)"), [ffoV])

                    x2V = ln_fm(attn_s, castb, None, 8, [(0, 256), (256, 256)],
                                (ffoV, fcpV, tok_pb01, tok_pb23),
                                res_out=attn_s, res_in=x1b)
                    tok_castb = x2V
                    tok_pb01 = x2V
                    tok_pb23 = x2V
                    tok_lnx_read = None

                    @blk.scalar
                    def _(e, j=j, x2V=x2V):
                        W(e, x2V)
                        e.dma_start(out=out_ft.ap().rearrange("(a f) t -> f a t", f=128)
                                    [:, :, j * BS:(j + 1) * BS],
                                    in_=attn_s[:]).then_inc(S["dO"], 16)
                        C["dO"] += 16

            @blk.gpsimd
            def _(e):
                W(e, ("dO", C["dO"]))
                if dbg:
                    W(e, ("dDBG", C["dDBG"]))
    return nc



def _host_prepare(inputs):
    W1p = np.ascontiguousarray(np.asarray(inputs["L0_attn_W_in1"], np.float32))
    w1x = np.ascontiguousarray(W1p.T[:, 2048:4256])
    w1x_t = np.zeros((18, 8, 128, 128), np.float32)
    for m in range(18):
        cwd = min(128, 2208 - m * 128)
        w1x_t[m, :, :, :cwd] = w1x[:, m * 128:m * 128 + cwd].reshape(8, 128, cwd)
    convw_t = np.ascontiguousarray(
        np.asarray(inputs["L0_attn_conv_w"], np.float32).transpose(2, 1, 0))
    w2 = np.ascontiguousarray(np.asarray(inputs["L0_attn_W_in2"], np.float32).T)
    w2_t = np.zeros((16, 8, 128, 128), np.float32)
    for m in range(16):
        w2_t[m] = w2[:, m * 128:(m + 1) * 128].reshape(8, 128, 128)
    shared = dict(
        w1x_t=w1x_t, convw_t=convw_t, w2_t=w2_t,
        w3_t=np.ascontiguousarray(np.asarray(inputs["L0_attn_W_in3"], np.float32).T),
        fw1_t=np.ascontiguousarray(np.asarray(inputs["L0_ffn_W1"], np.float32).T),
        fw2_t=np.ascontiguousarray(np.asarray(inputs["L0_ffn_W2"], np.float32).T),
        cb_row=np.asarray(inputs["L0_attn_conv_b"], np.float32).reshape(1, CH),
        b1_pm=np.ascontiguousarray(np.asarray(inputs["L0_ffn_b1"], np.float32).reshape(32, 128).T),
        b2_pm=np.ascontiguousarray(np.asarray(inputs["L0_ffn_b2"], np.float32).reshape(8, 128).T),
        lng_pm=np.ascontiguousarray(np.asarray(inputs["L0_attn_ln_g"], np.float32).reshape(16, 128).T),
        lnb_pm=np.ascontiguousarray(np.asarray(inputs["L0_attn_ln_b"], np.float32).reshape(16, 128).T),
        alog_c=np.asarray(inputs["L0_attn_A_log"], np.float32).reshape(32, 1),
        prelu_pm=np.broadcast_to(np.asarray(inputs["L0_attn_prelu_a"], np.float32)
                                 .reshape(1, 1), (128, 1)).copy(),
        triu_h=np.triu(np.ones((128, 128), np.float32)),
        i128_h=np.eye(128, dtype=np.float32),
        i32_h=np.eye(32, dtype=np.float32),
        i64_h=np.eye(64, dtype=np.float32),
        ones_h=np.ones((128, 128), np.float32),
        onesc_h=np.ones((128, 1), np.float32),
    )
    shared = {k: np.ascontiguousarray(v, dtype=np.float32) for k, v in shared.items()}

    u = np.asarray(inputs["L0_attn_u"], np.float32)
    in_maps = []
    for c in range(NCORES):
        b, hhalf = c // 2, c % 2
        t0 = hhalf * 2048
        useg = np.zeros((TH, D), np.float32)
        useg[3:] = u[b, t0:t0 + 2048]
        if hhalf == 1:
            useg[:3] = u[b, t0 - 3:t0]
        sel = np.zeros((64, 8), np.float32)
        if hhalf == 1:
            sel[:, c - 1] = 1.0
        m = dict(shared)
        m["u_ft"] = np.ascontiguousarray(useg.T)
        m["sel_pm"] = sel
        in_maps.append(m)
    return in_maps


def kernel(**inputs):
    in_maps = _host_prepare(inputs)
    nc = build(DBG)
    res = run_bass_kernel_spmd(nc, in_maps, core_ids=list(range(NCORES)))
    u = np.asarray(inputs["L0_attn_u"])
    out = np.empty((4, 4096, D), np.float32)
    for c in range(NCORES):
        b, hhalf = c // 2, c % 2
        out[b, hhalf * 2048:(hhalf + 1) * 2048] = res.results[c]["out_ft"].T
    if DBG:
        np.save("/tmp/kdbg.npy", np.stack([res.results[c]["dbg"] for c in range(NCORES)]))
    return out


# version 21 (final)
# speedup vs baseline: 1.0538x; 1.0268x over previous
import os
import numpy as np
import ml_dtypes
import concourse.bass as bass
import concourse.mybir as mybir
from concourse.bass_utils import run_bass_kernel_spmd
from contextlib import ExitStack

F32 = mybir.dt.float32
BF16 = mybir.dt.bfloat16
I32 = mybir.dt.int32
AF = mybir.ActivationFunctionType
ALU = mybir.AluOpType

D, DI, NH, HD, DS, CH = 1024, 2048, 32, 64, 64, 2176
T, TH, Q, NCHK, BS, NBLK, FF = 2048, 2051, 128, 16, 512, 4, 4096
EPS = 1e-5
RM = 12582912.0
NCORES = 8

DBG = os.environ.get("KDBG", "")
DBG_SHAPES = {
    "ln1": (128, 8 * TH), "xbc": (128, 17 * TH), "dt": (32, TH), "AC": (32, T),
    "bctok": (128, NCHK * 128), "scoresM": (128, NCHK * 128),
    "sfin": (64, DI), "sinit": (64, DI), "xt0": (128, DI),
    "hsb0": (128, 16 * 128), "zn0": (128, 16 * 256),
    "attn0": (128, 8 * BS), "x1b0": (128, 8 * BS),
    "h1b0": (128, 32 * BS), "ffn0": (128, 8 * BS),
}


def build(dbg="", ncores=NCORES):
    nc = bass.Bass("TRN2", target_bir_lowering=False)

    u_ft = nc.dram_tensor("u_ft", [D, TH], F32, kind="ExternalInput")
    w1x_t = nc.dram_tensor("w1x_t", [18, 8, 128, 128], F32, kind="ExternalInput")
    convw_t = nc.dram_tensor("convw_t", [4, CH, CH], F32, kind="ExternalInput")
    w2_t = nc.dram_tensor("w2_t", [16, 8, 128, 128], F32, kind="ExternalInput")
    w3_t = nc.dram_tensor("w3_t", [DI, D], F32, kind="ExternalInput")
    fw1_t = nc.dram_tensor("fw1_t", [D, FF], F32, kind="ExternalInput")
    fw2_t = nc.dram_tensor("fw2_t", [FF, D], F32, kind="ExternalInput")
    cb_row = nc.dram_tensor("cb_row", [1, CH], F32, kind="ExternalInput")
    b1_pm = nc.dram_tensor("b1_pm", [128, 32], F32, kind="ExternalInput")
    b2_pm = nc.dram_tensor("b2_pm", [128, 8], F32, kind="ExternalInput")
    lng_pm = nc.dram_tensor("lng_pm", [128, 16], F32, kind="ExternalInput")
    lnb_pm = nc.dram_tensor("lnb_pm", [128, 16], F32, kind="ExternalInput")
    alog_c = nc.dram_tensor("alog_c", [32, 1], F32, kind="ExternalInput")
    prelu_pm = nc.dram_tensor("prelu_pm", [128, 1], F32, kind="ExternalInput")
    triu_h = nc.dram_tensor("triu_h", [128, 128], F32, kind="ExternalInput")
    i128_h = nc.dram_tensor("i128_h", [128, 128], F32, kind="ExternalInput")
    i32_h = nc.dram_tensor("i32_h", [32, 32], F32, kind="ExternalInput")
    i64_h = nc.dram_tensor("i64_h", [64, 64], F32, kind="ExternalInput")
    ones_h = nc.dram_tensor("ones_h", [128, 128], F32, kind="ExternalInput")
    onesc_h = nc.dram_tensor("onesc_h", [128, 1], F32, kind="ExternalInput")
    sel_pm = nc.dram_tensor("sel_pm", [64, 8], F32, kind="ExternalInput")
    out_ft = nc.dram_tensor("out_ft", [D, T], F32, kind="ExternalOutput")

    w3g_sp = nc.dram_tensor("w3g_sp", [8, 16, 128, 128], BF16)
    wq1_sp = nc.dram_tensor("wq1_sp", [32, 8, 128, 128], BF16)
    wq2_sp = nc.dram_tensor("wq2_sp", [8, 32, 128, 128], BF16)
    xt_sp = nc.dram_tensor("xt_sp", [NCHK, 128, DI], BF16)
    ag_in = nc.dram_tensor("ag_in", [64, DI], F32)
    if ncores % 2 == 0:
        ag_out = nc.dram_tensor("ag_out", [ncores * 64, DI], F32, addr_space="Shared")
    else:
        ag_out = nc.dram_tensor("ag_out", [ncores * 64, DI], F32)

    dbg_t = nc.dram_tensor("dbg", list(DBG_SHAPES[dbg]), F32, kind="ExternalOutput") if dbg else None

    with ExitStack() as st:
        E = st.enter_context
        semnames = ("mT", "mV", "mA", "mP",
                    "dK", "dKg", "dUa", "dUb", "dUga", "dUgb",
                    "dW3a", "dW3b", "dSPa", "dSPb", "dSQa", "dSQb", "dSPXa", "dSPXb",
                    "dIWa", "dIWb", "dQa", "dQb",
                    "dCWa", "dCWb", "dCWc", "dCWd", "dCWe", "dCWf", "fCW", "dF2c",
                    "dQc", "dQd", "dSQc", "dSQd",
                    "dGWc", "dGWd", "dAWc", "dAWd", "dF1c", "dF1d",
                    "dXTa", "dXTb",
                    "dGWa", "dGWb", "dAWa", "dAWb", "dF1a", "dF1b", "dF2a", "dF2b",
                    "dUBa", "dUBb", "dUCa", "dUCb", "dX", "cc", "dG2", "dO", "dDBG")
        S = {n: E(nc.semaphore(n)) for n in semnames}
        C = {n: 0 for n in semnames}

        def W(e, *deps):
            for dep in deps:
                if dep is None:
                    continue
                sname, val = dep
                if val > 0:
                    e.wait_ge(S[sname], val)

        def I(inst, sname, n=1):
            C[sname] += n
            inst.then_inc(S[sname], n)
            return (sname, C[sname])

        def tok(sname):
            return (sname, C[sname])

        pb = [E(nc.psum_tensor(f"pb{i}", [128, 512], F32)) for i in range(8)]

        triu_f = E(nc.sbuf_tensor("triu_f", [128, 128], F32))
        i128_f = E(nc.sbuf_tensor("i128_f", [128, 128], F32))
        i32_f = E(nc.sbuf_tensor("i32_f", [32, 32], F32))
        i64_b = E(nc.sbuf_tensor("i64_b", [64, 64], BF16))
        i32_b = E(nc.sbuf_tensor("i32_b", [32, 32], BF16))
        ones_f = E(nc.sbuf_tensor("ones_f", [128, 1], F32))
        ones_b = E(nc.sbuf_tensor("ones_b", [128, 128], BF16))
        cbrow_b = E(nc.sbuf_tensor("cbrow_b", [1, CH], BF16))
        b1c = E(nc.sbuf_tensor("b1c", [128, 32], F32))
        b2c = E(nc.sbuf_tensor("b2c", [128, 8], F32))
        lngc = E(nc.sbuf_tensor("lngc", [128, 16], F32))
        lnbc_b = E(nc.sbuf_tensor("lnbc_b", [128, 16], BF16))
        alogc = E(nc.sbuf_tensor("alogc", [32, 1], F32))
        preluc = E(nc.sbuf_tensor("preluc", [128, 1], F32))
        selc = E(nc.sbuf_tensor("selc", [64, 8], F32))
        aneg = E(nc.sbuf_tensor("aneg", [32, 1], F32))
        c1c = E(nc.sbuf_tensor("c1c", [128, 1], F32))
        c2c = E(nc.sbuf_tensor("c2c", [128, 1], F32))
        bias3 = E(nc.sbuf_tensor("bias3", [128, 8], F32))
        gam1 = E(nc.sbuf_tensor("gam1", [128, 1], F32))
        gam2 = E(nc.sbuf_tensor("gam2", [128, 1], F32))
        rowsc = E(nc.sbuf_tensor("rowsc", [1, 1032], F32))
        sqs = E(nc.sbuf_tensor("sqs", [128, 2048], BF16))
        t1s = E(nc.sbuf_tensor("t1s", [128, 2048], F32))
        w3buf = [E(nc.sbuf_tensor(f"w3buf{i}", [128, D], BF16)) for i in range(3)]
        w3gb = [E(nc.sbuf_tensor(f"w3gb{i}", [128, D], BF16)) for i in range(2)]

        blk = E(nc.Block())

        actset_n = [0]

        def act_set(e, sid):
            actset_n[0] += 1
            e.add_instruction(mybir.InstLoadActFuncSet(
                name=f"actset{sid}_{actset_n[0]}", act_func_set_id=sid, ins=[], outs=[]))

        def dbg_tap(name, src_ap, wait_toks):
            if dbg != name:
                return

            @blk.gpsimd
            def _(e):
                W(e, *wait_toks)
                e.dma_start(out=dbg_t.ap(), in_=src_ap).then_inc(S["dDBG"], 16)
                C["dDBG"] += 16

        @blk.sync
        def _(e):
            for dst, src in ((triu_f, triu_h.ap()), (i128_f, i128_h.ap()),
                             (i32_f, i32_h.ap()), (ones_f, onesc_h.ap()),
                             (b1c, b1_pm.ap()), (b2c, b2_pm.ap()),
                             (lngc, lng_pm.ap()),
                             (alogc, alog_c.ap()), (preluc, prelu_pm.ap()),
                             (selc, sel_pm.ap())):
                e.dma_start(out=dst[:], in_=src).then_inc(S["dK"], 16)
                C["dK"] += 16

        @blk.gpsimd
        def _(e):
            for dst, src in ((i64_b, i64_h.ap()), (i32_b, i32_h.ap()),
                             (ones_b, ones_h.ap()), (cbrow_b, cb_row.ap()),
                             (lnbc_b, lnb_pm.ap())):
                e.dma_start(out=dst[:], in_=src).then_inc(S["dKg"], 16)
                C["dKg"] += 16
        K_ALL, KG_ALL = tok("dK"), tok("dKg")

        @blk.scalar
        def _(e):
            act_set(e, 6)
            W(e, K_ALL)
            I(e.activation(aneg[:], alogc[:], AF.Exp), "mA")
        anegA = tok("mA")

        @blk.vector
        def _(e):
            W(e, anegA)
            e.tensor_scalar(aneg[:], aneg[:], -1.0, None, ALU.mult)
            e.tensor_scalar(c1c[:], preluc[:], 1.0, 0.5, ALU.add, ALU.mult)
            I(e.tensor_scalar(c2c[:], preluc[:], -0.5, 0.5, ALU.mult, ALU.add), "mV")
        P0V = tok("mV")

        def ln_fm(x_f32, x_b16, out_b16, nk, widths, deps, res_out=None, res_in=None,
                  copy_out=None, copy_gate=None):
            sqv = sqs[:].rearrange("p (k w) -> p k w", k=nk)
            t1v = t1s[:].rearrange("p (k w) -> p k w", k=nk)
            nf = float(nk * 128)
            NW = len(widths)
            assert NW <= 2
            sqT = [None] * NW
            tsT = [None] * NW
            vrT = [None] * NW
            exT = [None] * NW
            bcT = [None] * NW
            mulT = [None] * NW
            cpT = [None] * NW

            stT = [None] * NW
            for i, (c0, wdt) in enumerate(widths):
                @blk.vector
                def _(e, i=i, c0=c0, wdt=wdt):
                    W(e, *deps)
                    if i >= 1:
                        W(e, stT[i - 1])
                    inst = None
                    for k in range(nk):
                        inst = e.tensor_tensor(sqv[:, k, 0:wdt], x_b16[:, k, c0:c0 + wdt],
                                               x_b16[:, k, c0:c0 + wdt], ALU.mult)
                    I(inst, "mV")
                sqT[i] = tok("mV")

                b0 = 2 * i

                @blk.tensor
                def _(e, i=i, c0=c0, wdt=wdt, b0=b0):
                    W(e, *deps, sqT[i], KG_ALL)
                    inst = None
                    for k in range(nk):
                        e.matmul(pb[b0][0:1, 0:wdt], ones_b[:, 0:1], x_b16[:, k, c0:c0 + wdt],
                                 start=(k == 0), stop=(k == nk - 1))
                    for k in range(nk):
                        inst = e.matmul(pb[b0 + 1][0:1, 0:wdt], ones_b[:, 0:1],
                                        sqv[:, k, 0:wdt],
                                        start=(k == 0), stop=(k == nk - 1))
                    I(inst, "mT")
                stT[i] = tok("mT")

            for i, (c0, wdt) in enumerate(widths):
                m0 = i * 256
                v0 = 512 + i * 256
                b0 = 2 * i

                @blk.vector
                def _(e, i=i, wdt=wdt, m0=m0, v0=v0, b0=b0):
                    W(e, stT[i])
                    t0 = I(e.tensor_scalar(rowsc[0:1, m0:m0 + wdt], pb[b0][0:1, 0:wdt],
                                           1.0 / nf, None, ALU.mult), "mV")
                    e.tensor_scalar(rowsc[0:1, v0:v0 + wdt], pb[b0 + 1][0:1, 0:wdt], 1.0 / nf, EPS,
                                    ALU.mult, ALU.add)
                    W(e, t0)
                    t1 = I(e.tensor_tensor(t1s[0:1, m0:m0 + wdt], rowsc[0:1, m0:m0 + wdt],
                                           rowsc[0:1, m0:m0 + wdt], ALU.mult), "mV")
                    W(e, t1)
                    I(e.tensor_tensor(rowsc[0:1, v0:v0 + wdt], rowsc[0:1, v0:v0 + wdt],
                                      t1s[0:1, m0:m0 + wdt], ALU.subtract), "mV")
                vrT[i] = tok("mV")
                tsT[i] = vrT[i]

            for i, (c0, wdt) in enumerate(widths):
                v0 = 512 + i * 256

                @blk.scalar
                def _(e, i=i, wdt=wdt, v0=v0):
                    W(e, vrT[i])
                    I(e.activation(rowsc[0:1, v0:v0 + wdt], rowsc[0:1, v0:v0 + wdt], AF.Ln), "mA")
                lnT = tok("mA")

                @blk.scalar
                def _(e, i=i, wdt=wdt, v0=v0, lnT=lnT):
                    W(e, lnT)
                    I(e.activation(rowsc[0:1, v0:v0 + wdt], rowsc[0:1, v0:v0 + wdt],
                                   AF.Exp, scale=-0.5), "mA")
                exT[i] = tok("mA")

            for i, (c0, wdt) in enumerate(widths):
                m0 = i * 256
                v0 = 512 + i * 256
                bb = 2 * ((i + 1) % NW) if NW == 2 else 2

                @blk.tensor
                def _(e, i=i, wdt=wdt, m0=m0, v0=v0, bb=bb):
                    W(e, exT[i], tsT[(i + 1) % NW] if NW == 2 else tsT[i], K_ALL)
                    e.matmul(pb[bb][:, 0:wdt], triu_f[0:1, :], rowsc[0:1, m0:m0 + wdt],
                             start=True, stop=True)
                    I(e.matmul(pb[bb + 1][:, 0:wdt], triu_f[0:1, :], rowsc[0:1, v0:v0 + wdt],
                               start=True, stop=True), "mT")
                bcT[i] = tok("mT")

            out_tok = None
            for i, (c0, wdt) in enumerate(widths):
                bb = 2 * ((i + 1) % NW) if NW == 2 else 2

                @blk.vector
                def _(e, i=i, c0=c0, wdt=wdt, bb=bb):
                    W(e, bcT[i])
                    if i >= 1:
                        W(e, mulT[i - 1])
                    inst = None
                    for k in range(nk):
                        inst = e.tensor_tensor(t1v[:, k, 0:wdt], x_f32[:, k, c0:c0 + wdt],
                                               pb[bb][:, 0:wdt], ALU.subtract)
                    I(inst, "mV")
                subT = tok("mV")

                @blk.vector
                def _(e, i=i, c0=c0, wdt=wdt, bb=bb, subT=subT):
                    W(e, subT)
                    inst = None
                    for k in range(nk):
                        if res_out is None:
                            inst = e.tensor_tensor(out_b16[:, k, c0:c0 + wdt], t1v[:, k, 0:wdt],
                                                   pb[bb + 1][:, 0:wdt], ALU.mult)
                        else:
                            inst = e.tensor_tensor(t1v[:, k, 0:wdt], t1v[:, k, 0:wdt],
                                                   pb[bb + 1][:, 0:wdt], ALU.mult)
                    I(inst, "mV")
                mulT[i] = tok("mV")

                if res_out is not None:
                    @blk.vector
                    def _(e, i=i, c0=c0, wdt=wdt):
                        W(e, mulT[i])
                        inst = None
                        for k in range(nk):
                            inst = e.tensor_tensor(res_out[:, k, c0:c0 + wdt], t1v[:, k, 0:wdt],
                                                   res_in[:, k, c0:c0 + wdt], ALU.add)
                        I(inst, "mV")
                    mulT[i] = tok("mV")
                if copy_out is not None:
                    @blk.vector
                    def _(e, i=i, c0=c0, wdt=wdt):
                        W(e, mulT[i], copy_gate)
                        inst = None
                        for k in range(nk):
                            inst = e.tensor_copy(copy_out[:, k, c0:c0 + wdt],
                                                 res_out[:, k, c0:c0 + wdt])
                        I(inst, "mP")
                    cpT[i] = tok("mP")
                out_tok = mulT[i]
            if copy_out is not None:
                return out_tok, cpT[NW - 1]
            return out_tok

        v0 = C["mV"]
        b3T0 = C["mT"]

        @blk.gpsimd
        def _(e):
            for k in range(16):
                if k >= 2:
                    e.wait_ge(S["mV"], v0 + (k - 1))
                    e.wait_ge(S["mT"], b3T0 + (k - 1))
                par = "dW3a" if k % 2 == 0 else "dW3b"
                e.dma_start(out=w3buf[k % 2][:],
                            in_=w3_t.ap()[k * 128:(k + 1) * 128, :]).then_inc(S[par], 16)
                C[par] += 16

        @blk.tensor
        def _(e):
            inst = None
            for k in range(16):
                par = "dW3a" if k % 2 == 0 else "dW3b"
                e.wait_ge(S[par], 16 * (k // 2 + 1))
                W(e, KG_ALL)
                for m in range(8):
                    inst = e.matmul(pb[m][:, 0:1], w3buf[k % 2][:, m * 128:(m + 1) * 128],
                                    lnbc_b[:, k:k + 1], start=(k == 0), stop=(k == 15))
                I(inst, "mT")
        assert C["mT"] == b3T0 + 16
        bias3T = tok("mT")

        @blk.vector
        def _(e):
            for k in range(16):
                par = "dW3a" if k % 2 == 0 else "dW3b"
                e.wait_ge(S[par], 16 * (k // 2 + 1))
                if k >= 2:
                    e.wait_ge(S["dSPa" if k % 2 == 0 else "dSPb"], 16 * (k // 2))
                W(e, K_ALL)
                I(e.tensor_scalar(w3gb[k % 2][:], w3buf[k % 2][:], lngc[:, k:k + 1], None, ALU.mult), "mV")
            W(e, bias3T)
            inst = None
            for m in range(8):
                inst = e.tensor_copy(bias3[:, m:m + 1], pb[m][:, 0:1])
            I(inst, "mV")
        bias3V = tok("mV")

        @blk.sync
        def _(e):
            for k in range(16):
                e.wait_ge(S["mV"], v0 + k + 1)
                par = "dSPa" if k % 2 == 0 else "dSPb"
                e.dma_start(out=w3g_sp.ap()[:, k].rearrange("m f c -> f m c"),
                            in_=w3gb[k % 2][:].rearrange("f (m c) -> f m c", m=8)).then_inc(S[par], 16)
                C[par] += 16
        w3spD = (("dSPa", C["dSPa"]), ("dSPb", C["dSPb"]))

        uview = u_ft.ap().rearrange("(a f) t -> f a t", f=128)

        with ExitStack() as s1:
            E1 = s1.enter_context
            xbc = E1(nc.sbuf_tensor("xbc", [128, 17, TH], BF16, side="right"))
            wip = [E1(nc.sbuf_tensor(f"wip{i}", [128, 8, 128], BF16, side="right")) for i in range(2)]
            lamr_f = E1(nc.sbuf_tensor("lamr_f", [1, NCHK * 32], F32, side="right"))
            lamr_b = E1(nc.sbuf_tensor("lamr_b", [1, NCHK * 32], BF16, side="right"))

            with ExitStack() as sD:
                ED = sD.enter_context
                dtr = ED(nc.sbuf_tensor("dtr", [32, TH], F32, side="right"))
                ATt = ED(nc.sbuf_tensor("ATt", [32, T], F32, side="right"))
                ACt = ED(nc.sbuf_tensor("ACt", [32, T], F32, side="right"))
                sp_sb = ED(nc.sbuf_tensor("sp_sb", [32, 2 * T], F32, side="right"))

                with ExitStack() as sL:
                    EL = sL.enter_context
                    ln1 = EL(nc.sbuf_tensor("ln1", [128, 8, TH], BF16, side="right"))
                    ubuf = [EL(nc.sbuf_tensor(f"ubuf{i}", [128, 8, 256], F32, side="right")) for i in range(2)]
                    ubbl = [EL(nc.sbuf_tensor(f"ubbl{i}", [128, 8, 256], BF16, side="right")) for i in range(2)]

                    UB = [(c, 256) for c in range(0, 2048, 256)] + [(2048, 3)]
                    lnToks = []
                    for wi, (c0, wdt) in enumerate(UB):
                        pa = "dUa" if wi % 2 == 0 else "dUb"
                        pg = "dUga" if wi % 2 == 0 else "dUgb"

                        @blk.sync
                        def _(e, wi=wi, c0=c0, wdt=wdt, pa=pa):
                            if wi >= 2:
                                W(e, lnToks[wi - 2])
                            e.dma_start(out=ubuf[wi % 2][:, :, 0:wdt],
                                        in_=uview[:, :, c0:c0 + wdt]).then_inc(S[pa], 16)
                            C[pa] += 16
                        uTok = tok(pa)

                        @blk.gpsimd
                        def _(e, wi=wi, c0=c0, wdt=wdt, pg=pg):
                            if wi >= 2:
                                W(e, lnToks[wi - 2])
                            e.dma_start(out=ubbl[wi % 2][:, :, 0:wdt],
                                        in_=uview[:, :, c0:c0 + wdt]).then_inc(S[pg], 16)
                            C[pg] += 16
                        ugTok = tok(pg)

                        lt = ln_fm(ubuf[wi % 2], ubbl[wi % 2],
                                   ln1[:, :, c0:c0 + wdt], 8, [(0, wdt)],
                                   (uTok, ugTok, P0V, bias3V))
                        lnToks.append(lt)
                    ln1V = lnToks[-1]
                    dbg_tap("ln1", ln1[:].rearrange("p a t -> p (a t)"), [ln1V])

                    IPB = [(0, 512, 0), (512, 512, 1), (1024, 512, 2), (1536, 512, 3), (2048, 3, 4)]
                    ipT0 = C["mT"]
                    vIP0 = C["mV"]

                    @blk.gpsimd
                    def _(e):
                        for m in range(18):
                            if m >= 2:
                                e.wait_ge(S["mT"], ipT0 + (m - 1))
                            par = "dIWa" if m % 2 == 0 else "dIWb"
                            e.dma_start(out=wip[m % 2][:],
                                        in_=w1x_t.ap()[m].rearrange("k f c -> f k c")).then_inc(S[par], 16)
                            C[par] += 16

                    @blk.tensor
                    def _(e):
                        W(e, ln1V)
                        for m in range(18):
                            par = "dIWa" if m % 2 == 0 else "dIWb"
                            e.wait_ge(S[par], 16 * (m // 2 + 1))
                            if m >= 1:
                                e.wait_ge(S["mV"], vIP0 + m)
                            inst = None
                            for k in range(8):
                                for (c0, wdt, bank) in IPB:
                                    inst = e.matmul(pb[bank][:, 0:wdt], wip[m % 2][:, k, :],
                                                    ln1[:, k, c0:c0 + wdt], start=(k == 0), stop=(k == 7))
                            I(inst, "mT")
                    assert C["mT"] == ipT0 + 18
                    ipT = tok("mT")

                    @blk.vector
                    def _(e):
                        for m in range(18):
                            e.wait_ge(S["mT"], ipT0 + m + 1)
                            inst = None
                            for (c0, wdt, bank) in IPB:
                                if m < 17:
                                    inst = e.tensor_copy(xbc[:, m, c0:c0 + wdt], pb[bank][:, 0:wdt])
                                else:
                                    inst = e.tensor_copy(dtr[0:32, c0:c0 + wdt], pb[bank][0:32, 0:wdt])
                            I(inst, "mV")
                    assert C["mV"] == vIP0 + 18
                    ipV = tok("mV")
                    dbg_tap("xbc", xbc[:].rearrange("p m t -> p (m t)"), [ipV])
                    dbg_tap("dt", dtr[:], [ipV])

                bctok = E(nc.sbuf_tensor("bctok", [128, NCHK, 128], F32))
                bfeat = E(nc.sbuf_tensor("bfeat", [64, NCHK, 128], BF16))
                cfeat = E(nc.sbuf_tensor("cfeat", [64, NCHK, 128], BF16))
                scoresM = E(nc.sbuf_tensor("scoresM", [128, NCHK, 128], BF16))
                aclt = E(nc.sbuf_tensor("aclt", [32, NCHK, 128], F32))
                eac = E(nc.sbuf_tensor("eac", [32, NCHK, 128], BF16))
                actok = E(nc.sbuf_tensor("actok", [128, NCHK, 32], F32))
                dtet = E(nc.sbuf_tensor("dtet", [128, NCHK, 32], F32))
                dtegt = E(nc.sbuf_tensor("dtegt", [128, NCHK, 32], F32))
                lam64 = E(nc.sbuf_tensor("lam64", [64, NCHK * 32], BF16))

                @blk.scalar
                def _(e):
                    W(e, ipV, P0V)
                    I(e.activation(sp_sb[:, 0:T], dtr[:, 3:3 + T], AF.Abs), "mA")
                absV = tok("mA")

                @blk.scalar
                def _(e):
                    W(e, absV)
                    I(e.activation(sp_sb[:, T:2 * T], sp_sb[:, 0:T], AF.Exp, scale=-1.0), "mA")
                spE = tok("mA")

                @blk.vector
                def _(e):
                    W(e, spE)
                    I(e.tensor_scalar(sp_sb[:, T:2 * T], sp_sb[:, T:2 * T], 1.0, None, ALU.add), "mV")
                spP1 = tok("mV")

                @blk.scalar
                def _(e):
                    W(e, spP1)
                    I(e.activation(sp_sb[:, T:2 * T], sp_sb[:, T:2 * T], AF.Ln), "mA")
                spA = tok("mA")

                @blk.vector
                def _(e):
                    W(e, spA)
                    t0 = I(e.tensor_scalar(sp_sb[:, 0:T], dtr[:, 3:3 + T], 0.0, None, ALU.max), "mV")
                    W(e, t0)
                    t1 = I(e.tensor_tensor(ATt[:], sp_sb[:, 0:T], sp_sb[:, T:2 * T], ALU.add), "mV")
                    W(e, t1)
                    t2 = I(e.tensor_scalar(sp_sb[:, 0:T], ATt[:], aneg[:, 0:1], None, ALU.mult), "mV")
                    W(e, t2)
                    I(e.tensor_tensor_scan(ACt[:], sp_sb[:, 0:T], sp_sb[:, 0:T], 0.0, ALU.add, ALU.bypass), "mV")
                acV = tok("mV")
                dbg_tap("AC", ACt[:], [acV])

                @blk.vector
                def _(e):
                    W(e, acV)
                    inst = None
                    for c in range(NCHK):
                        if c == 0:
                            inst = e.tensor_copy(aclt[:, 0, :], ACt[:, 0:128])
                        else:
                            inst = e.tensor_scalar(aclt[:, c, :], ACt[:, c * 128:(c + 1) * 128],
                                                   ACt[:, c * 128 - 1:c * 128], None, ALU.subtract)
                    inst = e.tensor_scalar(sp_sb[:, 0:T], ACt[:], ACt[:, T - 1:T], -1.0,
                                           ALU.subtract, ALU.mult)
                    I(inst, "mV")
                aclV = tok("mV")

                @blk.scalar
                def _(e):
                    W(e, aclV)
                    inst = None
                    for c in range(NCHK):
                        inst = e.activation(eac[:, c, :], aclt[:, c, :], AF.Exp)
                    inst = e.activation(sp_sb[:, T:2 * T], sp_sb[:, 0:T], AF.Exp)
                    I(inst, "mA")
                eacA = tok("mA")

                prevCp = None
                for c in range(NCHK):
                    @blk.vector
                    def _(e, c=c, pv=prevCp):
                        W(e, eacA)
                        if pv is not None:
                            W(e, pv)
                        I(e.tensor_scalar(sp_sb[:, 0:128], aclt[:, c, :],
                                          aclt[:, c, 127:128], -1.0,
                                          ALU.subtract, ALU.mult), "mV")
                    subV = tok("mV")

                    @blk.scalar
                    def _(e, subV=subV):
                        W(e, subV)
                        I(e.activation(sp_sb[:, 128:256], sp_sb[:, 0:128], AF.Exp), "mA")
                    dteA = tok("mA")

                    @blk.tensor
                    def _(e, c=c, dteA=dteA, pv=prevCp):
                        W(e, dteA, eacA, K_ALL)
                        if pv is not None:
                            W(e, pv)
                        e.transpose(pb[5][0:128, 0:32], aclt[:, c, :], i32_f[:])
                        e.transpose(pb[5][0:128, 32:64], sp_sb[:, 128:256], i32_f[:])
                        e.transpose(pb[5][0:128, 64:96], sp_sb[:, T + c * 128:T + (c + 1) * 128], i32_f[:])
                        I(e.transpose(pb[5][0:1, 96:128], aclt[:, c, 127:128], i32_f[:]), "mT")
                    trT = tok("mT")

                    @blk.vector
                    def _(e, c=c, trT=trT):
                        W(e, trT)
                        e.tensor_copy(actok[:, c, :], pb[5][0:128, 0:32])
                        e.tensor_copy(dtet[:, c, :], pb[5][0:128, 32:64])
                        e.tensor_copy(dtegt[:, c, :], pb[5][0:128, 64:96])
                        I(e.tensor_copy(lamr_f[0:1, c * 32:(c + 1) * 32], pb[5][0:1, 96:128]), "mV")
                    prevCp = tok("mV")
                prepV = prevCp

                @blk.scalar
                def _(e):
                    W(e, prepV)
                    I(e.activation(lamr_f[:], lamr_f[:], AF.Exp), "mA")
                lamA = tok("mA")

                @blk.vector
                def _(e):
                    W(e, lamA)
                    I(e.tensor_copy(lamr_b[:], lamr_f[:]), "mV")
                lamV = tok("mV")

                @blk.tensor
                def _(e):
                    W(e, lamV, KG_ALL)
                    inst = None
                    for c in range(NCHK):
                        inst = e.matmul(pb[6][0:64, c * 32:(c + 1) * 32], ones_b[0:1, 0:64],
                                        lamr_b[0:1, c * 32:(c + 1) * 32], start=True, stop=True)
                    I(inst, "mT")
                lamT = tok("mT")

                @blk.vector
                def _(e):
                    W(e, lamT)
                    I(e.tensor_copy(lam64[:], pb[6][0:64, 0:512]), "mV")
                lamVV = tok("mV")

            with ExitStack() as s2:
                E2 = s2.enter_context
                qbuf = [E2(nc.sbuf_tensor(f"qbuf{i}", [128, 512], F32, side="right")) for i in range(4)]
                qacc = E2(nc.sbuf_tensor("qacc", [128, 132], F32, side="right"))
                qw1p = [E2(nc.sbuf_tensor(f"qw1p{i}", [128, 512], F32, side="right")) for i in range(2)]
                qi1p = [E2(nc.sbuf_tensor(f"qi1p{i}", [128, 512], I32, side="right")) for i in range(2)]
                qo1p = [E2(nc.sbuf_tensor(f"qo1p{i}", [128, 512], BF16, side="right")) for i in range(4)]
                qw1 = qw1p[0]
                cw = [E2(nc.sbuf_tensor(f"cw{i}", [128, 512], BF16, side="right")) for i in range(6)]
                cst = E2(nc.sbuf_tensor("cst", [128, 8, 512], BF16, side="right"))

                QT = ([("w1", k, p, fw1_t.ap()[k * 128:(k + 1) * 128, p * 512:(p + 1) * 512])
                       for k in range(8) for p in range(8)] +
                      [("w2", k, p, fw2_t.ap()[k * 128:(k + 1) * 128, p * 512:(p + 1) * 512])
                       for k in range(32) for p in range(2)])
                NQ = len(QT)

                aQ0 = C["mA"]

                @blk.sync
                def _(e):
                    for i in range(NQ):
                        if i < 2:
                            W(e, ipT, prepV, lamVV)
                        else:
                            e.wait_ge(S["mA"], aQ0 + (i - 1))
                        par = "dQa" if i % 2 == 0 else "dQb"
                        e.dma_start(out=qbuf[i % 2][:], in_=QT[i][3]).then_inc(S[par], 16)
                        C[par] += 16
                qaTotA, qaTotB = C["dQa"], C["dQb"]

                @blk.scalar
                def _(e):
                    for i in range(NQ):
                        par = "dQa" if i % 2 == 0 else "dQb"
                        e.wait_ge(S[par], 16 * (i // 2 + 1))
                        if i >= 1:
                            e.wait_ge(S["mA"], aQ0 + i)
                        I(e.activation(qw1[:], qbuf[i % 2][:], AF.Abs,
                                       accum_out=qacc[:, i:i + 1]), "mA")
                assert C["mA"] == aQ0 + NQ
                qaccA = tok("mA")

                OBLK = [(0, 512), (512, 512), (1024, 512), (1536, 512), (2048, 128)]
                KI = [(kk, ii) for kk in range(4) for ii in range(17)]
                convT0 = C["mT"]
                convV0 = C["mV"]


                @blk.gpsimd
                def _(e):
                    n = 0
                    gi = 0
                    for (o0, ow) in OBLK:
                        for tg in range(2):
                            si = gi - 1
                            if 0 <= si < 8:
                                e.wait_ge(S["mV"], convV0 + 4 * (si + 1))
                                spar = "dSPXa" if si % 2 == 0 else "dSPXb"
                                so0 = OBLK[si // 2][0]
                                stg = si % 2
                                dst = xt_sp.ap()[stg * 8:(stg + 1) * 8, :, so0:so0 + 512] \
                                    .rearrange("c p o -> p c o")
                                e.dma_start(out=dst, in_=cst[:, :, 0:512]).then_inc(S[spar], 16)
                                C[spar] += 16
                            for (kk, ii) in KI:
                                if n < 6:
                                    W(e, ipV, prepV, lamVV)
                                else:
                                    e.wait_ge(S["fCW"], n - 5)
                                par = ("dCWa", "dCWb", "dCWc", "dCWd", "dCWe", "dCWf")[n % 6]
                                e.dma_start(out=cw[n % 6][:, 0:ow],
                                            in_=convw_t.ap()[kk, ii * 128:(ii + 1) * 128,
                                                             o0:o0 + ow]).then_inc(S[par], 16)
                                C[par] += 16
                                n += 1
                            gi += 1
                xtspD = (("dSPXa", C["dSPXa"]), ("dSPXb", C["dSPXb"]))

                @blk.tensor
                def _(e):
                    W(e, ipV, KG_ALL)
                    n = 0
                    gi = 0
                    for (o0, ow) in OBLK:
                        for tg in range(2):
                            if gi == 3:
                                e.wait_ge(S["mP"], 1)
                                e.wait_ge(S["mV"], convV0 + 4 * 3)
                                e.matmul(pb[7][0:1, 0:1], ones_f[:, 0:1], qacc[:, 128:129],
                                         start=True, stop=True).then_inc(S["mP"], 1)
                                e.matmul(pb[7][0:1, 1:2], ones_f[:, 0:1], qacc[:, 129:130],
                                         start=True, stop=True).then_inc(S["mP"], 1)
                                e.wait_ge(S["mP"], 6)
                                e.matmul(pb[7][:, 2:3], triu_f[0:1, :], rowsc[0:1, 1024:1025],
                                         start=True, stop=True).then_inc(S["mP"], 1)
                                e.matmul(pb[7][:, 3:4], triu_f[0:1, :], rowsc[0:1, 1025:1026],
                                         start=True, stop=True).then_inc(S["mP"], 1)
                                e.wait_ge(S["mP"], 10)
                            if gi >= 1:
                                e.wait_ge(S["mV"], convV0 + 4 * gi)
                            inst = None
                            for ki, (kk, ii) in enumerate(KI):
                                par = ("dCWa", "dCWb", "dCWc", "dCWd", "dCWe", "dCWf")[n % 6]
                                e.wait_ge(S[par], 16 * (n // 6 + 1))
                                for tc in range(8):
                                    tci = tg * 8 + tc
                                    tcol = tci * 128 + kk
                                    inst = e.matmul(pb[tc][:, 0:ow], xbc[:, ii, tcol:tcol + 128],
                                                    cw[n % 6][:, 0:ow], start=(ki == 0), stop=False)
                                I(inst, "fCW")
                                n += 1
                            for tc in range(8):
                                inst = e.matmul(pb[tc][:, 0:ow], ones_b[0:1, :],
                                                cbrow_b[0:1, o0:o0 + ow], start=False, stop=True)
                            I(inst, "mT")
                            gi += 1

                @blk.scalar
                def _(e):
                    e.wait_ge(S["mP"], 4)
                    e.activation(rowsc[0:1, 1024:1026], rowsc[0:1, 1024:1026],
                                 AF.Ln).then_inc(S["mP"], 1)
                    e.wait_ge(S["mP"], 5)
                    e.activation(rowsc[0:1, 1024:1026], rowsc[0:1, 1024:1026],
                                 AF.Exp, scale=-1.0).then_inc(S["mP"], 1)

                qd = {-2: 10, -1: 10}

                def emit_qtiles(e, i0, i1):
                    ids = list(range(i0, i1))
                    for p in range(0, len(ids), 2):
                        pair = ids[p:p + 2]
                        toks = {}
                        for step in range(4):
                            for t in pair:
                                b = t % 2
                                if step == 0:
                                    par = ("dQa", "dQb", "dQc", "dQd")[t % 4]
                                    base = (qaTotA, qaTotB, 0, 0)[t % 4]
                                    e.wait_ge(S[par], base + 16 * (t // 4 + 1))
                                    if t >= 4:
                                        e.wait_ge(S[("dSQa", "dSQb", "dSQc", "dSQd")[t % 4]],
                                                  16 * ((t - 4) // 4 + 1))
                                    e.wait_ge(S["mP"], qd[t - 2])
                                    g = gam1 if QT[t][0] == "w1" else gam2
                                    toks[t] = I(e.tensor_scalar(qw1p[b][:], qbuf[t % 4][:],
                                                                g[:, 0:1], RM,
                                                                ALU.mult, ALU.add), "mP")[1]
                                elif step == 1:
                                    e.wait_ge(S["mP"], toks[t])
                                    toks[t] = I(e.tensor_copy(qi1p[b][:], qw1p[b][:]), "mP")[1]
                                elif step == 2:
                                    e.wait_ge(S["mP"], toks[t])
                                    toks[t] = I(e.tensor_scalar(qw1p[b][:], qi1p[b][:],
                                                                -RM, 1.0,
                                                                ALU.add, ALU.min), "mP")[1]
                                else:
                                    e.wait_ge(S["mP"], toks[t])
                                    toks[t] = I(e.tensor_scalar(qo1p[t % 4][:], qw1p[b][:],
                                                                -1.0, None,
                                                                ALU.max), "mP")[1]
                                    qd[t] = toks[t]

                QBATCH = {4: (0, 22), 5: (22, 44), 6: (44, 66), 7: (66, 88),
                          8: (88, 108), 9: (108, 128)}

                @blk.vector
                def _(e):
                    gi = 0
                    prevStt = None
                    for obi, (o0, ow) in enumerate(OBLK):
                        for tg in range(2):
                            if gi == 3:
                                e.wait_ge(S["mP"], 3)
                                e.tensor_scalar(rowsc[0:1, 1024:1025], pb[7][0:1, 0:1],
                                                1.0 / (FF * D), EPS, ALU.mult, ALU.add)
                                e.tensor_scalar(rowsc[0:1, 1025:1026], pb[7][0:1, 1:2],
                                                1.0 / (FF * D), EPS,
                                                ALU.mult, ALU.add).then_inc(S["mP"], 1)
                                e.wait_ge(S["mP"], 8)
                                e.tensor_copy(gam1[:], pb[7][:, 2:3]).then_inc(S["mP"], 1)
                                e.tensor_copy(gam2[:], pb[7][:, 3:4]).then_inc(S["mP"], 1)
                                C["mP"] = 10
                            W(e, P0V, lamVV)
                            if 1 <= gi <= 8 and obi <= 4:
                                si = gi - 1
                                if si < 8:
                                    e.wait_ge(S["dSPXa" if si % 2 == 0 else "dSPXb"],
                                              16 * (si // 2 + 1))
                            for half in range(2):
                                e.wait_ge(S["mT"], convT0 + gi + 1)
                                W(e, prevStt)
                                inst = None
                                for tc2 in range(4):
                                    tc = half * 4 + tc2
                                    inst = e.scalar_tensor_tensor(
                                        t1s[:, tc2 * 512:tc2 * 512 + ow], pb[tc][:, 0:ow],
                                        0.0, preluc[:, 0:1].broadcast_to([128, ow]),
                                        ALU.min, ALU.mult)
                                negT = I(inst, "mV")
                                W(e, negT)
                                inst = None
                                for tc2 in range(4):
                                    tc = half * 4 + tc2
                                    tci = tg * 8 + tc
                                    if o0 < DI:
                                        dst = cst[:, tc, 0:ow]
                                    else:
                                        dst = bctok[:, tci, 0:ow]
                                    inst = e.scalar_tensor_tensor(dst, pb[tc][:, 0:ow], 0.0,
                                                                  t1s[:, tc2 * 512:tc2 * 512 + ow],
                                                                  ALU.max, ALU.add)
                                prevStt = I(inst, "mV")
                            if gi == 2:
                                W(e, qaccA)
                                e.tensor_reduce(qacc[:, 128:129], qacc[:, 0:64],
                                                mybir.AxisListType.X, ALU.add)
                                e.tensor_reduce(qacc[:, 129:130], qacc[:, 64:128],
                                                mybir.AxisListType.X, ALU.add).then_inc(S["mP"], 1)
                            if gi in QBATCH:
                                emit_qtiles(e, *QBATCH[gi])
                            gi += 1
                assert C["mV"] == convV0 + 4 * 10
                assert C["mP"] == 10 + 4 * NQ
                convV = tok("mV")
                qpassB = tok("mP")

                dbg_tap("bctok", bctok[:].rearrange("p c d -> p (c d)"), [convV])

                @blk.sync
                def _(e):
                    def spill(i):
                        e.wait_ge(S["mP"], qd[i])
                        wname, k, p = QT[i][0], QT[i][1], QT[i][2]
                        if wname == "w1":
                            dst = wq1_sp.ap()[4 * p:4 * p + 4, k].rearrange("m f c -> f m c")
                        else:
                            dst = wq2_sp.ap()[4 * p:4 * p + 4, k].rearrange("m f c -> f m c")
                        par = ("dSQa", "dSQb", "dSQc", "dSQd")[i % 4]
                        e.dma_start(out=dst,
                                    in_=qo1p[i % 4][:].rearrange("f (m c) -> f m c",
                                                                 m=4)).then_inc(S[par], 16)
                        C[par] += 16
                    W(e, qaccA)
                    for i in range(NQ):
                        if i >= 4:
                            spill(i - 4)
                        par = ("dQa", "dQb", "dQc", "dQd")[i % 4]
                        e.dma_start(out=qbuf[i % 4][:], in_=QT[i][3]).then_inc(S[par], 16)
                        C[par] += 16
                    for i in range(NQ - 4, NQ):
                        spill(i)
                qspD = (("dSQa", C["dSQa"]), ("dSQb", C["dSQb"]),
                        ("dSQc", C["dSQc"]), ("dSQd", C["dSQd"]))

        Sa = E(nc.sbuf_tensor("Sa", [64, NH * HD], F32))
        Sab = E(nc.sbuf_tensor("Sab", [64, NH * HD], BF16))
        hbb = E(nc.sbuf_tensor("hbb", [128, 16, 128], BF16))
        xtkb = [E(nc.sbuf_tensor(f"xtkb{i}", [128, DI], BF16)) for i in range(2)]
        ubf = E(nc.sbuf_tensor("ubf", [128, 8, BS], F32))
        castb = E(nc.sbuf_tensor("castb", [128, 8, BS], BF16))
        lnx = E(nc.sbuf_tensor("lnx", [128, 8, BS], BF16))
        u2h0s = E(nc.sbuf_tensor("u2h0s", [128, 4, 256], F32))

        with ExitStack() as s3:
            E3 = s3.enter_context
            bga = E3(nc.sbuf_tensor("bga", [128, NH * HD], BF16, side="right"))
            agst = E3(nc.sbuf_tensor("agst", [64, DI], F32, side="right"))
            gth = E3(nc.sbuf_tensor("gth", [64, 4 * DI], F32, side="right"))

            prevT2 = [None, None]
            for c in range(NCHK):
                bk = 5 + c % 2

                @blk.tensor
                def _(e, c=c, bk=bk, pv=prevT2[c % 2]):
                    W(e, convV, K_ALL)
                    if pv is not None:
                        W(e, pv)
                    e.transpose(pb[bk][0:64, 0:128], bctok[:, c, 0:64], i128_f[:])
                    I(e.transpose(pb[bk][0:64, 128:256], bctok[:, c, 64:128],
                                  i128_f[:]), "mT")
                trT = tok("mT")

                @blk.vector
                def _(e, c=c, bk=bk, trT=trT):
                    W(e, trT)
                    e.tensor_copy(bfeat[:, c, :], pb[bk][0:64, 0:128])
                    I(e.tensor_copy(cfeat[:, c, :], pb[bk][0:64, 128:256]), "mV")
                prevT2[c % 2] = tok("mV")
            bcV = prevT2[1]

            prevS2 = [None, None]
            for c in range(NCHK):
                bk = 5 + c % 2

                @blk.tensor
                def _(e, c=c, bk=bk, pv=prevS2[c % 2]):
                    W(e, bcV)
                    if pv is not None:
                        W(e, pv)
                    I(e.matmul(pb[bk][:, 0:128], bfeat[:, c, :], cfeat[:, c, :],
                               start=True, stop=True), "mT")
                scT = tok("mT")

                @blk.vector
                def _(e, c=c, bk=bk, scT=scT):
                    W(e, scT)
                    I(e.tensor_tensor(scoresM[:, c, :], pb[bk][:, 0:128],
                                      triu_f[:], ALU.mult), "mV")
                prevS2[c % 2] = tok("mV")
            scoresV = prevS2[1]
            dbg_tap("scoresM", scoresM[:].rearrange("p c l -> p (c l)"), [scoresV])

            expT0 = C["mT"]
            expV0 = C["mV"]

            @blk.sync
            def _(e):
                for c in range(NCHK):
                    if c < 2:
                        W(e, *xtspD)
                        W(e, convV)
                    else:
                        e.wait_ge(S["mT"], expT0 + (c - 1))
                    par = "dXTa" if c % 2 == 0 else "dXTb"
                    e.dma_start(out=xtkb[c % 2][:], in_=xt_sp.ap()[c]).then_inc(S[par], 16)
                    C[par] += 16

            bgav2 = [bga[:].rearrange("p (h d) -> p h d", h=NH),
                     hbb[:].rearrange("p a b -> p (a b)").rearrange("p (h d) -> p h d", h=NH)]

            @blk.vector
            def _(e):
                for c in range(NCHK):
                    if c >= 2:
                        e.wait_ge(S["mT"], expT0 + c - 1)
                    W(e, convV)
                    I(e.tensor_tensor(bgav2[c % 2],
                                      bctok[:, c, 0:64].unsqueeze(1).broadcast_to([128, NH, 64]),
                                      dtegt[:, c, :].unsqueeze(2).broadcast_to([128, NH, 64]),
                                      ALU.mult), "mV")
            assert C["mV"] == expV0 + NCHK

            @blk.tensor
            def _(e):
                inst = None
                for c in range(NCHK):
                    par = "dXTa" if c % 2 == 0 else "dXTb"
                    e.wait_ge(S[par], 16 * (c // 2 + 1))
                    e.wait_ge(S["mV"], expV0 + c + 1)
                    for h in range(NH):
                        bank = h // 8
                        col = (h % 8) * 64
                        inst = e.matmul(pb[bank][0:64, col:col + 64], bgav2[c % 2][:, h, :],
                                        xtkb[c % 2][:, h * 64:(h + 1) * 64],
                                        start=(c == 0 and h % 8 == 0),
                                        stop=(c == NCHK - 1 and h % 8 == 7))
                    I(inst, "mT")
            assert C["mT"] == expT0 + NCHK
            expT = tok("mT")
            dbg_tap("xt0", xtkb[0][:], [expT])

            @blk.vector
            def _(e):
                W(e, expT)
                inst = None
                for bank in range(4):
                    inst = e.tensor_copy(agst[:, bank * 512:(bank + 1) * 512], pb[bank][0:64, 0:512])
                I(inst, "mV")
            agV = tok("mV")
            dbg_tap("sfin", agst[:], [agV])

            @blk.sync
            def _(e):
                W(e, agV)
                e.dma_start(out=ag_in.ap(), in_=agst[:]).then_inc(S["dX"], 16)
                C["dX"] += 16

            w2b = [w3buf[i][:].rearrange("p (k c) -> p k c", k=8) for i in range(3)]
            NS = {"GW": 0, "AW": 0, "F1": 0, "F2": 0}

            @blk.sync
            def _(e):
                W(e, *xtspD)
                e.dma_start(out=ubf[:], in_=uview[:, :, 3:3 + BS]).then_inc(S["dUBa"], 16)
                C["dUBa"] += 16
            ubTok0 = tok("dUBa")

            @blk.gpsimd
            def _(e):
                W(e, *xtspD)
                W(e, *qspD)
                W(e, convV)
                e.dma_start(out=lnx[:], in_=uview[:, :, 3:3 + BS]).then_inc(S["dUCa"], 16)
                C["dUCa"] += 16
            ubTok20 = tok("dUCa")

            lnv0 = ln_fm(ubf, lnx, lnx, 8, [(0, 256), (256, 256)],
                         (ubTok0, ubTok20, agV, agV) + tuple(qspD))

            u2T0h = C["mT"]
            gw0h = {p: C[p] for p in ("dGWa", "dGWb", "dGWc")}

            @blk.gpsimd
            def _(e):
                for m in range(16):
                    NS["GW"] += 1
                    if m < 3:
                        W(e, bias3T)
                    else:
                        e.wait_ge(S["mT"], u2T0h + (m - 2))
                    par = ("dGWa", "dGWb", "dGWc")[m % 3]
                    e.dma_start(out=w2b[m % 3],
                                in_=w2_t.ap()[m].rearrange("k f c -> f k c")
                                ).then_inc(S[par], 16)
                    C[par] += 16

            sumV0 = C["mV"]

            @blk.gpsimd
            def _(e):
                e.wait_ge(S["dX"], 16)
                e.collective_compute("AllGather", ALU.bypass,
                                     replica_groups=[list(range(ncores))],
                                     ins=[ag_in.ap().opt()],
                                     outs=[ag_out.ap().opt()]).then_inc(S["cc"], 1)
                C["cc"] += 1
                e.wait_ge(S["cc"], 1)
                nb1 = min(4, ncores)
                for r in range(nb1):
                    e.dma_start(out=gth[0:64, r * DI:(r + 1) * DI],
                                in_=ag_out.ap()[r * 64:(r + 1) * 64, :]).then_inc(S["dG2"], 16)
                    C["dG2"] += 16
                if ncores > 4:
                    e.wait_ge(S["mV"], sumV0 + 4)
                    for r in range(4, ncores):
                        e.dma_start(out=gth[0:64, (r - 4) * DI:(r - 3) * DI],
                                    in_=ag_out.ap()[r * 64:(r + 1) * 64, :]).then_inc(S["dG2"], 16)
                        C["dG2"] += 16

            @blk.vector
            def _(e):
                W(e, ("dG2", 16 * min(4, ncores)), K_ALL, agV)
                prev = None
                for r in range(ncores):
                    if r == 4:
                        e.wait_ge(S["dG2"], 128)
                    W(e, prev)
                    slot = (r % 4) * DI
                    if r == 0:
                        prev = I(e.tensor_scalar(Sa[:], gth[0:64, 0:DI],
                                                 selc[:, 0:1], None, ALU.mult), "mV")
                    else:
                        prev = I(e.scalar_tensor_tensor(Sa[:], gth[0:64, slot:slot + DI],
                                                        selc[:, r:r + 1], Sa[:],
                                                        ALU.mult, ALU.add), "mV")
            sinitV = tok("mV")
            dbg_tap("sinit", Sa[:], [sinitV])

        with ExitStack() as s5:
            E5 = s5.enter_context
            wq2b = [E5(nc.sbuf_tensor(f"wq2b{i}", [128, 16, 128], BF16)) for i in range(2)]
            attn_s = E5(nc.sbuf_tensor("attn_s", [128, 8, BS], F32))
            x1b = E5(nc.sbuf_tensor("x1b", [128, 8, BS], F32))
            w3gv = [w3gb[0][:].rearrange("p (k c) -> p k c", k=8),
                    w3gb[1][:].rearrange("p (k c) -> p k c", k=8),
                    hbb[:, 0:8, :],
                    hbb[:, 8:16, :]]
            wq1v = w2b
            znv = castb[:].rearrange("p a b -> p (a b)").rearrange("p (k w) -> p k w", k=16)
            dsubv2 = [t1s[:, 0:512], t1s[:, 512:1024]]
            dexpv2 = [sqs[:, 0:512].rearrange("p (a b) -> p a b", a=4),
                      sqs[:, 512:1024].rearrange("p (a b) -> p a b", a=4)]

            tok_pb01 = agV
            tok_pb23 = agV
            tok_pb45 = convV
            tok_pb67 = scoresV
            tok_castb = None
            tok_ubf = None
            tok_chainEvac = sinitV
            tok_sab_pe = None
            tok_xt_pe = {-2: expT, -1: expT}
            tok_u2_read = None
            tok_znb_read = None
            tok_hsb_read = None
            tok_lnx_read = None
            tok_wbuf = bias3T
            tok_w3gbuf = bias3V
            tok_wq2b = sinitV
            tok_hbb = None

            for j in range(NBLK):
                ucol0 = 3 + j * BS
                upar = "dUBa" if j % 2 == 0 else "dUBb"

                with ExitStack() as sB:
                    EB = sB.enter_context
                    u2 = EB(nc.sbuf_tensor(f"u2_{j}", [128, 16, 256], F32, side="right"))
                    hsb = EB(nc.sbuf_tensor(f"hsb_{j}", [128, 16, 128], F32, side="right"))
                    decS = EB(nc.sbuf_tensor(f"decS_{j}", [128, 16, 128], BF16, side="right"))
                    cdec = EB(nc.sbuf_tensor(f"cdec_{j}", [64, 16, 128], BF16, side="right"))
                    bdv = EB(nc.sbuf_tensor(f"bdv_{j}", [128, 16, HD], BF16, side="right"))
                    lamI = EB(nc.sbuf_tensor(f"lamI_{j}", [64, 16, HD], BF16, side="right"))

                    ucpar = "dUCa" if j % 2 == 0 else "dUCb"
                    tok_castb_blk = tok_castb
                    if j == 0:
                        lnv = lnv0
                        ubTok, ubTok2 = ubTok0, ubTok20
                    else:
                        @blk.sync
                        def _(e, j=j, ucol0=ucol0):
                            W(e, tok_ubf)
                            e.dma_start(out=ubf[:], in_=uview[:, :, ucol0:ucol0 + BS]).then_inc(S[upar], 16)
                            C[upar] += 16
                        ubTok = tok(upar)

                        @blk.gpsimd
                        def _(e, j=j, ucol0=ucol0, ucpar=ucpar):
                            W(e, tok_lnx_u)
                            e.dma_start(out=lnx[:], in_=uview[:, :, ucol0:ucol0 + BS]).then_inc(S[ucpar], 16)
                            C[ucpar] += 16
                        ubTok2 = tok(ucpar)

                        lnv = ln_fm(ubf, lnx, lnx, 8, [(0, 256), (256, 256)],
                                    (ubTok, ubTok2, tok_pb01, tok_pb23))
                    tok_pb01 = lnv
                    tok_pb23 = lnv
                    lnx_free_half = {0: None, 1: None}

                    for half in range(2):
                        hc0 = half * 256
                        u2T0 = C["mT"]
                        u2A0 = C["mA"]

                        gw0 = gw0h if (j == 0 and half == 0) else \
                            {p: C[p] for p in ("dGWa", "dGWb", "dGWc")}
                        if not (j == 0 and half == 0):
                            @blk.gpsimd
                            def _(e, half=half, u2T0=u2T0, tok_wbuf=tok_wbuf):
                                for m in range(16):
                                    NS["GW"] += 1
                                    if m < 3:
                                        W(e, tok_wbuf)
                                    else:
                                        e.wait_ge(S["mT"], u2T0 + (m - 2))
                                    par = ("dGWa", "dGWb", "dGWc")[m % 3]
                                    e.dma_start(out=w2b[m % 3],
                                                in_=w2_t.ap()[m].rearrange("k f c -> f k c")
                                                ).then_inc(S[par], 16)
                                    C[par] += 16
                        else:
                            assert u2T0 == u2T0h

                        @blk.tensor
                        def _(e, half=half, hc0=hc0, u2T0=u2T0, u2A0=u2A0, lnv=lnv):
                            W(e, lnv, tok_pb01, tok_pb23)
                            for m in range(16):
                                par = ("dGWa", "dGWb", "dGWc")[m % 3]
                                e.wait_ge(S[par], gw0[par] + 16 * (m // 3 + 1))
                                if m >= 4:
                                    e.wait_ge(S["mA"], u2A0 + (m - 3))
                                inst = None
                                for k in range(8):
                                    inst = e.matmul(pb[m % 4][:, 0:256],
                                                    w2b[m % 3][:, k, :],
                                                    lnx[:, k, hc0:hc0 + 256],
                                                    start=(k == 0), stop=(k == 7))
                                I(inst, "mT")
                        assert C["mT"] == u2T0 + 16

                        @blk.scalar
                        def _(e, j=j, half=half, u2T0=u2T0, u2A0=u2A0, u2=u2):
                            act_set(e, 10)
                            if j == 0 and half == 0:
                                W(e, ("dX", 16))
                            for m in range(16):
                                e.wait_ge(S["mT"], u2T0 + m + 1)
                                dst = u2[:, m, :]
                                if j == 0 and half == 0 and m < 4:
                                    dst = u2h0s[:, m, :]
                                I(e.activation(dst, pb[m % 4][:, 0:256], AF.Gelu), "mA")
                            act_set(e, 6)
                        assert C["mA"] == u2A0 + 16
                        u2V = tok("mA")
                        tok_pb01 = u2V
                        tok_pb23 = u2V
                        tok_wbuf = ("mT", u2T0 + 16)
                        lnx_free_half[half] = ("mT", u2T0 + 16)

                        for c2 in range(2):
                            ci = j * 4 + half * 2 + c2
                            coff = c2 * 128
                            xpar = "dXTa" if ci % 2 == 0 else "dXTb"

                            @blk.sync
                            def _(e, ci=ci, xpar=xpar):
                                W(e, tok_xt_pe.get(ci - 2, expT))
                                e.dma_start(out=xtkb[ci % 2][:], in_=xt_sp.ap()[ci]).then_inc(S[xpar], 16)
                                C[xpar] += 16
                            xtTok = tok(xpar)

                            @blk.vector
                            def _(e, ci=ci):
                                W(e, tok_chainEvac, tok_sab_pe)
                                I(e.tensor_copy(Sab[:], Sa[:]), "mV")
                            sabV = tok("mV")

                            hh_done = None
                            yTs = []
                            for hh in range(2):
                                hbase = hh * 16
                                pbY0 = 4 + 2 * hh

                                @blk.vector
                                def _(e, ci=ci, hh=hh, hbase=hbase, hh_done=hh_done):
                                    W(e, lamVV, KG_ALL, hh_done)
                                    if hh == 0:
                                        W(e, tok_chainEvac)
                                    I(e.tensor_tensor(
                                        lamI[:],
                                        i64_b[:].unsqueeze(1).broadcast_to([64, 16, 64]),
                                        lam64[:, ci * 32 + hbase:ci * 32 + hbase + 16].unsqueeze(2)
                                        .broadcast_to([64, 16, 64]),
                                        ALU.mult), "mV")
                                lamIV = tok("mV")

                                @blk.vector
                                def _(e, ci=ci, hh=hh, hh_done=hh_done):
                                    W(e, hh_done)
                                    I(e.tensor_tensor(
                                        bdv[:],
                                        bctok[:, ci, 0:64].unsqueeze(1).broadcast_to([128, 16, 64]),
                                        dtet[:, ci, hh * 16:hh * 16 + 16].unsqueeze(2)
                                        .broadcast_to([128, 16, 64]),
                                        ALU.mult), "mV")
                                bdV = tok("mV")

                                prevDec = [hh_done, hh_done]
                                for hg in range(4):
                                    bank = 2 + hg % 2
                                    hgg = hbase + hg * 4

                                    @blk.tensor
                                    def _(e, ci=ci, hgg=hgg, bank=bank, pv=prevDec[hg % 2]):
                                        W(e, tok_pb23, pv, K_ALL)
                                        inst = None
                                        for hi in range(4):
                                            h = hgg + hi
                                            inst = e.matmul(pb[bank][:, hi * 128:(hi + 1) * 128],
                                                            i32_f[:, h:h + 1].broadcast_to([32, 128]),
                                                            aclt[:, ci, :], start=True, stop=True)
                                        I(inst, "mT")
                                    dbT = tok("mT")

                                    @blk.vector
                                    def _(e, ci=ci, hg=hg, hgg=hgg, bank=bank, dbT=dbT):
                                        W(e, dbT)
                                        inst = None
                                        for hi in range(4):
                                            h = hgg + hi
                                            inst = e.tensor_scalar(dsubv2[hg % 2][:, hi * 128:(hi + 1) * 128],
                                                                   pb[bank][:, hi * 128:(hi + 1) * 128],
                                                                   actok[:, ci, h:h + 1], 0.0,
                                                                   ALU.subtract, ALU.min)
                                        I(inst, "mV")
                                    dsV = tok("mV")

                                    @blk.scalar
                                    def _(e, hg=hg, dsV=dsV, pv=prevDec[hg % 2]):
                                        W(e, dsV, pv)
                                        I(e.activation(dexpv2[hg % 2].rearrange("p a b -> p (a b)"),
                                                       dsubv2[hg % 2], AF.Exp), "mA")
                                    dxA = tok("mA")

                                    @blk.vector
                                    def _(e, ci=ci, hg=hg, dxA=dxA):
                                        W(e, dxA)
                                        I(e.tensor_tensor(
                                            decS[:, hg * 4:(hg + 1) * 4, :],
                                            dexpv2[hg % 2],
                                            scoresM[:, ci, :].unsqueeze(1).broadcast_to([128, 4, 128]),
                                            ALU.mult), "mV")
                                    prevDec[hg % 2] = tok("mV")
                                decSV = prevDec[1]

                                prevCd = [prevDec[0], prevDec[1]]
                                for hg in range(4):
                                    bank = 2 + hg % 2
                                    hgg = hbase + hg * 4

                                    @blk.tensor
                                    def _(e, ci=ci, hgg=hgg, bank=bank, pv=prevCd[hg % 2]):
                                        W(e, pv, eacA, KG_ALL)
                                        inst = None
                                        for hi in range(4):
                                            h = hgg + hi
                                            inst = e.matmul(pb[bank][0:64, hi * 128:(hi + 1) * 128],
                                                            i32_b[:, h:h + 1].broadcast_to([32, 64]),
                                                            eac[:, ci, :], start=True, stop=True)
                                        I(inst, "mT")
                                    eaT = tok("mT")

                                    @blk.vector
                                    def _(e, ci=ci, hg=hg, bank=bank, eaT=eaT):
                                        W(e, eaT)
                                        I(e.tensor_tensor(
                                            cdec[:, hg * 4:(hg + 1) * 4, :],
                                            cfeat[:, ci, :].unsqueeze(1).broadcast_to([64, 4, 128]),
                                            pb[bank][0:64, 0:512].rearrange("p (a b) -> p a b", a=4),
                                            ALU.mult), "mV")
                                    prevCd[hg % 2] = tok("mV")
                                cdecV = prevCd[1]
                                tok_pb23 = cdecV

                                pbXf = tok_pb45 if hh == 0 else tok_pb67

                                @blk.tensor
                                def _(e, ci=ci, hh=hh, hbase=hbase, pbY0=pbY0, xtTok=xtTok,
                                      sabV=sabV, decSV=decSV, cdecV=cdecV, pbXf=pbXf):
                                    W(e, xtTok, sabV, decSV, cdecV, pbXf)
                                    inst = None
                                    for hp in range(8):
                                        bank = pbY0 + hp // 4
                                        col = (hp % 4) * 128
                                        for sub in range(2):
                                            h16 = 2 * hp + sub
                                            h = hbase + h16
                                            pr = slice(64 * sub, 64 * (sub + 1))
                                            inst = e.matmul(pb[bank][pr, col:col + 128],
                                                            xtkb[ci % 2][:, h * 64:(h + 1) * 64],
                                                            decS[:, h16, :], start=True, stop=False)
                                            inst = e.matmul(pb[bank][pr, col:col + 128],
                                                            Sab[:, h * 64:(h + 1) * 64],
                                                            cdec[:, h16, :], start=False, stop=True)
                                    I(inst, "mT")
                                yT = tok("mT")
                                yTs.append(yT)

                                @blk.tensor
                                def _(e, ci=ci, hh=hh, hbase=hbase, bdV=bdV, lamIV=lamIV,
                                      sabV=sabV, xtTok=xtTok):
                                    W(e, bdV, lamIV, sabV, xtTok, tok_pb01)
                                    inst = None
                                    for h16 in range(16):
                                        h = hbase + h16
                                        bank = h16 // 8
                                        col = (h16 % 8) * 64
                                        inst = e.matmul(pb[bank][0:64, col:col + 64],
                                                        lamI[:, h16, :], Sab[:, h * 64:(h + 1) * 64],
                                                        start=True, stop=False)
                                        inst = e.matmul(pb[bank][0:64, col:col + 64],
                                                        bdv[:, h16, :], xtkb[ci % 2][:, h * 64:(h + 1) * 64],
                                                        start=False, stop=True)
                                    I(inst, "mT")
                                cwT = tok("mT")

                                @blk.vector
                                def _(e, hh=hh, cwT=cwT):
                                    W(e, cwT)
                                    e.tensor_copy(Sa[:, hh * 1024:hh * 1024 + 512], pb[0][0:64, 0:512])
                                    I(e.tensor_copy(Sa[:, hh * 1024 + 512:hh * 1024 + 1024],
                                                    pb[1][0:64, 0:512]), "mV")
                                chainEv = tok("mV")
                                tok_pb01 = chainEv
                                hh_done = chainEv

                            tok_chainEvac = hh_done
                            tok_sab_pe = hh_done
                            tok_xt_pe[ci] = hh_done

                            @blk.vector
                            def _(e, ci=ci, coff=coff, yTs=tuple(yTs), u2V=u2V, u2=u2):
                                W(e, yTs[0], yTs[1], u2V, tok_hsb_read, tok_hbb)
                                inst = None
                                for b4 in range(4):
                                    src = u2[:, b4 * 4:(b4 + 1) * 4, coff:coff + 128]
                                    if ci < 2 and b4 == 0:
                                        src = u2h0s[:, 0:4, coff:coff + 128]
                                    e.tensor_tensor(
                                        hsb[:, b4 * 4:(b4 + 1) * 4, :],
                                        src,
                                        pb[4 + b4][:, 0:512].rearrange("p (a b) -> p a b", a=4),
                                        ALU.mult)
                                    inst = e.tensor_tensor(
                                        hbb[:, b4 * 4:(b4 + 1) * 4, :],
                                        src,
                                        pb[4 + b4][:, 0:512].rearrange("p (a b) -> p a b", a=4),
                                        ALU.mult)
                                I(inst, "mV")
                            hV = tok("mV")
                            tok_pb45 = hV
                            tok_pb67 = hV
                            tok_u2_read = hV
                            if dbg == "hsb0" and ci == 0:
                                dbg_tap("hsb0", hsb[:].rearrange("p a b -> p (a b)"), [hV])

                            hbV = hV

                            gv = ln_fm(hsb, hbb, znv[:, :, coff:coff + 128], 16, [(0, 128)],
                                       (hV, hbV, tok_pb01, tok_pb23, tok_znb_read,
                                        tok_castb_blk))
                            tok_hsb_read = gv
                            tok_pb01 = gv
                            tok_pb23 = gv
                            znReady = gv
                            if dbg == "zn0" and ci == 1:
                                dbg_tap("zn0", castb[:].rearrange("p a b -> p (a b)"), [gv])

                        atT0 = C["mT"]
                        atV0 = C["mV"]
                        aw0 = {p: C[p] for p in ("dAWa", "dAWb", "dAWc", "dAWd")}

                        @blk.sync
                        def _(e, half=half, atT0=atT0, tok_w3gbuf=tok_w3gbuf):
                            for m in range(8):
                                for piece in range(2):
                                    idx = m * 2 + piece
                                    NS["AW"] += 1
                                    if idx < 4:
                                        W(e, *w3spD)
                                        W(e, tok_w3gbuf, znReady)
                                    else:
                                        e.wait_ge(S["mT"], atT0 + (idx - 4) // 2 + 1)
                                    par = ("dAWa", "dAWb", "dAWc", "dAWd")[idx % 4]
                                    e.dma_start(out=w3gv[idx % 4],
                                                in_=w3g_sp.ap()[m, piece * 8:(piece + 1) * 8]
                                                .rearrange("k f c -> f k c")).then_inc(S[par], 16)
                                    C[par] += 16

                        @blk.tensor
                        def _(e, half=half, hc0=hc0, atT0=atT0, atV0=atV0, znReady=znReady,
                              aw0=aw0):
                            W(e, tok_pb45, tok_pb67, znReady)
                            for m in range(8):
                                if m >= 4:
                                    e.wait_ge(S["mV"], atV0 + (m - 3))
                                inst = None
                                for piece in range(2):
                                    idx = m * 2 + piece
                                    par = ("dAWa", "dAWb", "dAWc", "dAWd")[idx % 4]
                                    e.wait_ge(S[par], aw0[par] + 16 * (idx // 4 + 1))
                                    for k8 in range(8):
                                        k = piece * 8 + k8
                                        inst = e.matmul(pb[4 + m % 4][:, 0:256],
                                                        w3gv[idx % 4][:, k8, :], znv[:, k, :],
                                                        start=(k == 0), stop=(k == 15))
                                I(inst, "mT")
                        assert C["mT"] == atT0 + 8

                        @blk.vector
                        def _(e, j=j, half=half, hc0=hc0, atT0=atT0, atV0=atV0,
                              lnx_free=lnx_free_half[half]):
                            if half == 0:
                                W(e, ("dO", 16 * j))
                            for m in range(8):
                                e.wait_ge(S["mT"], atT0 + m + 1)
                                W(e, bias3V)
                                ev = I(e.tensor_scalar(attn_s[:, m, hc0:hc0 + 256],
                                                       pb[4 + m % 4][:, 0:256],
                                                       bias3[:, m:m + 1], None, ALU.add), "mV")
                                W(e, ev)
                                if m == 0:
                                    W(e, lnx_free)
                                I(e.tensor_copy(lnx[:, m, hc0:hc0 + 256],
                                                attn_s[:, m, hc0:hc0 + 256]), "mP")
                        assert C["mV"] == atV0 + 8
                        attnV = tok("mV")
                        acpV = tok("mP")
                        tok_pb45 = attnV
                        tok_pb67 = attnV
                        tok_znb_read = attnV
                        tok_hbb = ("mT", atT0 + 8)
                        tok_w3gbuf = ("mT", atT0 + 8)

                    if dbg == "attn0" and j == 0:
                        dbg_tap("attn0", attn_s[:].rearrange("p a b -> p (a b)"), [acpV])

                    x1V, x1cV = ln_fm(attn_s, lnx, None, 8, [(0, 256), (256, 256)],
                                      (attnV, acpV, tok_pb01, tok_pb23, ubTok),
                                      res_out=x1b, res_in=ubf,
                                      copy_out=castb, copy_gate=tok_w3gbuf)
                    tok_castb = x1V
                    tok_pb01 = x1V
                    tok_pb23 = x1V
                    tok_ubf = x1V
                    if dbg == "x1b0" and j == 0:
                        dbg_tap("x1b0", x1b[:].rearrange("p a b -> p (a b)"), [x1V])

                    ln2V = ln_fm(x1b, castb, lnx, 8, [(0, 256), (256, 256)],
                                 (x1V, x1cV, tok_pb01, tok_pb23))
                    tok_castb = ln2V
                    tok_pb01 = ln2V
                    tok_pb23 = ln2V
                    tok_lnx_read = None

                with ExitStack() as sF:
                    h1b = sF.enter_context(nc.sbuf_tensor(f"h1b_{j}", [128, 32, BS], BF16, side="right"))
                    wq2b2 = sF.enter_context(nc.sbuf_tensor(f"wq2b2_{j}", [128, 16, 128], BF16, side="right"))
                    wq2b3 = [wq2b[0], wq2b[1], wq2b2]

                    f1T0 = C["mT"]
                    f1A0 = C["mA"]

                    f10 = {p: C[p] for p in ("dF1a", "dF1b", "dF1c")}

                    @blk.sync
                    def _(e, j=j, f1T0=f1T0, tok_wbuf=tok_wbuf):
                        for m in range(32):
                            NS["F1"] += 1
                            if m < 3:
                                W(e, *qspD)
                                W(e, ln2V, tok_wbuf)
                            else:
                                e.wait_ge(S["mT"], f1T0 + (m - 2))
                            par = ("dF1a", "dF1b", "dF1c")[m % 3]
                            e.dma_start(out=wq1v[m % 3],
                                        in_=wq1_sp.ap()[m].rearrange("k f c -> f k c")
                                        ).then_inc(S[par], 16)
                            C[par] += 16

                    @blk.tensor
                    def _(e, f1T0=f1T0, f1A0=f1A0, ln2V=ln2V, f10=f10):
                        W(e, ln2V, tok_pb45, tok_pb67)
                        for m in range(32):
                            par = ("dF1a", "dF1b", "dF1c")[m % 3]
                            e.wait_ge(S[par], f10[par] + 16 * (m // 3 + 1))
                            if m >= 4:
                                e.wait_ge(S["mA"], f1A0 + (m - 3))
                            inst = None
                            for k in range(8):
                                inst = e.matmul(pb[4 + m % 4][:, 0:512], wq1v[m % 3][:, k, :],
                                                lnx[:, k, :], start=(k == 0), stop=(k == 7))
                            I(inst, "mT")
                    assert C["mT"] == f1T0 + 32

                    @blk.scalar
                    def _(e, f1T0=f1T0, f1A0=f1A0):
                        for m in range(32):
                            e.wait_ge(S["mT"], f1T0 + m + 1)
                            W(e, K_ALL)
                            I(e.activation(h1b[:, m, :], pb[4 + m % 4][:, 0:512], AF.Relu,
                                           bias=b1c[:, m:m + 1]), "mA")
                    assert C["mA"] == f1A0 + 32
                    h1V = tok("mA")
                    tok_pb45 = h1V
                    tok_pb67 = h1V
                    tok_wbuf = ("mT", f1T0 + 32)
                    tok_lnx_u = ("mT", f1T0 + 32)
                    if dbg == "h1b0" and j == 0:
                        dbg_tap("h1b0", h1b[:].rearrange("p a b -> p (a b)"), [h1V])

                    f2T0 = C["mT"]
                    f2V0 = C["mV"]

                    @blk.sync
                    def _(e, f2T0=f2T0, tok_wq2b=tok_wq2b, wq2b3=wq2b3):
                        for m in range(8):
                            for piece in range(2):
                                idx = m * 2 + piece
                                NS["F2"] += 1
                                n = NS["F2"] - 1
                                if idx < 3:
                                    W(e, *qspD)
                                    W(e, *xtspD)
                                    W(e, tok_wq2b)
                                else:
                                    e.wait_ge(S["mT"], f2T0 + (idx - 3) // 2 + 1)
                                par = ("dF2a", "dF2b", "dF2c")[n % 3]
                                e.dma_start(out=wq2b3[n % 3][:],
                                            in_=wq2_sp.ap()[m, piece * 16:(piece + 1) * 16]
                                            .rearrange("k f c -> f k c")).then_inc(S[par], 16)
                                C[par] += 16

                    @blk.tensor
                    def _(e, f2T0=f2T0, f2V0=f2V0, h1V=h1V, wq2b3=wq2b3):
                        W(e, h1V, tok_pb67, tok_pb45)
                        nbase = NS["F2"] - 16
                        for m in range(8):
                            if m >= 4:
                                e.wait_ge(S["mV"], f2V0 + (m - 3))
                            inst = None
                            for piece in range(2):
                                n = nbase + m * 2 + piece
                                par = ("dF2a", "dF2b", "dF2c")[n % 3]
                                e.wait_ge(S[par], 16 * (n // 3 + 1))
                                for k16 in range(16):
                                    k = piece * 16 + k16
                                    inst = e.matmul(pb[4 + m % 4][:, 0:512], wq2b3[n % 3][:, k16, :],
                                                    h1b[:, k, :], start=(k == 0), stop=(k == 31))
                            I(inst, "mT")
                    assert C["mT"] == f2T0 + 8

                    @blk.vector
                    def _(e, f2T0=f2T0, f2V0=f2V0, ln2V=ln2V):
                        for m in range(8):
                            e.wait_ge(S["mT"], f2T0 + m + 1)
                            W(e, K_ALL)
                            ev = I(e.tensor_scalar(attn_s[:, m, :], pb[4 + m % 4][:, 0:512],
                                                   b2c[:, m:m + 1], None, ALU.add), "mV")
                            W(e, ev)
                            I(e.tensor_copy(castb[:, m, :], attn_s[:, m, :]), "mP")
                    assert C["mV"] == f2V0 + 8
                    ffoV = tok("mV")
                    fcpV = tok("mP")
                    tok_pb67 = ffoV
                    tok_pb45 = ffoV
                    tok_wq2b = ("mT", f2T0 + 8)
                    if dbg == "ffn0" and j == 0:
                        dbg_tap("ffn0", attn_s[:].rearrange("p a b -> p (a b)"), [ffoV])

                    x2V = ln_fm(attn_s, castb, None, 8, [(0, 256), (256, 256)],
                                (ffoV, fcpV, tok_pb01, tok_pb23),
                                res_out=attn_s, res_in=x1b)
                    tok_castb = x2V
                    tok_pb01 = x2V
                    tok_pb23 = x2V
                    tok_lnx_read = None

                    @blk.scalar
                    def _(e, j=j, x2V=x2V):
                        W(e, x2V)
                        e.dma_start(out=out_ft.ap().rearrange("(a f) t -> f a t", f=128)
                                    [:, :, j * BS:(j + 1) * BS],
                                    in_=attn_s[:]).then_inc(S["dO"], 16)
                        C["dO"] += 16

            @blk.gpsimd
            def _(e):
                W(e, ("dO", C["dO"]))
                if dbg:
                    W(e, ("dDBG", C["dDBG"]))
    return nc



def _host_prepare(inputs):
    W1p = np.ascontiguousarray(np.asarray(inputs["L0_attn_W_in1"], np.float32))
    w1x = np.ascontiguousarray(W1p.T[:, 2048:4256])
    w1x_t = np.zeros((18, 8, 128, 128), np.float32)
    for m in range(18):
        cwd = min(128, 2208 - m * 128)
        w1x_t[m, :, :, :cwd] = w1x[:, m * 128:m * 128 + cwd].reshape(8, 128, cwd)
    convw_t = np.ascontiguousarray(
        np.asarray(inputs["L0_attn_conv_w"], np.float32).transpose(2, 1, 0))
    w2 = np.ascontiguousarray(np.asarray(inputs["L0_attn_W_in2"], np.float32).T)
    w2_t = np.zeros((16, 8, 128, 128), np.float32)
    for m in range(16):
        w2_t[m] = w2[:, m * 128:(m + 1) * 128].reshape(8, 128, 128)
    shared = dict(
        w1x_t=w1x_t, convw_t=convw_t, w2_t=w2_t,
        w3_t=np.ascontiguousarray(np.asarray(inputs["L0_attn_W_in3"], np.float32).T),
        fw1_t=np.ascontiguousarray(np.asarray(inputs["L0_ffn_W1"], np.float32).T),
        fw2_t=np.ascontiguousarray(np.asarray(inputs["L0_ffn_W2"], np.float32).T),
        cb_row=np.asarray(inputs["L0_attn_conv_b"], np.float32).reshape(1, CH),
        b1_pm=np.ascontiguousarray(np.asarray(inputs["L0_ffn_b1"], np.float32).reshape(32, 128).T),
        b2_pm=np.ascontiguousarray(np.asarray(inputs["L0_ffn_b2"], np.float32).reshape(8, 128).T),
        lng_pm=np.ascontiguousarray(np.asarray(inputs["L0_attn_ln_g"], np.float32).reshape(16, 128).T),
        lnb_pm=np.ascontiguousarray(np.asarray(inputs["L0_attn_ln_b"], np.float32).reshape(16, 128).T),
        alog_c=np.asarray(inputs["L0_attn_A_log"], np.float32).reshape(32, 1),
        prelu_pm=np.broadcast_to(np.asarray(inputs["L0_attn_prelu_a"], np.float32)
                                 .reshape(1, 1), (128, 1)).copy(),
        triu_h=np.triu(np.ones((128, 128), np.float32)),
        i128_h=np.eye(128, dtype=np.float32),
        i32_h=np.eye(32, dtype=np.float32),
        i64_h=np.eye(64, dtype=np.float32),
        ones_h=np.ones((128, 128), np.float32),
        onesc_h=np.ones((128, 1), np.float32),
    )
    shared = {k: np.ascontiguousarray(v, dtype=np.float32) for k, v in shared.items()}

    u = np.asarray(inputs["L0_attn_u"], np.float32)
    in_maps = []
    for c in range(NCORES):
        b, hhalf = c // 2, c % 2
        t0 = hhalf * 2048
        useg = np.zeros((TH, D), np.float32)
        useg[3:] = u[b, t0:t0 + 2048]
        if hhalf == 1:
            useg[:3] = u[b, t0 - 3:t0]
        sel = np.zeros((64, 8), np.float32)
        if hhalf == 1:
            sel[:, c - 1] = 1.0
        m = dict(shared)
        m["u_ft"] = np.ascontiguousarray(useg.T)
        m["sel_pm"] = sel
        in_maps.append(m)
    return in_maps


def kernel(**inputs):
    in_maps = _host_prepare(inputs)
    nc = build(DBG)
    res = run_bass_kernel_spmd(nc, in_maps, core_ids=list(range(NCORES)))
    u = np.asarray(inputs["L0_attn_u"])
    out = np.empty((4, 4096, D), np.float32)
    for c in range(NCORES):
        b, hhalf = c // 2, c % 2
        out[b, hhalf * 2048:(hhalf + 1) * 2048] = res.results[c]["out_ft"].T
    if DBG:
        np.save("/tmp/kdbg.npy", np.stack([res.results[c]["dbg"] for c in range(NCORES)]))
    return out
